# version 15
# baseline (speedup 1.0000x reference)
from contextlib import ExitStack

import numpy as np
import concourse.bass as bass
import concourse.mybir as mybir
from concourse.bass_utils import run_bass_kernel_spmd

F32 = mybir.dt.float32
BF16 = mybir.dt.bfloat16
F16 = mybir.dt.float16
AF = mybir.ActivationFunctionType
ALU = mybir.AluOpType
AX = mybir.AxisListType

S = 2048
D = 1024
H = 16
DH = 64
HID = 4096
NCORES = 8
NT = S // 128
QTW = 512
NQT = S // QTW
GRP = 256
NGRP = S // GRP
VW = DH + 1

DRY = [False]


class Eng:

    def __init__(self, raw, sem, chain):
        self.raw = raw
        self.sem = sem
        self.chain = chain
        self.n = 0
        self._waited = {}

    def state(self):
        return (self.n, dict(self._waited))

    def restore(self, st):
        self.n, self._waited = st[0], dict(st[1])

    def endcount(self):
        return self.n

    def tok(self):
        return (self.sem, self.n)

    def wait(self, tok):
        if tok is None:
            return
        if isinstance(tok, list):
            for t in tok:
                self.wait(t)
            return
        sem, val = tok
        if val <= 0:
            return
        key = id(sem)
        if self._waited.get(key, -1) >= val:
            return
        if not DRY[0]:
            self.raw.wait_ge(sem, val)
        self._waited[key] = val

    def emit(self, name, *a, waits=(), **kw):
        if self.chain and self.n > 0:
            self.wait((self.sem, self.n))
        for t in waits:
            self.wait(t)
        if not DRY[0]:
            inst = getattr(self.raw, name)(*a, **kw)
            inst.then_inc(self.sem, 1)
        self.n += 1
        return (self.sem, self.n)


class Queue:

    def __init__(self, raw, sems):
        self.raw = raw
        self.sems = list(sems)
        self.counts = [0] * len(self.sems)
        self.i = 0
        self._waited = {}

    def state(self):
        return (list(self.counts), self.i, dict(self._waited))

    def restore(self, st):
        self.counts, self.i, self._waited = list(st[0]), st[1], dict(st[2])

    def endcount(self):
        return tuple(self.counts)

    def wait(self, tok):
        if tok is None:
            return
        if isinstance(tok, list):
            for t in tok:
                self.wait(t)
            return
        sem, val = tok
        if val <= 0:
            return
        key = id(sem)
        if self._waited.get(key, -1) >= val:
            return
        if not DRY[0]:
            self.raw.wait_ge(sem, val)
        self._waited[key] = val

    def emit(self, name, *a, waits=(), **kw):
        j = self.i % len(self.sems)
        self.i += 1
        self.wait((self.sems[j], self.counts[j]))
        for t in waits:
            self.wait(t)
        if not DRY[0]:
            inst = getattr(self.raw, name)(*a, **kw)
            inst.then_inc(self.sems[j], 16)
        self.counts[j] += 16
        return (self.sems[j], self.counts[j])

    def toks(self):
        return [(s, c) for s, c in zip(self.sems, self.counts) if c > 0]


def build_program(debug=False):
    nc = bass.Bass("TRN2", target_bir_lowering=False, debug=False)

    x_h = nc.dram_tensor("x", [S, D], F32, kind="ExternalInput").ap()
    wq_h = nc.dram_tensor("wq", [D, D], F32, kind="ExternalInput").ap()
    wk_h = nc.dram_tensor("wk", [D, D], F32, kind="ExternalInput").ap()
    wv_h = nc.dram_tensor("wv", [D, D], F32, kind="ExternalInput").ap()
    w1t_h = nc.dram_tensor("w1t", [D, HID], F32, kind="ExternalInput").ap()
    w2t_h = nc.dram_tensor("w2t", [HID, D], F32, kind="ExternalInput").ap()
    convw_h = nc.dram_tensor("convw", [128, 96], F32, kind="ExternalInput").ap()
    convb_h = nc.dram_tensor("convb", [128, 32], F32, kind="ExternalInput").ap()
    idbf_h = nc.dram_tensor("idbf", [128, 128], BF16, kind="ExternalInput").ap()
    idf_h = nc.dram_tensor("idf", [128, 128], F32, kind="ExternalInput").ap()
    mask_h = nc.dram_tensor("mask", [128, 128], BF16, kind="ExternalInput").ap()
    onesb_h = nc.dram_tensor("onesb", [128, 1], BF16, kind="ExternalInput").ap()
    onesf_h = nc.dram_tensor("onesf", [128, 1], F32, kind="ExternalInput").ap()
    onesrow_h = nc.dram_tensor("onesrow", [1, 128], F32, kind="ExternalInput").ap()

    out_h = nc.dram_tensor("out", [S, D], F32, kind="ExternalOutput").ap()

    x1scr = nc.dram_tensor("x1scr", [S, D], F32, kind="Internal").ap()
    t1scr = nc.dram_tensor("t1scr", [D, HID], BF16, kind="Internal").ap()
    t2scr = nc.dram_tensor("t2scr", [HID, D], BF16, kind="Internal").ap()

    dbg = {}
    if debug:
        def dbg_t(name, shape, dt=F32):
            dbg[name] = nc.dram_tensor(name, shape, dt, kind="ExternalOutput").ap()
        dbg_t("d_qt", [128, 8 * S], BF16)
        dbg_t("d_kt", [128, 8 * S], BF16)
        dbg_t("d_vaug", [128, NT * H * VW], BF16)
        dbg_t("d_x1", [S, D], F32)
        dbg_t("d_xqt", [128, 8 * (S + 2)], BF16)
        dbg_t("d_t1t", [D, HID], BF16)
        dbg_t("d_t2t", [HID, D], BF16)
        dbg_t("d_hq", [128, 32 * GRP], BF16)
        dbg_t("d_hsil", [128, 32 * GRP], F32)
        dbg_t("d_nb", [1, 16], F32)

    es = ExitStack()
    with es:
        s_sq = [es.enter_context(nc.semaphore(f"s_sq{i}")) for i in range(4)]
        s_gq = [es.enter_context(nc.semaphore(f"s_gq{i}")) for i in range(4)]
        s_t = es.enter_context(nc.semaphore("s_t"))
        s_v = es.enter_context(nc.semaphore("s_v"))
        s_a = es.enter_context(nc.semaphore("s_a"))
        s_g = es.enter_context(nc.semaphore("s_g"))
        idbf = es.enter_context(nc.sbuf_tensor("idbf_s", [128, 128], BF16))
        idf = es.enter_context(nc.sbuf_tensor("idf_s", [128, 128], F32))
        maskt = es.enter_context(nc.sbuf_tensor("mask_s", [128, 128], BF16))
        onesb = es.enter_context(nc.sbuf_tensor("onesb_s", [128, 1], BF16))
        onesf = es.enter_context(nc.sbuf_tensor("onesf_s", [128, 1], F32))
        onesrow = es.enter_context(nc.sbuf_tensor("onesrow_s", [1, 128], F32))
        convw = es.enter_context(nc.sbuf_tensor("convw_s", [128, 96], F32))
        convb = es.enter_context(nc.sbuf_tensor("convb_s", [128, 32], F32))
        smalls = es.enter_context(nc.sbuf_tensor("smalls", [128, 64], F32))
        nrm = es.enter_context(nc.sbuf_tensor("nrm", [1, 64], F32))
        block = es.enter_context(nc.Block())

        SQ = Queue(nc.sync, s_sq)
        GQ = Queue(nc.gpsimd, s_gq)
        TE = Eng(nc.tensor, s_t, chain=False)
        VE = Eng(nc.vector, s_v, chain=True)
        SE = Eng(nc.scalar, s_a, chain=True)
        GE = Eng(nc.gpsimd, s_g, chain=True)
        ENGS = [SQ, GQ, TE, VE, SE, GE]

        toks = {}
        phase_fns = []

        def sec(key):
            def deco(fn):
                phase_fns.append((key, fn))
                return fn
            return deco

        sec_sq = sec("sync")
        sec_gq = sec("gpsimd")
        sec_t = sec("tensor")
        sec_v = sec("vector")
        sec_s = sec("scalar")
        sec_g = sec("gpsimd")

        def flush_phase():
            snaps = [e.state() for e in ENGS]
            DRY[0] = True
            for _, fn in phase_fns:
                fn(None)
            DRY[0] = False
            ends = [e.endcount() for e in ENGS]
            for e, st in zip(ENGS, snaps):
                e.restore(st)
            for key, raw in (("sync", None), ("gpsimd", None),
                             ("tensor", None), ("vector", None),
                             ("scalar", None)):
                fns = [f for k, f in phase_fns if k == key]
                if not fns:
                    continue

                def runner(eng, fns=fns):
                    for f in fns:
                        f(eng)
                getattr(block, key)(runner)
            for e, ec in zip(ENGS, ends):
                assert e.endcount() == ec, "dry/real emission diverged"
            phase_fns.clear()

        @sec_sq
        def _(sy):
            SQ.emit("dma_start", out=idbf[:], in_=idbf_h[:])
            SQ.emit("dma_start", out=idf[:], in_=idf_h[:])
            SQ.emit("dma_start", out=maskt[:], in_=mask_h[:])
            SQ.emit("dma_start", out=onesb[:], in_=onesb_h[:])
            SQ.emit("dma_start", out=onesf[:], in_=onesf_h[:])
            SQ.emit("dma_start", out=onesrow[:], in_=onesrow_h[:])
            SQ.emit("dma_start", out=convw[:], in_=convw_h[:])
            SQ.emit("dma_start", out=convb[:], in_=convb_h[:])
            toks["consts"] = SQ.toks()

        @sec_v
        def _(ve):
            toks["c1536"] = VE.emit("memset", smalls[:, 7:8], 1536.0)
            toks["cn1536"] = VE.emit("memset", smalls[:, 28:29], -1536.0)

        flush_phase()

        esA = ExitStack()
        qT = esA.enter_context(nc.sbuf_tensor("qT", [128, 8 * S], BF16))
        kT = esA.enter_context(nc.sbuf_tensor("kT", [128, 8 * S], BF16))
        vaug = esA.enter_context(nc.sbuf_tensor("vaug", [128, NT * H * VW], BF16))

        esA1 = ExitStack()
        with esA1:
            Wqb = esA1.enter_context(nc.sbuf_tensor("Wqb", [128, 8 * D], BF16))
            Wkb = esA1.enter_context(nc.sbuf_tensor("Wkb", [128, 8 * D], BF16))
            Wvb = esA1.enter_context(nc.sbuf_tensor("Wvb", [128, 8 * D], BF16))
            wstage = esA1.enter_context(nc.sbuf_tensor("wstage", [128, 2 * 2048], F32))
            xstage = esA1.enter_context(nc.sbuf_tensor("xstage", [128, 2 * D], F32))
            lnstage = esA1.enter_context(nc.sbuf_tensor("lnstage", [128, 2 * D], BF16))
            ln1T = esA1.enter_context(nc.sbuf_tensor("ln1T", [128, 8 * 512], BF16))
            bn6 = esA1.enter_context(nc.sbuf_tensor("bn6", [128, 12], F32))
            psQK0 = esA1.enter_context(nc.psum_tensor("psQK0", [128, 512], F32))
            psQK1 = esA1.enter_context(nc.psum_tensor("psQK1", [128, 512], F32))
            psQK2 = esA1.enter_context(nc.psum_tensor("psQK2", [128, 512], F32))
            psTr = esA1.enter_context(nc.psum_tensor("psTr", [128, 1024], BF16))
            psQK = [psQK0, psQK1, psQK2]

            def emit_wcast(qq):
                wi, quarter = divmod(qq, 4)
                w_h, Wb = [(wq_h, Wqb), (wk_h, Wkb), (wv_h, Wvb)][wi]
                wsl = wstage[:, (qq % 2) * 2048:(qq % 2 + 1) * 2048]
                pk = (("wcast", (qq - 2) // 4, (qq - 2) % 4)
                      if qq >= 2 else None)

                @sec_sq
                def _(sy, wi=wi, w_h=w_h, quarter=quarter, wsl=wsl, pk=pk):
                    toks[("wdma", wi, quarter)] = SQ.emit(
                        "dma_start", out=wsl,
                        in_=w_h[256 * quarter:256 * (quarter + 1), :]
                        .rearrange("(c p) d -> p c d", p=128),
                        waits=([toks[pk]] if pk else ()))

                @sec_s
                def _(se, wi=wi, Wb=Wb, quarter=quarter, wsl=wsl):
                    toks[("wcast", wi, quarter)] = SE.emit(
                        "activation",
                        out=Wb[:, (2 * quarter) * D:(2 * quarter + 2) * D],
                        in_=wsl, func=AF.Identity,
                        waits=[toks[("wdma", wi, quarter)]])
            @sec_g
            def _(g):
                ap = vaug[:].rearrange("p (s h w) -> p (s h) w", h=H, w=VW)
                toks["vones"] = GE.emit("memset", ap[:, :, DH:DH + 1], 1.0)

            for sb in range(4):
                for st4 in range(4):
                    st = 4 * sb + st4
                    par = st % 2
                    if st < 4:
                        for qq in range(3 * st, 3 * st + 3):
                            emit_wcast(qq)

                    @sec_gq
                    def _(g, st=st, par=par):
                        w = []
                        if st >= 2:
                            w.append(toks[("ln", st - 2)])
                        toks[("xdma", st)] = GQ.emit(
                            "dma_start",
                            out=xstage[:, par * D:(par + 1) * D],
                            in_=x_h[128 * st:128 * (st + 1), :], waits=w)

                    @sec_v
                    def _(ve, st=st, par=par):
                        xs = xstage[:, par * D:(par + 1) * D]
                        VE.wait(toks[("xdma", st)])
                        VE.emit("bn_stats", bn6[:, 0:6], xs[:, 0:512])
                        VE.emit("bn_stats", bn6[:, 6:12], xs[:, 512:1024])
                        VE.emit("bn_aggr", smalls[:, 2:4], bn6[:, 0:12])
                        toks[("veps", st)] = VE.emit(
                            "tensor_scalar_add", smalls[:, 4:5],
                            smalls[:, 3:4], 1e-5)

                    @sec_s
                    def _(se, st=st):
                        SE.wait(toks[("veps", st)])
                        SE.emit("activation", out=smalls[:, 5:6],
                                in_=smalls[:, 4:5], func=AF.Ln)
                        toks[("rstd", st)] = SE.emit(
                            "activation", out=smalls[:, 0:1],
                            in_=smalls[:, 5:6], func=AF.Exp, scale=-0.5)

                    @sec_v
                    def _(ve, st=st, par=par):
                        VE.wait(toks[("rstd", st)])
                        VE.emit("scalar_tensor_tensor", smalls[:, 1:2],
                                smalls[:, 2:3], -1.0, smalls[:, 0:1],
                                ALU.mult, ALU.mult)
                        xs = xstage[:, par * D:(par + 1) * D]
                        ls = lnstage[:, par * D:(par + 1) * D]
                        toks[("ln", st)] = VE.emit(
                            "tensor_scalar", ls, xs, smalls[:, 0:1],
                            smalls[:, 1:2], ALU.mult, ALU.add)

                    @sec_t
                    def _(te, st=st, par=par):
                        ls = lnstage[:, par * D:(par + 1) * D]
                        TE.wait(toks[("ln", st)])
                        TE.wait(toks["consts"])
                        if st >= 1:
                            TE.wait(toks[("lnT", st - 1)])
                        for c in range(8):
                            TE.emit("transpose",
                                    psTr[:, 128 * c:128 * (c + 1)],
                                    ls[:, 128 * c:128 * (c + 1)], idbf[:])
                        toks[("lntr", st)] = TE.tok()

                    @sec_v
                    def _(ve, st=st, st4=st4):
                        VE.wait(toks[("lntr", st)])
                        src = psTr[:].rearrange("p (c w) -> p c w", w=128)
                        dst = ln1T[:].rearrange("p (c w) -> p c w", w=512)
                        toks[("lnT", st)] = VE.emit(
                            "tensor_copy",
                            dst[:, :, 128 * st4:128 * (st4 + 1)], src[:])

                bank_last = {}

                def qkv_mm(Wb, outbuf, sb, proj):
                    for mt in range(8):
                        bidx = (proj * 8 + mt) % 3
                        b = psQK[bidx]

                        @sec_t
                        def _(te, Wb=Wb, mt=mt, b=b, sb=sb, proj=proj,
                              lk=bank_last.get(bidx)):
                            TE.wait(toks[("lnT", 4 * sb + 3)])
                            TE.wait(toks[("wcast", proj, 3)])
                            if lk is not None:
                                TE.wait(toks[lk])
                            for kc in range(8):
                                TE.emit(
                                    "matmul", b[:],
                                    Wb[:, kc * D + 128 * mt:kc * D + 128 * (mt + 1)],
                                    ln1T[:, 512 * kc:512 * (kc + 1)],
                                    start=(kc == 0), stop=(kc == 7))
                            toks[("qkv_mm", sb, proj, mt)] = TE.tok()

                        @sec_v
                        def _(ve, outbuf=outbuf, mt=mt, b=b, sb=sb,
                              proj=proj):
                            VE.wait(toks[("qkv_mm", sb, proj, mt)])
                            toks[("qkv_ev", sb, proj, mt)] = VE.emit(
                                "tensor_copy",
                                outbuf[:, mt * S + 512 * sb:mt * S + 512 * (sb + 1)],
                                b[:])
                        bank_last[bidx] = ("qkv_ev", sb, proj, mt)

                qkv_mm(Wqb, qT, sb, 0)
                qkv_mm(Wkb, kT, sb, 1)
                for ms in range(4):
                    for n in range(2):
                        j = ms * 2 + n
                        bidx = (16 + j) % 3
                        b = psQK[bidx]

                        @sec_t
                        def _(te, ms=ms, n=n, b=b, sb=sb, j=j,
                              lk=bank_last.get(bidx)):
                            TE.wait(toks[("lnT", 4 * sb + 3)])
                            TE.wait(toks[("wcast", 2, 3)])
                            if lk is not None:
                                TE.wait(toks[lk])
                            for kc in range(8):
                                TE.emit(
                                    "matmul", b[:],
                                    ln1T[:, 512 * kc + 128 * ms:512 * kc + 128 * (ms + 1)],
                                    Wvb[:, kc * D + 512 * n:kc * D + 512 * (n + 1)],
                                    start=(kc == 0), stop=(kc == 7))
                            toks[("qkv_mm", sb, 2, j)] = TE.tok()

                        @sec_v
                        def _(ve, ms=ms, n=n, b=b, sb=sb, j=j):
                            st = 4 * sb + ms
                            VE.wait(toks[("qkv_mm", sb, 2, j)])
                            VE.wait(toks["vones"])
                            dst = vaug[:, st * H * VW + 8 * n * VW:
                                       st * H * VW + (8 * n + 8) * VW]
                            dst = dst.rearrange("p (h w) -> p h w", w=VW)
                            src = b[:].rearrange("p (h w) -> p h w", w=DH)
                            toks[("qkv_ev", sb, 2, j)] = VE.emit(
                                "tensor_copy", dst[:, :, 0:DH], src[:])
                        bank_last[bidx] = ("qkv_ev", sb, 2, j)

            flush_phase()
            toks["qkv_done_v"] = VE.tok()

        esA2 = ExitStack()
        with esA2:
            PT = esA2.enter_context(nc.sbuf_tensor("PT", [128, 4 * 16 * 512], BF16))
            PTW = 16 * 512
            x1st = esA2.enter_context(nc.sbuf_tensor("x1st", [128, 4 * D], F32))
            sqb = esA2.enter_context(nc.sbuf_tensor("sqb", [128, 2048], BF16))
            wbuf = esA2.enter_context(nc.sbuf_tensor("wbuf", [128, 2 * 2048], F32))
            t1s = esA2.enter_context(nc.sbuf_tensor("t1s", [128, 2 * 2048], BF16))
            recb = esA2.enter_context(nc.sbuf_tensor("recb", [128, 4], F32))
            psS4 = esA2.enter_context(nc.psum_tensor("psS4", [128, 2048], F32))
            psPV0 = esA2.enter_context(nc.psum_tensor("psPV0", [128, 512], F32))
            psPV1 = esA2.enter_context(nc.psum_tensor("psPV1", [128, 512], F32))
            psN = esA2.enter_context(nc.psum_tensor("psN", [128, 512], F32))
            psS = [psS4[:, 0:1024], psS4[:, 1024:2048]]
            psPV = [psPV0, psPV1]

            for wix, buf in ((0, qT), (1, kT)):
                for n in range(4):
                    def sq_half(wix, n, hf, buf):
                        @sec_v
                        def _(ve, buf=buf, wix=wix, n=n, hf=hf):
                            VE.wait(toks["qkv_done_v"])
                            if hf == 1:
                                VE.wait(toks[("nrm_mm_h", wix, n)])
                            if (wix, n) != (0, 0):
                                pw = (wix, n - 1) if n else (wix - 1, 3)
                                VE.wait(toks[("nrm_mm",) + pw])
                            src = buf[:].rearrange(
                                "p (c s) -> p c s",
                                s=S)[:, 4 * hf:4 * hf + 4,
                                     512 * n:512 * (n + 1)]
                            dst = sqb[:].rearrange("p (c s) -> p c s",
                                                   s=512)
                            toks[("sq", wix, n, hf)] = VE.emit(
                                "tensor_tensor", dst, src, src, ALU.mult)

                    sq_half(wix, n, 0, buf)

                    @sec_t
                    def _(te, wix=wix, n=n):
                        TE.wait(toks["consts"])
                        if (wix, n) != (0, 0):
                            pw = (wix, n - 1) if n else (wix - 1, 3)
                            TE.wait(toks[("nrm_rd",) + pw])
                        for c in range(4):
                            TE.wait(toks[("sq", wix, n, 0)])
                            TE.emit("matmul", psN[0:1, 0:512],
                                    onesb[:, 0:1],
                                    sqb[:, 512 * c:512 * (c + 1)],
                                    start=(c == 0), stop=False)
                        toks[("nrm_mm_h", wix, n)] = TE.tok()

                    sq_half(wix, n, 1, buf)

                    @sec_t
                    def _(te, wix=wix, n=n):
                        for c in range(4):
                            TE.wait(toks[("sq", wix, n, 1)])
                            TE.emit("matmul", psN[0:1, 0:512],
                                    onesb[:, 0:1],
                                    sqb[:, 512 * c:512 * (c + 1)],
                                    start=False, stop=(c == 3))
                        toks[("nrm_mm", wix, n)] = TE.tok()

                    @sec_v
                    def _(ve, wix=wix, n=n):
                        VE.wait(toks[("nrm_mm", wix, n)])
                        col = 8 * wix + n
                        toks[("nrm_rd", wix, n)] = VE.emit(
                            "tensor_reduce", nrm[0:1, col:col + 1],
                            psN[0:1, 0:512], AX.X, ALU.max)

            @sec_v
            def _(ve):
                VE.emit("tensor_reduce", nrm[0:1, 16:17], nrm[0:1, 0:4],
                        AX.X, ALU.max)
                VE.emit("tensor_reduce", nrm[0:1, 17:18], nrm[0:1, 8:12],
                        AX.X, ALU.max)
                toks["c2"] = VE.emit(
                    "tensor_tensor", nrm[0:1, 18:19], nrm[0:1, 16:17],
                    nrm[0:1, 17:18], ALU.mult)

            @sec_s
            def _(se):
                SE.wait(toks["c2"])
                SE.emit("activation", out=nrm[0:1, 19:20],
                        in_=nrm[0:1, 18:19], func=AF.Ln)
                toks["cval"] = SE.emit(
                    "activation", out=nrm[0:1, 20:21],
                    in_=nrm[0:1, 19:20], func=AF.Exp, scale=0.5)

            @sec_v
            def _(ve):
                VE.wait(toks["cval"])
                VE.emit("tensor_scalar", nrm[0:1, 21:22], nrm[0:1, 20:21],
                        0.125, 80.0, ALU.mult, ALU.min)
                toks["negb"] = VE.emit(
                    "tensor_scalar_mul", nrm[0:1, 22:23], nrm[0:1, 21:22],
                    -1.0)

            @sec_t
            def _(te):
                TE.wait(toks["negb"])
                TE.emit("matmul", psN[:, 0:1], onesrow[0:1, :],
                        nrm[0:1, 22:23], start=True, stop=True)
                toks["negb_mm"] = TE.tok()

            @sec_v
            def _(ve):
                VE.wait(toks["negb_mm"])
                toks["expbias"] = VE.emit(
                    "tensor_copy", smalls[:, 6:7], psN[:, 0:1])

            if debug:
                @sec_v
                def _(ve):
                    toks["dbg_nb"] = VE.emit("tensor_copy", nrm[0:1, 32:39],
                                             nrm[0:1, 16:23])

                @sec_sq
                def _(sy):
                    SQ.emit("dma_start", out=dbg["d_nb"][0:1, 0:7],
                            in_=nrm[0:1, 32:39], waits=[toks["dbg_nb"]])

            def mk_w1_abs(i):
                c, hf = divmod(i, 2)

                def f():
                    @sec_sq
                    def _(sy):
                        w = [toks["qkv_done_v"]]
                        if i >= 2:
                            w.append(toks[("w1_abs", i - 2)])
                        toks[("w1_dma", i)] = SQ.emit(
                            "dma_start",
                            out=wbuf[:, (i % 2) * 2048:(i % 2 + 1) * 2048],
                            in_=w1t_h[128 * c:128 * (c + 1),
                                      2048 * hf:2048 * (hf + 1)], waits=w)

                    @sec_s
                    def _(se):
                        SE.wait(toks[("w1_dma", i)])
                        wsl = wbuf[:, (i % 2) * 2048:(i % 2 + 1) * 2048]
                        toks[("w1_abs", i)] = SE.emit(
                            "activation", out=wsl, in_=wsl,
                            func=AF.Abs, accum_out=smalls[:, 32 + i:33 + i])
                return f

            def mk_w2_abs(i):
                def f():
                    @sec_sq
                    def _(sy):
                        w = [toks["qkv_done_v"]]
                        if i >= 2:
                            w.append(toks[("w2_abs", i - 2)])
                        else:
                            w.append(toks[("w1_abs", 14 + i)])
                        toks[("w2_dma", i)] = SQ.emit(
                            "dma_start",
                            out=wbuf[:, (i % 2) * 2048:(i % 2 + 1) * 2048]
                            .rearrange("p (c d) -> p c d", d=D),
                            in_=w2t_h[256 * i:256 * (i + 1), :]
                            .rearrange("(c p) d -> p c d", p=128), waits=w)

                    @sec_s
                    def _(se):
                        SE.wait(toks[("w2_dma", i)])
                        wsl = wbuf[:, (i % 2) * 2048:(i % 2 + 1) * 2048]
                        toks[("w2_abs", i)] = SE.emit(
                            "activation", out=wsl, in_=wsl,
                            func=AF.Abs, accum_out=smalls[:, 48 + i:49 + i])
                return f

            def mk_finalize(which):
                base = 32 if which == 1 else 48
                scol = 8 if which == 1 else 10
                icol = 9 if which == 1 else 11
                nelem = float(D * HID)

                def f():
                    @sec_v
                    def _(ve):
                        toks[("wfin_red", which)] = VE.emit(
                            "tensor_reduce", smalls[:, 30:31],
                            smalls[:, base:base + 16], AX.X, ALU.add,
                            waits=[toks[("w1_abs" if which == 1 else
                                         "w2_abs", 15)]])

                    @sec_t
                    def _(te):
                        TE.wait(toks[("wfin_red", which)])
                        if which == 2:
                            TE.wait(toks[("wsc", 1)])
                        TE.emit("matmul", psN[0:1, 8:9], onesf[:, 0:1],
                                smalls[:, 30:31], start=True, stop=True)
                        toks[("wfin_mm", which)] = TE.tok()

                    @sec_v
                    def _(ve):
                        VE.wait(toks[("wfin_mm", which)])
                        VE.emit("tensor_scalar", nrm[0:1, 40:41],
                                psN[0:1, 8:9], 1.0 / nelem, 1e-5,
                                ALU.mult, ALU.max)
                        toks[("wfin_sc", which)] = VE.emit(
                            "reciprocal", nrm[0:1, 41:42], nrm[0:1, 40:41])

                    @sec_t
                    def _(te):
                        TE.wait(toks[("wfin_sc", which)])
                        TE.emit("matmul", psN[:, 9:10], onesrow[0:1, :],
                                nrm[0:1, 41:42], start=True, stop=True)
                        TE.emit("matmul", psN[:, 10:11], onesrow[0:1, :],
                                nrm[0:1, 40:41], start=True, stop=True)
                        toks[("wfin_bc", which)] = TE.tok()

                    @sec_v
                    def _(ve):
                        VE.wait(toks[("wfin_bc", which)])
                        VE.emit("tensor_copy", smalls[:, scol:scol + 1],
                                psN[:, 9:10])
                        toks[("wsc", which)] = VE.emit(
                            "tensor_copy", smalls[:, icol:icol + 1],
                            psN[:, 10:11])
                return f

            def mk_w1_quant(i):
                c, hf = divmod(i, 2)

                def f():
                    @sec_sq
                    def _(sy):
                        w = []
                        if i >= 2:
                            w.append(toks[("w1q_t", i - 2)])
                        else:
                            w.append(toks[("w2_abs", 14 + i)])
                        toks[("w1q_dma", i)] = SQ.emit(
                            "dma_start",
                            out=wbuf[:, (i % 2) * 2048:(i % 2 + 1) * 2048],
                            in_=w1t_h[128 * c:128 * (c + 1),
                                      2048 * hf:2048 * (hf + 1)], waits=w)

                    @sec_v
                    def _(ve):
                        VE.wait(toks[("w1q_dma", i)])
                        VE.wait(toks[("wsc", 1)])
                        if i >= 2:
                            VE.wait(toks[("w1q_out", i - 2)])
                        wsl = wbuf[:, (i % 2) * 2048:(i % 2 + 1) * 2048]
                        tsl = t1s[:, (i % 2) * 2048:(i % 2 + 1) * 2048]
                        VE.emit("tensor_scalar", tsl, wsl,
                                smalls[:, 8:9], 0.5, ALU.mult, ALU.is_ge)
                        VE.emit("tensor_scalar", wsl, wsl,
                                smalls[:, 8:9], -0.5, ALU.mult, ALU.is_le)
                        toks[("w1q_t", i)] = VE.emit(
                            "tensor_tensor", tsl, tsl, wsl, ALU.subtract)

                    @sec_sq
                    def _(sy):
                        toks[("w1q_out", i)] = SQ.emit(
                            "dma_start",
                            out=t1scr[128 * c:128 * (c + 1),
                                      2048 * hf:2048 * (hf + 1)],
                            in_=t1s[:, (i % 2) * 2048:(i % 2 + 1) * 2048],
                            waits=[toks[("w1q_t", i)]])
                return f

            def mk_w2_quant(i):
                def f():
                    @sec_sq
                    def _(sy):
                        w = []
                        if i >= 2:
                            w.append(toks[("w2q_t", i - 2)])
                        else:
                            w.append(toks[("w1q_t", 14 + i)])
                        toks[("w2q_dma", i)] = SQ.emit(
                            "dma_start",
                            out=wbuf[:, (i % 2) * 2048:(i % 2 + 1) * 2048]
                            .rearrange("p (c d) -> p c d", d=D),
                            in_=w2t_h[256 * i:256 * (i + 1), :]
                            .rearrange("(c p) d -> p c d", p=128), waits=w)

                    @sec_v
                    def _(ve):
                        VE.wait(toks[("w2q_dma", i)])
                        VE.wait(toks[("wsc", 2)])
                        if i >= 2:
                            VE.wait(toks[("w2q_out", i - 2)])
                        else:
                            VE.wait(toks[("w1q_out", 14 + i)])
                        wsl = wbuf[:, (i % 2) * 2048:(i % 2 + 1) * 2048]
                        tsl = t1s[:, (i % 2) * 2048:(i % 2 + 1) * 2048]
                        VE.emit("tensor_scalar", tsl, wsl,
                                smalls[:, 10:11], 0.5, ALU.mult, ALU.is_ge)
                        VE.emit("tensor_scalar", wsl, wsl,
                                smalls[:, 10:11], -0.5, ALU.mult, ALU.is_le)
                        toks[("w2q_t", i)] = VE.emit(
                            "tensor_tensor", tsl, tsl, wsl, ALU.subtract)

                    @sec_sq
                    def _(sy):
                        toks[("w2q_out", i)] = SQ.emit(
                            "dma_start",
                            out=t2scr[256 * i:256 * (i + 1), :]
                            .rearrange("(c p) d -> p c d", p=128),
                            in_=t1s[:, (i % 2) * 2048:(i % 2 + 1) * 2048]
                            .rearrange("p (c d) -> p c d", d=D),
                            waits=[toks[("w2q_t", i)]])
                return f

            wsteps = ([mk_w1_abs(i) for i in range(16)] + [mk_finalize(1)]
                      + [mk_w2_abs(i) for i in range(16)] + [mk_finalize(2)]
                      + [mk_w1_quant(i) for i in range(16)]
                      + [mk_w2_quant(i) for i in range(16)])
            wstep_i = [0]

            def emit_wsteps(k):
                while k > 0 and wstep_i[0] < len(wsteps):
                    wsteps[wstep_i[0]]()
                    wstep_i[0] += 1
                    k -= 1

            for t in range(NQT):
                KC = 4 * t + 4
                q0 = QTW * t
                xpar = t % 2

                @sec_gq
                def _(g, t=t):
                    w = [toks["qkv_done_v"]]
                    if t >= 1:
                        w.append(toks[("x1_store", t - 1)])
                    toks[("x1_load", t)] = GQ.emit(
                        "dma_start",
                        out=x1st[:].rearrange("p (c d) -> p c d", d=D),
                        in_=x_h[QTW * t:QTW * (t + 1), :]
                        .rearrange("(c p) d -> p c d", p=128), waits=w)

                for hp in range(8):
                    hA, hB = 2 * hp, 2 * hp + 1

                    def emit_sc(t, hp, kc, KC, q0):
                        hpar = kc % 2

                        @sec_t
                        def _(te, t=t, hp=hp, kc=kc, hpar=hpar, KC=KC,
                              q0=q0):
                            if kc >= 2:
                                TE.wait(toks[("exp", t, hp, kc - 2)])
                            elif hp >= 1:
                                TE.wait(toks[("exp", t, hp - 1, KC - 2 + kc)])
                            elif t >= 1:
                                TE.wait(toks[("exp", t - 1, 7, 4 * t - 2 + kc)])
                            if kc == 0:
                                if hp >= 1:
                                    TE.wait(toks[("pv", t, hp - 1, 1)])
                                elif t >= 1:
                                    TE.wait(toks[("pv", t - 1, 7, 1)])
                            TE.wait(toks["qkv_done_v"])
                            bP = psS[hpar]
                            TE.emit("matmul", bP[:, 0:QTW],
                                    kT[0:64, hp * S + 128 * kc:hp * S + 128 * (kc + 1)],
                                    qT[0:64, hp * S + q0:hp * S + q0 + QTW],
                                    start=True, stop=True,
                                    tile_position=(0, 0),
                                    skip_group_check=True)
                            TE.emit("matmul", bP[:, QTW:2 * QTW],
                                    kT[64:128, hp * S + 128 * kc:hp * S + 128 * (kc + 1)],
                                    qT[64:128, hp * S + q0:hp * S + q0 + QTW],
                                    start=True, stop=True,
                                    tile_position=(64, 0),
                                    skip_group_check=True)
                            toks[("sc", t, hp, kc)] = TE.tok()

                    def exp_pvwaits(t, hp, kc):
                        SE.wait(toks["expbias"])
                        if kc == 0:
                            if hp >= 2:
                                SE.wait(toks[("pv", t, hp - 2, 0)])
                                SE.wait(toks[("pv", t, hp - 2, 1)])
                            elif t >= 1:
                                SE.wait(toks[("pv", t - 1, 6 + hp, 0)])
                                SE.wait(toks[("pv", t - 1, 6 + hp, 1)])

                    def emit_exp1(t, hp, kc):
                        hpar = kc % 2

                        @sec_s
                        def _(se, t=t, hp=hp, kc=kc, hpar=hpar):
                            SE.wait(toks[("sc", t, hp, kc)])
                            exp_pvwaits(t, hp, kc)
                            c0 = max(0, (kc - 4 * t) * 128)
                            bP = psS[hpar]
                            src_ap = bP[:].rearrange(
                                "p (h w) -> p h w", w=QTW)[:, :, c0:QTW]
                            PTp = PT[:, (hp % 2) * 2 * PTW:
                                      (hp % 2 + 1) * 2 * PTW]
                            dst_ap = PTp.rearrange(
                                "p (h w) -> p h w", w=16 * 512)[
                                :, :, 512 * kc + c0:512 * (kc + 1)]
                            toks[("exp", t, hp, kc)] = SE.emit(
                                "activation", out=dst_ap, in_=src_ap,
                                func=AF.Exp, scale=0.125,
                                bias=smalls[:, 6:7])

                    def emit_exp2(t, hp, kc):
                        @sec_s
                        def _(se, t=t, hp=hp, kc=kc):
                            SE.wait(toks[("sc", t, hp, kc)])
                            SE.wait(toks[("sc", t, hp, kc + 1)])
                            exp_pvwaits(t, hp, kc)
                            src_ap = psS4[:].rearrange(
                                "p (x h w) -> p h x w", x=2, h=2, w=QTW)
                            PTp = PT[:, (hp % 2) * 2 * PTW:
                                      (hp % 2 + 1) * 2 * PTW]
                            dst_ap = PTp.rearrange(
                                "p (h w) -> p h w", w=16 * 512)[
                                :, :, 512 * kc:512 * (kc + 2)]
                            dst_ap = dst_ap.rearrange(
                                "p h (x w) -> p h x w", w=QTW)
                            tok = SE.emit(
                                "activation", out=dst_ap, in_=src_ap,
                                func=AF.Exp, scale=0.125,
                                bias=smalls[:, 6:7])
                            toks[("exp", t, hp, kc)] = tok
                            toks[("exp", t, hp, kc + 1)] = tok

                    kc = 0
                    while kc < KC:
                        if kc + 1 < 4 * t:
                            emit_sc(t, hp, kc, KC, q0)
                            emit_sc(t, hp, kc + 1, KC, q0)
                            emit_exp2(t, hp, kc)
                            kc += 2
                        else:
                            emit_sc(t, hp, kc, KC, q0)
                            emit_exp1(t, hp, kc)
                            kc += 1

                    PTp_A = PT[:, (hp % 2) * 2 * PTW:
                               (hp % 2) * 2 * PTW + PTW]
                    PTp_B = PT[:, (hp % 2) * 2 * PTW + PTW:
                               (hp % 2 + 1) * 2 * PTW]

                    @sec_v
                    def _(ve, t=t, hp=hp, PTp_A=PTp_A, PTp_B=PTp_B):
                        for j in range(4):
                            kc = 4 * t + j
                            VE.wait(toks[("exp", t, hp, kc)])
                            for P in (PTp_A, PTp_B):
                                sl = P[:, 512 * kc + 128 * j:512 * kc + 128 * (j + 1)]
                                VE.emit("tensor_tensor", sl, sl, maskt[:],
                                        ALU.mult)
                        toks[("mask", t, hp)] = VE.tok()

                    for hi, (head, P) in enumerate(((hA, PTp_A),
                                                    (hB, PTp_B))):
                        pv = psPV[hi]

                        @sec_t
                        def _(te, t=t, hp=hp, hi=hi, head=head, P=P, pv=pv):
                            if hp >= 1:
                                TE.wait(toks[("epi", t, hp - 1, hi)])
                            elif t >= 1:
                                TE.wait(toks[("epi", t - 1, 7, hi)])
                            TE.wait(toks[("exp", t, hp, 4 * t + 3)])
                            TE.wait(toks[("mask", t, hp)])
                            first = True
                            for m in range(4):
                                for kc in range(4 * t + m + 1):
                                    last = (m == 3) and (kc == 4 * t + 3)
                                    TE.emit(
                                        "matmul", pv[:, VW * m:VW * m + VW],
                                        P[:, 512 * kc + 128 * m:512 * kc + 128 * (m + 1)],
                                        vaug[:, kc * H * VW + head * VW:
                                             kc * H * VW + (head + 1) * VW],
                                        start=first, stop=last,
                                        skip_group_check=True)
                                    first = False
                            toks[("pv", t, hp, hi)] = TE.tok()

                        @sec_v
                        def _(ve, t=t, hp=hp, hi=hi, head=head, pv=pv,
                              xpar=xpar):
                            VE.wait(toks[("pv", t, hp, hi)])
                            VE.wait(toks[("x1_load", t)])
                            rs = pv[:, 0:4 * VW].rearrange(
                                "p (m w) -> p m w", w=VW)[:, :, DH:DH + 1]
                            VE.emit("reciprocal", recb[:, 0:4], rs)
                            for m in range(4):
                                xsl = x1st[:, m * D + head * DH:
                                           m * D + (head + 1) * DH]
                                VE.emit("scalar_tensor_tensor", xsl,
                                        pv[:, VW * m:VW * m + DH],
                                        recb[:, m:m + 1], xsl,
                                        ALU.mult, ALU.add)
                            toks[("epi", t, hp, hi)] = VE.tok()

                    emit_wsteps(2)

                @sec_gq
                def _(g, t=t):
                    toks[("x1_store", t)] = GQ.emit(
                        "dma_start",
                        out=x1scr[QTW * t:QTW * (t + 1), :]
                        .rearrange("(c p) d -> p c d", p=128),
                        in_=x1st[:].rearrange("p (c d) -> p c d", d=D),
                        waits=[toks[("epi", t, 7, 0)], toks[("epi", t, 7, 1)]])

            emit_wsteps(len(wsteps))

            if debug:
                @sec_sq
                def _(sy):
                    SQ.emit("dma_start", out=dbg["d_qt"][:], in_=qT[:],
                            waits=[toks["qkv_done_v"]])
                    SQ.emit("dma_start", out=dbg["d_kt"][:], in_=kT[:])
                    SQ.emit("dma_start", out=dbg["d_vaug"][:], in_=vaug[:])
                    SQ.emit("dma_start", out=dbg["d_x1"][:], in_=x1scr[:],
                            waits=[toks[("x1_store", t)] for t in range(4)])
                    SQ.emit("dma_start", out=dbg["d_t1t"][:], in_=t1scr[:],
                            waits=[toks[("w1q_out", i)] for i in range(16)])
                    SQ.emit("dma_start", out=dbg["d_t2t"][:], in_=t2scr[:],
                            waits=[toks[("w2q_out", i)] for i in range(16)])

            flush_phase()
            toks["A_t"] = TE.tok()
            toks["A_v"] = VE.tok()
            toks["A_a"] = SE.tok()
            toks["A_g"] = GE.tok()
            toks["A_sq"] = SQ.toks()
            toks["A_gq"] = GQ.toks()

        esA.close()
        barrier = ([toks["A_t"], toks["A_v"], toks["A_a"], toks["A_g"]]
                   + toks["A_sq"] + toks["A_gq"])

        esB = ExitStack()
        with esB:
            t2T = esB.enter_context(nc.sbuf_tensor("t2T", [128, 32 * D], BF16))
            xqT = esB.enter_context(nc.sbuf_tensor("xqT", [128, 8 * (S + 2)], BF16))
            hsil = esB.enter_context(nc.sbuf_tensor("hsil", [128, 32 * GRP], F32))
            hq = esB.enter_context(nc.sbuf_tensor("hq", [128, 32 * GRP], BF16))
            t1b = esB.enter_context(nc.sbuf_tensor("t1b", [128, 2 * 8 * 512], BF16))
            x1t = esB.enter_context(nc.sbuf_tensor("x1t", [128, 2 * D], F32))
            x1g = esB.enter_context(nc.sbuf_tensor("x1g", [128, 2 * 2 * D], F32))
            hf16 = esB.enter_context(nc.sbuf_tensor("hf16", [128, 2048], F16))
            xqf16 = hf16[:, 0:D]
            xqs = esB.enter_context(nc.sbuf_tensor("xqs", [128, 2 * D], BF16))
            s2row = esB.enter_context(nc.sbuf_tensor("s2row", [128, GRP], F32))
            tmpc = esB.enter_context(nc.sbuf_tensor("tmpc", [128, 2 * GRP], F32))
            cbuf = esB.enter_context(nc.sbuf_tensor("cbuf", [128, 8 * GRP], F32))
            s2st = esB.enter_context(nc.sbuf_tensor("s2st", [1, 256], F32))
            amx = esB.enter_context(nc.sbuf_tensor("amx", [128, GRP], F32))
            amc = esB.enter_context(nc.sbuf_tensor("amc", [128, 8], F32))
            bn6b = esB.enter_context(nc.sbuf_tensor("bn6b", [128, 12], F32))
            psM0 = esB.enter_context(nc.psum_tensor("psM0", [128, 512], F32))
            psM1 = esB.enter_context(nc.psum_tensor("psM1", [128, 512], F32))
            psM2 = esB.enter_context(nc.psum_tensor("psM2", [128, 512], F32))
            psM3 = esB.enter_context(nc.psum_tensor("psM3", [128, 512], F32))
            psO0 = esB.enter_context(nc.psum_tensor("psO0", [128, 512], F32))
            psO1 = esB.enter_context(nc.psum_tensor("psO1", [128, 512], F32))
            psNb = esB.enter_context(nc.psum_tensor("psNb", [128, 512], F32))
            psTb = esB.enter_context(nc.psum_tensor("psTb", [128, 1024], BF16))
            psM = [psM0, psM1, psM2, psM3]
            psO = [psO0, psO1]

            @sec_sq
            def _(sy):
                for q4 in range(4):
                    toks[("t2T", q4)] = SQ.emit(
                        "dma_start",
                        out=t2T[:, q4 * 8 * D:(q4 + 1) * 8 * D]
                        .rearrange("p (c d) -> p c d", d=D),
                        in_=t2scr[1024 * q4:1024 * (q4 + 1), :]
                        .rearrange("(c p) d -> p c d", p=128),
                        waits=barrier + [toks[("w2q_out", i)]
                                         for i in range(16)])

            @sec_g
            def _(g):
                ap = xqT[:].rearrange("p (c s) -> p c s", s=S + 2)
                GE.emit("memset", ap[:, :, 0:1], 0.0, waits=barrier)
                toks["xq_guard"] = GE.emit("memset", ap[:, :, S + 1:S + 2],
                                           0.0)

            def emit_xq_tile(st):
                par = st % 2

                @sec_gq
                def _(g, st=st, par=par):
                    w = list(barrier)
                    if st >= 2:
                        w.append(toks[("xq_done", st - 2)])
                    toks[("x1t_dma", st)] = GQ.emit(
                        "dma_start", out=x1t[:, par * D:(par + 1) * D],
                        in_=x1scr[128 * st:128 * (st + 1), :], waits=w)

                @sec_v
                def _(ve, st=st, par=par):
                    xs = x1t[:, par * D:(par + 1) * D]
                    VE.wait(toks[("x1t_dma", st)])
                    for b in barrier:
                        VE.wait(b)
                    VE.emit("bn_stats", bn6b[:, 0:6], xs[:, 0:512])
                    VE.emit("bn_stats", bn6b[:, 6:12], xs[:, 512:1024])
                    VE.emit("bn_aggr", smalls[:, 2:4], bn6b[:, 0:12])
                    VE.emit("tensor_reduce", smalls[:, 16:17], xs, AX.X,
                            ALU.max)
                    VE.emit("tensor_reduce", smalls[:, 17:18], xs, AX.X,
                            ALU.max, negate=True)
                    toks[("veps2", st)] = VE.emit(
                        "tensor_scalar_add", smalls[:, 4:5], smalls[:, 3:4],
                        1e-5)

                @sec_s
                def _(se, st=st):
                    SE.wait(toks[("veps2", st)])
                    for b in barrier:
                        SE.wait(b)
                    SE.emit("activation", out=smalls[:, 5:6],
                            in_=smalls[:, 4:5], func=AF.Ln)
                    toks[("rstd2", st)] = SE.emit(
                        "activation", out=smalls[:, 0:1], in_=smalls[:, 5:6],
                        func=AF.Exp, scale=-0.5)

                @sec_v
                def _(ve, st=st, par=par):
                    VE.wait(toks[("rstd2", st)])
                    VE.emit("tensor_tensor", smalls[:, 18:19],
                            smalls[:, 16:17], smalls[:, 2:3], ALU.subtract)
                    VE.emit("tensor_tensor", smalls[:, 19:20],
                            smalls[:, 17:18], smalls[:, 2:3], ALU.add)
                    VE.emit("tensor_tensor", smalls[:, 20:21],
                            smalls[:, 18:19], smalls[:, 19:20], ALU.max)
                    VE.emit("tensor_tensor", smalls[:, 21:22],
                            smalls[:, 20:21], smalls[:, 0:1], ALU.mult)
                    VE.emit("tensor_scalar_max", smalls[:, 22:23],
                            smalls[:, 21:22], 1e-5)
                    VE.emit("reciprocal", smalls[:, 23:24], smalls[:, 22:23])
                    VE.emit("tensor_scalar_mul", smalls[:, 24:25],
                            smalls[:, 23:24], 127.0)
                    VE.emit("tensor_scalar_mul", smalls[:, 25:26],
                            smalls[:, 22:23], 1.0 / 127.0)
                    VE.emit("tensor_tensor", smalls[:, 26:27],
                            smalls[:, 25:26], smalls[:, 9:10], ALU.mult)
                    VE.emit("tensor_tensor", smalls[:, 12:13],
                            smalls[:, 0:1], smalls[:, 24:25], ALU.mult)
                    VE.emit("scalar_tensor_tensor", smalls[:, 13:14],
                            smalls[:, 2:3], -1.0, smalls[:, 12:13],
                            ALU.mult, ALU.mult)
                    VE.emit("tensor_scalar_add", smalls[:, 13:14],
                            smalls[:, 13:14], 1536.0)
                    toks[("xqpre", st)] = VE.emit(
                        "tensor_scalar_mul", smalls[:, 27:28],
                        smalls[:, 26:27], -1536.0)

                @sec_s
                def _(se, st=st, par=par):
                    SE.wait(toks[("xqpre", st)])
                    if st >= 2:
                        SE.wait(toks[("xqtr", st - 2)])
                    SE.emit("activation", out=xqf16,
                            in_=x1t[:, par * D:(par + 1) * D],
                            func=AF.Identity, scale=smalls[:, 12:13],
                            bias=smalls[:, 13:14])
                    toks[("xqs", st)] = SE.emit(
                        "activation", out=xqs[:, par * D:(par + 1) * D],
                        in_=xqf16, func=AF.Identity,
                        scale=smalls[:, 26:27], bias=smalls[:, 27:28])

            def emit_xq_back(st):
                par = st % 2

                @sec_t
                def _(te, st=st, par=par):
                    TE.wait(toks[("xqs", st)])
                    for b in barrier:
                        TE.wait(b)
                    if st >= 1:
                        TE.wait(toks[("xq_done", st - 1)])
                    for c in range(8):
                        TE.emit("transpose", psTb[:, 128 * c:128 * (c + 1)],
                                xqs[:, par * D + 128 * c:par * D + 128 * (c + 1)],
                                idbf[:])
                    toks[("xqtr", st)] = TE.tok()

                @sec_v
                def _(ve, st=st):
                    VE.wait(toks[("xqtr", st)])
                    src = psTb[:].rearrange("p (c w) -> p c w", w=128)
                    dst = xqT[:].rearrange("p (c s) -> p c s", s=S + 2)
                    toks[("xq_done", st)] = VE.emit(
                        "tensor_copy",
                        dst[:, :, 1 + 128 * st:1 + 128 * (st + 1)], src[:])

            def emit_group_A(g):
                gpar = g % 2
                qg = g - 1

                @sec_gq
                def _(gp, g=g, gpar=gpar):
                    w = list(barrier)
                    if g >= 2:
                        w.append(toks[("out_store", g - 2)])
                    toks[("x1g_dma", g)] = GQ.emit(
                        "dma_start",
                        out=x1g[:, gpar * 2 * D:(gpar + 1) * 2 * D]
                        .rearrange("p (c d) -> p c d", d=D),
                        in_=x1scr[GRP * g:GRP * (g + 1), :]
                        .rearrange("(c p) d -> p c d", p=128), waits=w)

                for e in range(8):
                    bpar = (g * 8 + e) % 2
                    if qg >= 0 and e < 2:
                        emit_hmul_ve(qg, e)

                    @sec_sq
                    def _(sy, g=g, e=e, bpar=bpar):
                        w = list(barrier)
                        w += [toks[("w1q_out", i)] for i in range(16)]
                        if g * 8 + e >= 2:
                            w.append(toks[("t1_used", g * 8 + e - 2)])
                        toks[("t1_dma", g, e)] = SQ.emit(
                            "dma_start",
                            out=t1b[:, bpar * 8 * 512:(bpar + 1) * 8 * 512]
                            .rearrange("p (c h) -> p c h", h=512),
                            in_=t1scr[:, 512 * e:512 * (e + 1)]
                            .rearrange("(c p) h -> p c h", p=128), waits=w)

                    for ht in range(4):
                        hh = 4 * e + ht
                        bank = psM[hh % 4]
                        cpar = hh % 2

                        @sec_t
                        def _(te, g=g, e=e, ht=ht, hh=hh, bank=bank,
                              bpar=bpar):
                            TE.wait(toks[("t1_dma", g, e)])
                            TE.wait(toks[("xq_done", min(2 * g + 2, NT - 1))])
                            TE.wait(toks["xq_guard"])
                            if hh >= 4:
                                TE.wait(toks[("conv", g, hh - 4)])
                            elif g >= 1:
                                TE.wait(toks[("conv", g - 1, hh + 28)])
                            for kc in range(8):
                                TE.emit(
                                    "matmul", bank[:, 0:GRP + 2],
                                    t1b[:, bpar * 8 * 512 + 512 * kc + 128 * ht:
                                        bpar * 8 * 512 + 512 * kc + 128 * (ht + 1)],
                                    xqT[:, (S + 2) * kc + GRP * g:
                                        (S + 2) * kc + GRP * g + GRP + 2],
                                    start=(kc == 0), stop=(kc == 7))
                            toks[("mm1", g, hh)] = TE.tok()
                            if ht == 3:
                                toks[("t1_used", g * 8 + e)] = \
                                    toks[("mm1", g, hh)]

                        @sec_s
                        def _(se, g=g, hh=hh, bank=bank, cpar=cpar):
                            SE.wait(toks[("mm1", g, hh)])
                            SE.wait(toks["consts"])
                            if hh >= 2:
                                SE.wait(toks[("conv", g, hh - 2)])
                            elif g >= 1:
                                SE.wait(toks[("conv", g - 1, hh + 30)])
                            t0 = tmpc[:, cpar * GRP:(cpar + 1) * GRP]
                            toks[("cv1", g, hh)] = SE.emit(
                                "activation", out=t0, in_=bank[:, 0:GRP],
                                func=AF.Identity,
                                scale=convw[:, 3 * hh:3 * hh + 1],
                                bias=convb[:, hh:hh + 1])

                        @sec_v
                        def _(ve, g=g, hh=hh, bank=bank, cpar=cpar):
                            VE.wait(toks[("cv1", g, hh)])
                            qlast = hh - 8 - (hh % 4) + 3
                            if qlast >= 0:
                                VE.wait(toks[("sgm", g, qlast)])
                                VE.wait(toks[("silu", g, qlast)])
                            elif g >= 1:
                                VE.wait(toks[("sgm", g - 1, qlast + 32)])
                                VE.wait(toks[("silu", g - 1, qlast + 32)])
                            t0 = tmpc[:, cpar * GRP:(cpar + 1) * GRP]
                            qpar = (hh // 4) % 2
                            co = cbuf[:, (qpar * 4 + hh % 4) * GRP:
                                      (qpar * 4 + hh % 4 + 1) * GRP]
                            VE.emit("scalar_tensor_tensor", t0,
                                    bank[:, 1:GRP + 1],
                                    convw[:, 3 * hh + 1:3 * hh + 2], t0,
                                    ALU.mult, ALU.add)
                            toks[("conv", g, hh)] = VE.emit(
                                "scalar_tensor_tensor", co,
                                bank[:, 2:GRP + 2],
                                convw[:, 3 * hh + 2:3 * hh + 3], t0,
                                ALU.mult, ALU.add)

                        if hh % 4 == 3:
                            if e == 2 and 2 * g + 5 < NT:
                                emit_xq_tile(2 * g + 5)
                            if e == 5 and 2 * g + 6 < NT:
                                emit_xq_tile(2 * g + 6)
                            if qg >= 0 and e % 2 == 0:
                                emit_shift(qg, e // 2)

                            @sec_s
                            def _(se, g=g, hh=hh):
                                SE.wait(toks[("conv", g, hh)])
                                if g >= 1:
                                    SE.wait(toks[("hqt", g - 1,
                                                  (hh - 3) // 8)])
                                qpar = (hh // 4) % 2
                                cq = cbuf[:, qpar * 4 * GRP:
                                          (qpar + 1) * 4 * GRP]
                                toks[("sgm", g, hh)] = SE.emit(
                                    "activation",
                                    out=hsil[:, GRP * (hh - 3):GRP * (hh + 1)],
                                    in_=cq, func=AF.Sigmoid)

                            @sec_g
                            def _(ge, g=g, hh=hh):
                                GE.wait(toks[("sgm", g, hh)])
                                qpar = (hh // 4) % 2
                                cq = cbuf[:, qpar * 4 * GRP:
                                          (qpar + 1) * 4 * GRP]
                                hsl = hsil[:, GRP * (hh - 3):GRP * (hh + 1)]
                                toks[("silu", g, hh)] = GE.emit(
                                    "tensor_tensor", hsl, cq, hsl, ALU.mult)



            def emit_Bamax(g):
                @sec_v
                def _(ve, g=g):
                    for hh in (3, 7, 11, 15, 19, 23, 27, 31):
                        VE.wait(toks[("silu", g, hh)])
                    src = hsil[:].rearrange("p (c s) -> p s c", c=32)
                    toks[("amaxp", g)] = VE.emit(
                        "tensor_reduce", amx[:], src, AX.X, ALU.max,
                        apply_absolute_value=True)

            def emit_Brest(g):
                @sec_t
                def _(te, g=g):
                    TE.wait(toks[("amaxp", g)])
                    if g >= 1:
                        TE.wait(toks[("s2row", g - 1)])
                    TE.emit("transpose", psNb[0:128, 0:128], amx[:, 0:128],
                            idf[:])
                    TE.emit("transpose", psNb[0:128, 128:256],
                            amx[:, 128:256], idf[:])
                    toks[("amtr", g)] = TE.tok()

                @sec_v
                def _(ve, g=g):
                    ab = (g % 2) * 4
                    VE.wait(toks[("amtr", g)])
                    VE.emit("tensor_reduce", amc[:, ab:ab + 1],
                            psNb[:, 0:128], AX.X, ALU.max)
                    VE.emit("tensor_reduce", amc[:, ab + 1:ab + 2],
                            psNb[:, 128:256], AX.X, ALU.max)
                    VE.emit("tensor_scalar_max", amc[:, ab + 2:ab + 3],
                            amc[:, ab:ab + 1], 1e-5)
                    VE.emit("tensor_scalar_max", amc[:, ab + 3:ab + 4],
                            amc[:, ab + 1:ab + 2], 1e-5)
                    VE.emit("tensor_scalar_mul", amc[:, ab:ab + 1],
                            amc[:, ab + 2:ab + 3], 1.0 / 127.0)
                    VE.emit("tensor_scalar_mul", amc[:, ab + 1:ab + 2],
                            amc[:, ab + 3:ab + 4], 1.0 / 127.0)
                    VE.emit("tensor_tensor", amc[:, ab:ab + 1],
                            amc[:, ab:ab + 1], smalls[:, 11:12], ALU.mult)
                    VE.emit("tensor_tensor", amc[:, ab + 1:ab + 2],
                            amc[:, ab + 1:ab + 2], smalls[:, 11:12],
                            ALU.mult)
                    VE.emit("reciprocal", amc[:, ab + 2:ab + 3],
                            amc[:, ab + 2:ab + 3])
                    VE.emit("reciprocal", amc[:, ab + 3:ab + 4],
                            amc[:, ab + 3:ab + 4])
                    VE.emit("tensor_scalar_mul", amc[:, ab + 2:ab + 3],
                            amc[:, ab + 2:ab + 3], 127.0)
                    toks[("s2col", g)] = VE.emit(
                        "tensor_scalar_mul", amc[:, ab + 3:ab + 4],
                        amc[:, ab + 3:ab + 4], 127.0)

                @sec_t
                def _(te, g=g):
                    ab = (g % 2) * 4
                    TE.wait(toks[("s2col", g)])
                    TE.emit("transpose", psNb[0:1, 256:384],
                            amc[:, ab + 2:ab + 3], idf[:])
                    TE.emit("transpose", psNb[0:1, 384:512],
                            amc[:, ab + 3:ab + 4], idf[:])
                    toks[("s2tr", g)] = TE.tok()

                @sec_v
                def _(ve, g=g):
                    VE.wait(toks[("s2tr", g)])
                    VE.emit("tensor_copy", s2st[0:1, 0:128],
                            psNb[0:1, 256:384])
                    toks[("s2st", g)] = VE.emit(
                        "tensor_copy", s2st[0:1, 128:256],
                        psNb[0:1, 384:512])

                @sec_t
                def _(te, g=g):
                    TE.wait(toks[("s2st", g)])
                    TE.emit("matmul", psNb[:, 256:512], onesrow[0:1, :],
                            s2st[0:1, 0:256], start=True, stop=True)
                    toks[("s2bc", g)] = TE.tok()

                @sec_v
                def _(ve, g=g):
                    VE.wait(toks[("s2bc", g)])
                    if g >= 1:
                        VE.wait(toks[("hmul", g - 1, 3)])
                    toks[("s2row", g)] = VE.emit(
                        "tensor_copy", s2row[:], psNb[:, 256:512])

                if debug and g == 0:
                    @sec_sq
                    def _(sy, g=g):
                        toks["dbg_g0a"] = SQ.emit(
                            "dma_start", out=dbg["d_hsil"][:],
                            in_=hsil[:], waits=[toks[("amaxp", g)]])


            def emit_Q_pool(g):
                @sec_g
                def _(ge, g=g):
                    GE.wait(toks[("s2row", g)])
                    if g == 0 and "dbg_g0a" in toks:
                        GE.wait(toks["dbg_g0a"])
                    for sl in (2, 3):
                        for c in range(8):
                            hh = 8 * sl + c
                            hsl = hsil[:, GRP * hh:GRP * (hh + 1)]
                            GE.emit("tensor_tensor", hsl, hsl, s2row[:],
                                    ALU.mult)
                        toks[("hmul", g, sl)] = GE.tok()

            def emit_hmul_ve(qg, sl):
                @sec_v
                def _(ve, qg=qg, sl=sl):
                    VE.wait(toks[("s2row", qg)])
                    if qg == 0 and "dbg_g0a" in toks:
                        VE.wait(toks["dbg_g0a"])
                    for c in range(8):
                        hh = 8 * sl + c
                        hsl = hsil[:, GRP * hh:GRP * (hh + 1)]
                        VE.emit("tensor_tensor", hsl, hsl, s2row[:],
                                ALU.mult)
                    toks[("hmul", qg, sl)] = VE.tok()

            def emit_shift(qg, sl):
                @sec_s
                def _(se, qg=qg, sl=sl):
                    SE.wait(toks[("hmul", qg, sl)])
                    SE.wait(toks["consts"])
                    if qg >= 1 and sl == 0:
                        SE.wait(toks[("mm2", qg - 1, 1, 1)])
                    if qg == 1 and sl == 0 and "dbg_g0b" in toks:
                        SE.wait(toks["dbg_g0b"])
                    SE.emit("activation", out=hf16[:],
                            in_=hsil[:, 8 * GRP * sl:8 * GRP * (sl + 1)],
                            func=AF.Identity, bias=smalls[:, 7:8])
                    toks[("hqt", qg, sl)] = SE.emit(
                        "activation",
                        out=hq[:, 8 * GRP * sl:8 * GRP * (sl + 1)],
                        in_=hf16[:], func=AF.Identity,
                        bias=smalls[:, 28:29])

            def emit_Cmm(g, m):
                for n in range(2):
                        ob = psO[n]

                        @sec_t
                        def _(te, g=g, m=m, n=n, ob=ob):
                            for sl in range(4):
                                TE.wait(toks[("hqt", g, sl)])
                            for q4 in range(4):
                                TE.wait(toks[("t2T", q4)])
                            if m >= 1:
                                TE.wait(toks[("mm2_ev", g, m - 1, n)])
                            elif g >= 1:
                                TE.wait(toks[("mm2_ev", g - 1, 1, n)])
                            for kc in range(32):
                                TE.emit(
                                    "matmul", ob[:],
                                    hq[:, GRP * kc + 128 * m:GRP * kc + 128 * (m + 1)],
                                    t2T[:, D * kc + 512 * n:D * kc + 512 * (n + 1)],
                                    start=(kc == 0), stop=(kc == 31))
                            toks[("mm2", g, m, n)] = TE.tok()

            def emit_Cev(g, m, store=False):
                gpar = g % 2
                for n in range(2):
                        ob = psO[n]

                        @sec_v
                        def _(ve, g=g, m=m, n=n, ob=ob, gpar=gpar):
                            VE.wait(toks[("mm2", g, m, n)])
                            VE.wait(toks[("x1g_dma", g)])
                            xsl = x1g[:, gpar * 2 * D + m * D + 512 * n:
                                      gpar * 2 * D + m * D + 512 * (n + 1)]
                            toks[("mm2_ev", g, m, n)] = VE.emit(
                                "scalar_tensor_tensor", xsl, ob[:],
                                amc[:, (g % 2) * 4 + m:(g % 2) * 4 + m + 1],
                                xsl, ALU.mult, ALU.add)

                if not store:
                    return

                @sec_gq
                def _(gp, g=g, gpar=gpar):
                    toks[("out_store", g)] = GQ.emit(
                        "dma_start",
                        out=out_h[GRP * g:GRP * (g + 1), :]
                        .rearrange("(c p) d -> p c d", p=128),
                        in_=x1g[:, gpar * 2 * D:(gpar + 1) * 2 * D]
                        .rearrange("p (c d) -> p c d", d=D),
                        waits=[toks[("mm2_ev", g, m, n)]
                               for m in range(2) for n in range(2)])

            for st in range(5):
                emit_xq_tile(st)
                if st < 3:
                    emit_xq_back(st)
            for g in range(NGRP):
                if g >= 1:
                    emit_Q_pool(g - 1)
                for st in (2 * g + 3, 2 * g + 4):
                    if st < NT:
                        emit_xq_back(st)
                emit_group_A(g)
                if debug and g == 1:
                    @sec_sq
                    def _(sy):
                        toks["dbg_g0b"] = SQ.emit(
                            "dma_start", out=dbg["d_hq"][:], in_=hq[:],
                            waits=[toks[("hqt", 0, 3)]])
                if g >= 1:
                    emit_Cmm(g - 1, 0)
                    emit_Cev(g - 1, 0)
                emit_Bamax(g)
                if g >= 1:
                    emit_Cmm(g - 1, 1)
                emit_Brest(g)
                if g >= 1:
                    emit_Cev(g - 1, 1, store=True)
            emit_Q_pool(NGRP - 1)
            emit_hmul_ve(NGRP - 1, 0)
            emit_hmul_ve(NGRP - 1, 1)
            for sl in range(4):
                emit_shift(NGRP - 1, sl)
            emit_Cmm(NGRP - 1, 0)
            emit_Cev(NGRP - 1, 0)
            emit_Cmm(NGRP - 1, 1)
            emit_Cev(NGRP - 1, 1, store=True)

            if debug:
                @sec_sq
                def _(sy):
                    SQ.emit("dma_start", out=dbg["d_xqt"][:], in_=xqT[:],
                            waits=[toks[("xq_done", NT - 1)],
                                   toks["xq_guard"]])

            @sec_gq
            def _(gp):
                GQ.wait(toks[("out_store", NGRP - 1)])
                for t in SQ.toks():
                    GQ.wait(t)
                for t in GQ.toks():
                    GQ.wait(t)

            flush_phase()

    nc.finalize()
    return nc, dbg


_PROG_CACHE = {}


def _get_program(debug=False):
    key = bool(debug)
    if key not in _PROG_CACHE:
        _PROG_CACHE[key] = build_program(debug=debug)
    return _PROG_CACHE[key]


def make_inputs(L0_attn_x, L0_attn_W_query, L0_attn_W_key, L0_attn_W_value,
                L0_ffn_w1, L0_ffn_conv_w, L0_ffn_conv_b, L0_ffn_w2):
    import ml_dtypes
    x = np.ascontiguousarray(np.asarray(L0_attn_x, dtype=np.float32))
    wq = np.ascontiguousarray(np.asarray(L0_attn_W_query, dtype=np.float32))
    wk = np.ascontiguousarray(np.asarray(L0_attn_W_key, dtype=np.float32))
    wv = np.ascontiguousarray(np.asarray(L0_attn_W_value, dtype=np.float32))
    w1t = np.ascontiguousarray(np.asarray(L0_ffn_w1, dtype=np.float32).T)
    w2t = np.ascontiguousarray(np.asarray(L0_ffn_w2, dtype=np.float32).T)
    cw = np.asarray(L0_ffn_conv_w, dtype=np.float32).reshape(HID, 3)
    cb = np.asarray(L0_ffn_conv_b, dtype=np.float32)
    convw = np.ascontiguousarray(
        cw.reshape(32, 128, 3).transpose(1, 0, 2).reshape(128, 96))
    convb = np.ascontiguousarray(cb.reshape(32, 128).T)
    idbf = np.eye(128, dtype=ml_dtypes.bfloat16)
    idf = np.eye(128, dtype=np.float32)
    mask = np.triu(np.ones((128, 128), dtype=np.float32)).astype(
        ml_dtypes.bfloat16)
    onesb = np.ones((128, 1), dtype=ml_dtypes.bfloat16)
    onesf = np.ones((128, 1), dtype=np.float32)
    onesrow = np.ones((1, 128), dtype=np.float32)

    shared = dict(wq=wq, wk=wk, wv=wv, w1t=w1t, w2t=w2t, convw=convw,
                  convb=convb, idbf=idbf, idf=idf, mask=mask, onesb=onesb,
                  onesf=onesf, onesrow=onesrow)
    in_maps = []
    for c in range(NCORES):
        m = dict(shared)
        m["x"] = np.ascontiguousarray(x[c])
        in_maps.append(m)
    return in_maps


def kernel(L0_attn_x, L0_attn_W_query, L0_attn_W_key, L0_attn_W_value,
           L0_ffn_w1, L0_ffn_conv_w, L0_ffn_conv_b, L0_ffn_w2):
    in_maps = make_inputs(L0_attn_x, L0_attn_W_query, L0_attn_W_key,
                          L0_attn_W_value, L0_ffn_w1, L0_ffn_conv_w,
                          L0_ffn_conv_b, L0_ffn_w2)
    nc, _ = _get_program(debug=False)
    res = run_bass_kernel_spmd(nc, in_maps, core_ids=list(range(NCORES)))
    out = np.stack([np.asarray(r["out"]) for r in res.results], axis=0)
    return out.astype(np.float32)


# version 16
# speedup vs baseline: 1.0427x; 1.0427x over previous
from contextlib import ExitStack

import numpy as np
import concourse.bass as bass
import concourse.mybir as mybir
from concourse.bass_utils import run_bass_kernel_spmd

F32 = mybir.dt.float32
BF16 = mybir.dt.bfloat16
F16 = mybir.dt.float16
AF = mybir.ActivationFunctionType
ALU = mybir.AluOpType
AX = mybir.AxisListType

S = 2048
D = 1024
H = 16
DH = 64
HID = 4096
NCORES = 8
NT = S // 128
QTW = 512
NQT = S // QTW
GRP = 256
NGRP = S // GRP
VW = DH + 1

DRY = [False]


class Eng:

    def __init__(self, raw, sem, chain):
        self.raw = raw
        self.sem = sem
        self.chain = chain
        self.n = 0
        self._waited = {}

    def state(self):
        return (self.n, dict(self._waited))

    def restore(self, st):
        self.n, self._waited = st[0], dict(st[1])

    def endcount(self):
        return self.n

    def tok(self):
        return (self.sem, self.n)

    def wait(self, tok):
        if tok is None:
            return
        if isinstance(tok, list):
            for t in tok:
                self.wait(t)
            return
        sem, val = tok
        if val <= 0:
            return
        key = id(sem)
        if self._waited.get(key, -1) >= val:
            return
        if not DRY[0]:
            self.raw.wait_ge(sem, val)
        self._waited[key] = val

    def emit(self, name, *a, waits=(), **kw):
        if self.chain and self.n > 0:
            self.wait((self.sem, self.n))
        for t in waits:
            self.wait(t)
        if not DRY[0]:
            inst = getattr(self.raw, name)(*a, **kw)
            inst.then_inc(self.sem, 1)
        self.n += 1
        return (self.sem, self.n)


class Queue:

    def __init__(self, raw, sems):
        self.raw = raw
        self.sems = list(sems)
        self.counts = [0] * len(self.sems)
        self.i = 0
        self._waited = {}

    def state(self):
        return (list(self.counts), self.i, dict(self._waited))

    def restore(self, st):
        self.counts, self.i, self._waited = list(st[0]), st[1], dict(st[2])

    def endcount(self):
        return tuple(self.counts)

    def wait(self, tok):
        if tok is None:
            return
        if isinstance(tok, list):
            for t in tok:
                self.wait(t)
            return
        sem, val = tok
        if val <= 0:
            return
        key = id(sem)
        if self._waited.get(key, -1) >= val:
            return
        if not DRY[0]:
            self.raw.wait_ge(sem, val)
        self._waited[key] = val

    def emit(self, name, *a, waits=(), **kw):
        j = self.i % len(self.sems)
        self.i += 1
        self.wait((self.sems[j], self.counts[j]))
        for t in waits:
            self.wait(t)
        if not DRY[0]:
            inst = getattr(self.raw, name)(*a, **kw)
            inst.then_inc(self.sems[j], 16)
        self.counts[j] += 16
        return (self.sems[j], self.counts[j])

    def toks(self):
        return [(s, c) for s, c in zip(self.sems, self.counts) if c > 0]


def build_program(debug=False):
    nc = bass.Bass("TRN2", target_bir_lowering=False, debug=False)

    x_h = nc.dram_tensor("x", [S, D], F32, kind="ExternalInput").ap()
    wq_h = nc.dram_tensor("wq", [D, D], F32, kind="ExternalInput").ap()
    wk_h = nc.dram_tensor("wk", [D, D], F32, kind="ExternalInput").ap()
    wv_h = nc.dram_tensor("wv", [D, D], F32, kind="ExternalInput").ap()
    w1t_h = nc.dram_tensor("w1t", [D, HID], F32, kind="ExternalInput").ap()
    w2t_h = nc.dram_tensor("w2t", [HID, D], F32, kind="ExternalInput").ap()
    convw_h = nc.dram_tensor("convw", [128, 96], F32, kind="ExternalInput").ap()
    convb_h = nc.dram_tensor("convb", [128, 32], F32, kind="ExternalInput").ap()
    idbf_h = nc.dram_tensor("idbf", [128, 128], BF16, kind="ExternalInput").ap()
    idf_h = nc.dram_tensor("idf", [128, 128], F32, kind="ExternalInput").ap()
    mask_h = nc.dram_tensor("mask", [128, 128], BF16, kind="ExternalInput").ap()
    onesb_h = nc.dram_tensor("onesb", [128, 1], BF16, kind="ExternalInput").ap()
    onesf_h = nc.dram_tensor("onesf", [128, 1], F32, kind="ExternalInput").ap()
    onesrow_h = nc.dram_tensor("onesrow", [1, 128], F32, kind="ExternalInput").ap()

    out_h = nc.dram_tensor("out", [S, D], F32, kind="ExternalOutput").ap()

    x1scr = nc.dram_tensor("x1scr", [S, D], F32, kind="Internal").ap()
    t1scr = nc.dram_tensor("t1scr", [D, HID], BF16, kind="Internal").ap()
    t2scr = nc.dram_tensor("t2scr", [HID, D], BF16, kind="Internal").ap()

    dbg = {}
    if debug:
        def dbg_t(name, shape, dt=F32):
            dbg[name] = nc.dram_tensor(name, shape, dt, kind="ExternalOutput").ap()
        dbg_t("d_qt", [128, 8 * S], BF16)
        dbg_t("d_kt", [128, 8 * S], BF16)
        dbg_t("d_vaug", [128, NT * H * VW], BF16)
        dbg_t("d_x1", [S, D], F32)
        dbg_t("d_xqt", [128, 8 * (S + 2)], BF16)
        dbg_t("d_t1t", [D, HID], BF16)
        dbg_t("d_t2t", [HID, D], BF16)
        dbg_t("d_hq", [128, 32 * GRP], BF16)
        dbg_t("d_hsil", [128, 32 * GRP], F32)
        dbg_t("d_nb", [1, 16], F32)

    es = ExitStack()
    with es:
        s_sq = [es.enter_context(nc.semaphore(f"s_sq{i}")) for i in range(4)]
        s_gq = [es.enter_context(nc.semaphore(f"s_gq{i}")) for i in range(4)]
        s_t = es.enter_context(nc.semaphore("s_t"))
        s_v = es.enter_context(nc.semaphore("s_v"))
        s_a = es.enter_context(nc.semaphore("s_a"))
        s_g = es.enter_context(nc.semaphore("s_g"))
        idbf = es.enter_context(nc.sbuf_tensor("idbf_s", [128, 128], BF16))
        idf = es.enter_context(nc.sbuf_tensor("idf_s", [128, 128], F32))
        maskt = es.enter_context(nc.sbuf_tensor("mask_s", [128, 128], BF16))
        onesb = es.enter_context(nc.sbuf_tensor("onesb_s", [128, 1], BF16))
        onesf = es.enter_context(nc.sbuf_tensor("onesf_s", [128, 1], F32))
        onesrow = es.enter_context(nc.sbuf_tensor("onesrow_s", [1, 128], F32))
        convw = es.enter_context(nc.sbuf_tensor("convw_s", [128, 96], F32))
        convb = es.enter_context(nc.sbuf_tensor("convb_s", [128, 32], F32))
        smalls = es.enter_context(nc.sbuf_tensor("smalls", [128, 64], F32))
        nrm = es.enter_context(nc.sbuf_tensor("nrm", [1, 64], F32))
        block = es.enter_context(nc.Block())

        SQ = Queue(nc.sync, s_sq)
        GQ = Queue(nc.gpsimd, s_gq)
        TE = Eng(nc.tensor, s_t, chain=False)
        VE = Eng(nc.vector, s_v, chain=True)
        SE = Eng(nc.scalar, s_a, chain=True)
        GE = Eng(nc.gpsimd, s_g, chain=True)
        ENGS = [SQ, GQ, TE, VE, SE, GE]

        toks = {}
        phase_fns = []

        def sec(key):
            def deco(fn):
                phase_fns.append((key, fn))
                return fn
            return deco

        sec_sq = sec("sync")
        sec_gq = sec("gpsimd")
        sec_t = sec("tensor")
        sec_v = sec("vector")
        sec_s = sec("scalar")
        sec_g = sec("gpsimd")

        def flush_phase():
            snaps = [e.state() for e in ENGS]
            DRY[0] = True
            for _, fn in phase_fns:
                fn(None)
            DRY[0] = False
            ends = [e.endcount() for e in ENGS]
            for e, st in zip(ENGS, snaps):
                e.restore(st)
            for key, raw in (("sync", None), ("gpsimd", None),
                             ("tensor", None), ("vector", None),
                             ("scalar", None)):
                fns = [f for k, f in phase_fns if k == key]
                if not fns:
                    continue

                def runner(eng, fns=fns):
                    for f in fns:
                        f(eng)
                getattr(block, key)(runner)
            for e, ec in zip(ENGS, ends):
                assert e.endcount() == ec, "dry/real emission diverged"
            phase_fns.clear()

        @sec_sq
        def _(sy):
            SQ.emit("dma_start", out=idbf[:], in_=idbf_h[:])
            SQ.emit("dma_start", out=idf[:], in_=idf_h[:])
            SQ.emit("dma_start", out=maskt[:], in_=mask_h[:])
            SQ.emit("dma_start", out=onesb[:], in_=onesb_h[:])
            SQ.emit("dma_start", out=onesf[:], in_=onesf_h[:])
            SQ.emit("dma_start", out=onesrow[:], in_=onesrow_h[:])
            SQ.emit("dma_start", out=convw[:], in_=convw_h[:])
            SQ.emit("dma_start", out=convb[:], in_=convb_h[:])
            toks["consts"] = SQ.toks()

        @sec_v
        def _(ve):
            toks["c1536"] = VE.emit("memset", smalls[:, 7:8], 1536.0)
            toks["cn1536"] = VE.emit("memset", smalls[:, 28:29], -1536.0)

        flush_phase()

        esA = ExitStack()
        qT = esA.enter_context(nc.sbuf_tensor("qT", [128, 8 * S], BF16))
        kT = esA.enter_context(nc.sbuf_tensor("kT", [128, 8 * S], BF16))
        vaug = esA.enter_context(nc.sbuf_tensor("vaug", [128, NT * H * VW], BF16))

        esA1 = ExitStack()
        with esA1:
            Wqb = esA1.enter_context(nc.sbuf_tensor("Wqb", [128, 8 * D], BF16))
            Wkb = esA1.enter_context(nc.sbuf_tensor("Wkb", [128, 8 * D], BF16))
            Wvb = esA1.enter_context(nc.sbuf_tensor("Wvb", [128, 8 * D], BF16))
            wstage = esA1.enter_context(nc.sbuf_tensor("wstage", [128, 2 * 2048], F32))
            xstage = esA1.enter_context(nc.sbuf_tensor("xstage", [128, 2 * D], F32))
            lnstage = esA1.enter_context(nc.sbuf_tensor("lnstage", [128, 2 * D], BF16))
            ln1T = esA1.enter_context(nc.sbuf_tensor("ln1T", [128, 8 * 512], BF16))
            bn6 = esA1.enter_context(nc.sbuf_tensor("bn6", [128, 12], F32))
            psQK0 = esA1.enter_context(nc.psum_tensor("psQK0", [128, 512], F32))
            psQK1 = esA1.enter_context(nc.psum_tensor("psQK1", [128, 512], F32))
            psQK2 = esA1.enter_context(nc.psum_tensor("psQK2", [128, 512], F32))
            psTr = esA1.enter_context(nc.psum_tensor("psTr", [128, 1024], BF16))
            psQK = [psQK0, psQK1, psQK2]

            def emit_wcast(qq):
                wi, quarter = divmod(qq, 4)
                w_h, Wb = [(wq_h, Wqb), (wk_h, Wkb), (wv_h, Wvb)][wi]
                wsl = wstage[:, (qq % 2) * 2048:(qq % 2 + 1) * 2048]
                pk = (("wcast", (qq - 2) // 4, (qq - 2) % 4)
                      if qq >= 2 else None)

                @sec_sq
                def _(sy, wi=wi, w_h=w_h, quarter=quarter, wsl=wsl, pk=pk):
                    toks[("wdma", wi, quarter)] = SQ.emit(
                        "dma_start", out=wsl,
                        in_=w_h[256 * quarter:256 * (quarter + 1), :]
                        .rearrange("(c p) d -> p c d", p=128),
                        waits=([toks[pk]] if pk else ()))

                @sec_s
                def _(se, wi=wi, Wb=Wb, quarter=quarter, wsl=wsl):
                    toks[("wcast", wi, quarter)] = SE.emit(
                        "activation",
                        out=Wb[:, (2 * quarter) * D:(2 * quarter + 2) * D],
                        in_=wsl, func=AF.Identity,
                        waits=[toks[("wdma", wi, quarter)]])
            @sec_g
            def _(g):
                ap = vaug[:].rearrange("p (s h w) -> p (s h) w", h=H, w=VW)
                toks["vones"] = GE.emit("memset", ap[:, :, DH:DH + 1], 1.0)

            for sb in range(4):
                for st4 in range(4):
                    st = 4 * sb + st4
                    par = st % 2
                    if st < 4:
                        for qq in range(3 * st, 3 * st + 3):
                            emit_wcast(qq)

                    @sec_gq
                    def _(g, st=st, par=par):
                        w = []
                        if st >= 2:
                            w.append(toks[("ln", st - 2)])
                        toks[("xdma", st)] = GQ.emit(
                            "dma_start",
                            out=xstage[:, par * D:(par + 1) * D],
                            in_=x_h[128 * st:128 * (st + 1), :], waits=w)

                    @sec_v
                    def _(ve, st=st, par=par):
                        xs = xstage[:, par * D:(par + 1) * D]
                        VE.wait(toks[("xdma", st)])
                        VE.emit("bn_stats", bn6[:, 0:6], xs[:, 0:512])
                        VE.emit("bn_stats", bn6[:, 6:12], xs[:, 512:1024])
                        VE.emit("bn_aggr", smalls[:, 2:4], bn6[:, 0:12])
                        toks[("veps", st)] = VE.emit(
                            "tensor_scalar_add", smalls[:, 4:5],
                            smalls[:, 3:4], 1e-5)

                    @sec_s
                    def _(se, st=st):
                        SE.wait(toks[("veps", st)])
                        SE.emit("activation", out=smalls[:, 5:6],
                                in_=smalls[:, 4:5], func=AF.Ln)
                        toks[("rstd", st)] = SE.emit(
                            "activation", out=smalls[:, 0:1],
                            in_=smalls[:, 5:6], func=AF.Exp, scale=-0.5)

                    @sec_v
                    def _(ve, st=st, par=par):
                        VE.wait(toks[("rstd", st)])
                        VE.emit("scalar_tensor_tensor", smalls[:, 1:2],
                                smalls[:, 2:3], -1.0, smalls[:, 0:1],
                                ALU.mult, ALU.mult)
                        xs = xstage[:, par * D:(par + 1) * D]
                        ls = lnstage[:, par * D:(par + 1) * D]
                        toks[("ln", st)] = VE.emit(
                            "tensor_scalar", ls, xs, smalls[:, 0:1],
                            smalls[:, 1:2], ALU.mult, ALU.add)

                    @sec_t
                    def _(te, st=st, par=par):
                        ls = lnstage[:, par * D:(par + 1) * D]
                        TE.wait(toks[("ln", st)])
                        TE.wait(toks["consts"])
                        if st >= 1:
                            TE.wait(toks[("lnT", st - 1)])
                        for c in range(8):
                            TE.emit("transpose",
                                    psTr[:, 128 * c:128 * (c + 1)],
                                    ls[:, 128 * c:128 * (c + 1)], idbf[:])
                        toks[("lntr", st)] = TE.tok()

                    @sec_v
                    def _(ve, st=st, st4=st4):
                        VE.wait(toks[("lntr", st)])
                        src = psTr[:].rearrange("p (c w) -> p c w", w=128)
                        dst = ln1T[:].rearrange("p (c w) -> p c w", w=512)
                        toks[("lnT", st)] = VE.emit(
                            "tensor_copy",
                            dst[:, :, 128 * st4:128 * (st4 + 1)], src[:])

                bank_last = {}

                def qkv_mm(Wb, outbuf, sb, proj):
                    for mt in range(8):
                        bidx = (proj * 8 + mt) % 3
                        b = psQK[bidx]

                        @sec_t
                        def _(te, Wb=Wb, mt=mt, b=b, sb=sb, proj=proj,
                              lk=bank_last.get(bidx)):
                            TE.wait(toks[("lnT", 4 * sb + 3)])
                            TE.wait(toks[("wcast", proj, 3)])
                            if lk is not None:
                                TE.wait(toks[lk])
                            for kc in range(8):
                                TE.emit(
                                    "matmul", b[:],
                                    Wb[:, kc * D + 128 * mt:kc * D + 128 * (mt + 1)],
                                    ln1T[:, 512 * kc:512 * (kc + 1)],
                                    start=(kc == 0), stop=(kc == 7))
                            toks[("qkv_mm", sb, proj, mt)] = TE.tok()

                        @sec_v
                        def _(ve, outbuf=outbuf, mt=mt, b=b, sb=sb,
                              proj=proj):
                            VE.wait(toks[("qkv_mm", sb, proj, mt)])
                            toks[("qkv_ev", sb, proj, mt)] = VE.emit(
                                "tensor_copy",
                                outbuf[:, mt * S + 512 * sb:mt * S + 512 * (sb + 1)],
                                b[:])
                        bank_last[bidx] = ("qkv_ev", sb, proj, mt)

                qkv_mm(Wqb, qT, sb, 0)
                qkv_mm(Wkb, kT, sb, 1)
                for ms in range(4):
                    for n in range(2):
                        j = ms * 2 + n
                        bidx = (16 + j) % 3
                        b = psQK[bidx]

                        @sec_t
                        def _(te, ms=ms, n=n, b=b, sb=sb, j=j,
                              lk=bank_last.get(bidx)):
                            TE.wait(toks[("lnT", 4 * sb + 3)])
                            TE.wait(toks[("wcast", 2, 3)])
                            if lk is not None:
                                TE.wait(toks[lk])
                            for kc in range(8):
                                TE.emit(
                                    "matmul", b[:],
                                    ln1T[:, 512 * kc + 128 * ms:512 * kc + 128 * (ms + 1)],
                                    Wvb[:, kc * D + 512 * n:kc * D + 512 * (n + 1)],
                                    start=(kc == 0), stop=(kc == 7))
                            toks[("qkv_mm", sb, 2, j)] = TE.tok()

                        @sec_v
                        def _(ve, ms=ms, n=n, b=b, sb=sb, j=j):
                            st = 4 * sb + ms
                            VE.wait(toks[("qkv_mm", sb, 2, j)])
                            VE.wait(toks["vones"])
                            dst = vaug[:, st * H * VW + 8 * n * VW:
                                       st * H * VW + (8 * n + 8) * VW]
                            dst = dst.rearrange("p (h w) -> p h w", w=VW)
                            src = b[:].rearrange("p (h w) -> p h w", w=DH)
                            toks[("qkv_ev", sb, 2, j)] = VE.emit(
                                "tensor_copy", dst[:, :, 0:DH], src[:])
                        bank_last[bidx] = ("qkv_ev", sb, 2, j)

            flush_phase()
            toks["qkv_done_v"] = VE.tok()

        esA2 = ExitStack()
        with esA2:
            PT = esA2.enter_context(nc.sbuf_tensor("PT", [128, 4 * 16 * 512], BF16))
            PTW = 16 * 512
            x1st = esA2.enter_context(nc.sbuf_tensor("x1st", [128, 4 * D], F32))
            sqb = esA2.enter_context(nc.sbuf_tensor("sqb", [128, 2048], BF16))
            wbuf = esA2.enter_context(nc.sbuf_tensor("wbuf", [128, 2 * 2048], F32))
            t1s = esA2.enter_context(nc.sbuf_tensor("t1s", [128, 2 * 2048], BF16))
            recb = esA2.enter_context(nc.sbuf_tensor("recb", [128, 4], F32))
            psS4 = esA2.enter_context(nc.psum_tensor("psS4", [128, 2048], F32))
            psPV0 = esA2.enter_context(nc.psum_tensor("psPV0", [128, 512], F32))
            psPV1 = esA2.enter_context(nc.psum_tensor("psPV1", [128, 512], F32))
            psN = esA2.enter_context(nc.psum_tensor("psN", [128, 512], F32))
            psS = [psS4[:, 0:1024], psS4[:, 1024:2048]]
            psPV = [psPV0, psPV1]

            for wix, buf in ((0, qT), (1, kT)):
                for n in range(4):
                    def sq_half(wix, n, hf, buf):
                        @sec_v
                        def _(ve, buf=buf, wix=wix, n=n, hf=hf):
                            VE.wait(toks["qkv_done_v"])
                            if hf == 1:
                                VE.wait(toks[("nrm_mm_h", wix, n)])
                            if (wix, n) != (0, 0):
                                pw = (wix, n - 1) if n else (wix - 1, 3)
                                VE.wait(toks[("nrm_mm",) + pw])
                            src = buf[:].rearrange(
                                "p (c s) -> p c s",
                                s=S)[:, 4 * hf:4 * hf + 4,
                                     512 * n:512 * (n + 1)]
                            dst = sqb[:].rearrange("p (c s) -> p c s",
                                                   s=512)
                            toks[("sq", wix, n, hf)] = VE.emit(
                                "tensor_tensor", dst, src, src, ALU.mult)

                    sq_half(wix, n, 0, buf)

                    @sec_t
                    def _(te, wix=wix, n=n):
                        TE.wait(toks["consts"])
                        if (wix, n) != (0, 0):
                            pw = (wix, n - 1) if n else (wix - 1, 3)
                            TE.wait(toks[("nrm_rd",) + pw])
                        for c in range(4):
                            TE.wait(toks[("sq", wix, n, 0)])
                            TE.emit("matmul", psN[0:1, 0:512],
                                    onesb[:, 0:1],
                                    sqb[:, 512 * c:512 * (c + 1)],
                                    start=(c == 0), stop=False)
                        toks[("nrm_mm_h", wix, n)] = TE.tok()

                    sq_half(wix, n, 1, buf)

                    @sec_t
                    def _(te, wix=wix, n=n):
                        for c in range(4):
                            TE.wait(toks[("sq", wix, n, 1)])
                            TE.emit("matmul", psN[0:1, 0:512],
                                    onesb[:, 0:1],
                                    sqb[:, 512 * c:512 * (c + 1)],
                                    start=False, stop=(c == 3))
                        toks[("nrm_mm", wix, n)] = TE.tok()

                    @sec_v
                    def _(ve, wix=wix, n=n):
                        VE.wait(toks[("nrm_mm", wix, n)])
                        col = 8 * wix + n
                        toks[("nrm_rd", wix, n)] = VE.emit(
                            "tensor_reduce", nrm[0:1, col:col + 1],
                            psN[0:1, 0:512], AX.X, ALU.max)

            @sec_v
            def _(ve):
                VE.emit("tensor_reduce", nrm[0:1, 16:17], nrm[0:1, 0:4],
                        AX.X, ALU.max)
                VE.emit("tensor_reduce", nrm[0:1, 17:18], nrm[0:1, 8:12],
                        AX.X, ALU.max)
                toks["c2"] = VE.emit(
                    "tensor_tensor", nrm[0:1, 18:19], nrm[0:1, 16:17],
                    nrm[0:1, 17:18], ALU.mult)

            @sec_s
            def _(se):
                SE.wait(toks["c2"])
                SE.emit("activation", out=nrm[0:1, 19:20],
                        in_=nrm[0:1, 18:19], func=AF.Ln)
                toks["cval"] = SE.emit(
                    "activation", out=nrm[0:1, 20:21],
                    in_=nrm[0:1, 19:20], func=AF.Exp, scale=0.5)

            @sec_v
            def _(ve):
                VE.wait(toks["cval"])
                VE.emit("tensor_scalar", nrm[0:1, 21:22], nrm[0:1, 20:21],
                        0.125, 80.0, ALU.mult, ALU.min)
                toks["negb"] = VE.emit(
                    "tensor_scalar_mul", nrm[0:1, 22:23], nrm[0:1, 21:22],
                    -1.0)

            @sec_t
            def _(te):
                TE.wait(toks["negb"])
                TE.emit("matmul", psN[:, 0:1], onesrow[0:1, :],
                        nrm[0:1, 22:23], start=True, stop=True)
                toks["negb_mm"] = TE.tok()

            @sec_v
            def _(ve):
                VE.wait(toks["negb_mm"])
                toks["expbias"] = VE.emit(
                    "tensor_copy", smalls[:, 6:7], psN[:, 0:1])

            if debug:
                @sec_v
                def _(ve):
                    toks["dbg_nb"] = VE.emit("tensor_copy", nrm[0:1, 32:39],
                                             nrm[0:1, 16:23])

                @sec_sq
                def _(sy):
                    SQ.emit("dma_start", out=dbg["d_nb"][0:1, 0:7],
                            in_=nrm[0:1, 32:39], waits=[toks["dbg_nb"]])

            def mk_w1_abs(i):
                c, hf = divmod(i, 2)

                def f():
                    @sec_sq
                    def _(sy):
                        w = [toks["qkv_done_v"]]
                        if i >= 2:
                            w.append(toks[("w1_abs", i - 2)])
                        toks[("w1_dma", i)] = SQ.emit(
                            "dma_start",
                            out=wbuf[:, (i % 2) * 2048:(i % 2 + 1) * 2048],
                            in_=w1t_h[128 * c:128 * (c + 1),
                                      2048 * hf:2048 * (hf + 1)], waits=w)

                    @sec_s
                    def _(se):
                        SE.wait(toks[("w1_dma", i)])
                        wsl = wbuf[:, (i % 2) * 2048:(i % 2 + 1) * 2048]
                        toks[("w1_abs", i)] = SE.emit(
                            "activation", out=wsl, in_=wsl,
                            func=AF.Abs, accum_out=smalls[:, 32 + i:33 + i])
                return f

            def mk_w2_abs(i):
                def f():
                    @sec_sq
                    def _(sy):
                        w = [toks["qkv_done_v"]]
                        if i >= 2:
                            w.append(toks[("w2_abs", i - 2)])
                        else:
                            w.append(toks[("w1_abs", 14 + i)])
                        toks[("w2_dma", i)] = SQ.emit(
                            "dma_start",
                            out=wbuf[:, (i % 2) * 2048:(i % 2 + 1) * 2048]
                            .rearrange("p (c d) -> p c d", d=D),
                            in_=w2t_h[256 * i:256 * (i + 1), :]
                            .rearrange("(c p) d -> p c d", p=128), waits=w)

                    @sec_s
                    def _(se):
                        SE.wait(toks[("w2_dma", i)])
                        wsl = wbuf[:, (i % 2) * 2048:(i % 2 + 1) * 2048]
                        toks[("w2_abs", i)] = SE.emit(
                            "activation", out=wsl, in_=wsl,
                            func=AF.Abs, accum_out=smalls[:, 48 + i:49 + i])
                return f

            def mk_finalize(which):
                base = 32 if which == 1 else 48
                scol = 8 if which == 1 else 10
                icol = 9 if which == 1 else 11
                nelem = float(D * HID)

                def f():
                    @sec_v
                    def _(ve):
                        toks[("wfin_red", which)] = VE.emit(
                            "tensor_reduce", smalls[:, 30:31],
                            smalls[:, base:base + 16], AX.X, ALU.add,
                            waits=[toks[("w1_abs" if which == 1 else
                                         "w2_abs", 15)]])

                    @sec_t
                    def _(te):
                        TE.wait(toks[("wfin_red", which)])
                        if which == 2:
                            TE.wait(toks[("wsc", 1)])
                        TE.emit("matmul", psN[0:1, 8:9], onesf[:, 0:1],
                                smalls[:, 30:31], start=True, stop=True)
                        toks[("wfin_mm", which)] = TE.tok()

                    @sec_v
                    def _(ve):
                        VE.wait(toks[("wfin_mm", which)])
                        VE.emit("tensor_scalar", nrm[0:1, 40:41],
                                psN[0:1, 8:9], 1.0 / nelem, 1e-5,
                                ALU.mult, ALU.max)
                        toks[("wfin_sc", which)] = VE.emit(
                            "reciprocal", nrm[0:1, 41:42], nrm[0:1, 40:41])

                    @sec_t
                    def _(te):
                        TE.wait(toks[("wfin_sc", which)])
                        TE.emit("matmul", psN[:, 9:10], onesrow[0:1, :],
                                nrm[0:1, 41:42], start=True, stop=True)
                        TE.emit("matmul", psN[:, 10:11], onesrow[0:1, :],
                                nrm[0:1, 40:41], start=True, stop=True)
                        toks[("wfin_bc", which)] = TE.tok()

                    @sec_v
                    def _(ve):
                        VE.wait(toks[("wfin_bc", which)])
                        VE.emit("tensor_copy", smalls[:, scol:scol + 1],
                                psN[:, 9:10])
                        toks[("wsc", which)] = VE.emit(
                            "tensor_copy", smalls[:, icol:icol + 1],
                            psN[:, 10:11])
                return f

            def mk_w1_quant(i):
                c, hf = divmod(i, 2)

                def f():
                    @sec_sq
                    def _(sy):
                        w = []
                        if i >= 2:
                            w.append(toks[("w1q_t", i - 2)])
                        else:
                            w.append(toks[("w2_abs", 14 + i)])
                        toks[("w1q_dma", i)] = SQ.emit(
                            "dma_start",
                            out=wbuf[:, (i % 2) * 2048:(i % 2 + 1) * 2048],
                            in_=w1t_h[128 * c:128 * (c + 1),
                                      2048 * hf:2048 * (hf + 1)], waits=w)

                    @sec_v
                    def _(ve):
                        VE.wait(toks[("w1q_dma", i)])
                        VE.wait(toks[("wsc", 1)])
                        if i >= 2:
                            VE.wait(toks[("w1q_out", i - 2)])
                        wsl = wbuf[:, (i % 2) * 2048:(i % 2 + 1) * 2048]
                        tsl = t1s[:, (i % 2) * 2048:(i % 2 + 1) * 2048]
                        VE.emit("tensor_scalar", tsl, wsl,
                                smalls[:, 8:9], 0.5, ALU.mult, ALU.is_ge)
                        VE.emit("tensor_scalar", wsl, wsl,
                                smalls[:, 8:9], -0.5, ALU.mult, ALU.is_le)
                        toks[("w1q_t", i)] = VE.emit(
                            "tensor_tensor", tsl, tsl, wsl, ALU.subtract)

                    @sec_sq
                    def _(sy):
                        toks[("w1q_out", i)] = SQ.emit(
                            "dma_start",
                            out=t1scr[128 * c:128 * (c + 1),
                                      2048 * hf:2048 * (hf + 1)],
                            in_=t1s[:, (i % 2) * 2048:(i % 2 + 1) * 2048],
                            waits=[toks[("w1q_t", i)]])
                return f

            def mk_w2_quant(i):
                def f():
                    @sec_sq
                    def _(sy):
                        w = []
                        if i >= 2:
                            w.append(toks[("w2q_t", i - 2)])
                        else:
                            w.append(toks[("w1q_t", 14 + i)])
                        toks[("w2q_dma", i)] = SQ.emit(
                            "dma_start",
                            out=wbuf[:, (i % 2) * 2048:(i % 2 + 1) * 2048]
                            .rearrange("p (c d) -> p c d", d=D),
                            in_=w2t_h[256 * i:256 * (i + 1), :]
                            .rearrange("(c p) d -> p c d", p=128), waits=w)

                    @sec_v
                    def _(ve):
                        VE.wait(toks[("w2q_dma", i)])
                        VE.wait(toks[("wsc", 2)])
                        if i >= 2:
                            VE.wait(toks[("w2q_out", i - 2)])
                        else:
                            VE.wait(toks[("w1q_out", 14 + i)])
                        wsl = wbuf[:, (i % 2) * 2048:(i % 2 + 1) * 2048]
                        tsl = t1s[:, (i % 2) * 2048:(i % 2 + 1) * 2048]
                        VE.emit("tensor_scalar", tsl, wsl,
                                smalls[:, 10:11], 0.5, ALU.mult, ALU.is_ge)
                        VE.emit("tensor_scalar", wsl, wsl,
                                smalls[:, 10:11], -0.5, ALU.mult, ALU.is_le)
                        toks[("w2q_t", i)] = VE.emit(
                            "tensor_tensor", tsl, tsl, wsl, ALU.subtract)

                    @sec_sq
                    def _(sy):
                        toks[("w2q_out", i)] = SQ.emit(
                            "dma_start",
                            out=t2scr[256 * i:256 * (i + 1), :]
                            .rearrange("(c p) d -> p c d", p=128),
                            in_=t1s[:, (i % 2) * 2048:(i % 2 + 1) * 2048]
                            .rearrange("p (c d) -> p c d", d=D),
                            waits=[toks[("w2q_t", i)]])
                return f

            wsteps = ([mk_w1_abs(i) for i in range(16)] + [mk_finalize(1)]
                      + [mk_w2_abs(i) for i in range(16)] + [mk_finalize(2)]
                      + [mk_w1_quant(i) for i in range(16)]
                      + [mk_w2_quant(i) for i in range(16)])
            wstep_i = [0]

            def emit_wsteps(k):
                while k > 0 and wstep_i[0] < len(wsteps):
                    wsteps[wstep_i[0]]()
                    wstep_i[0] += 1
                    k -= 1

            pend = [None]

            def flush_pv():
                if pend[0] is None:
                    return
                t, hp, PTp_A, PTp_B = pend[0]
                pend[0] = None
                hA, hB = 2 * hp, 2 * hp + 1
                for hi, (head, P) in enumerate(((hA, PTp_A), (hB, PTp_B))):
                    pv = psPV[hi]

                    @sec_t
                    def _(te, t=t, hp=hp, hi=hi, head=head, P=P, pv=pv):
                        if hp >= 1:
                            TE.wait(toks[("epi", t, hp - 1, hi)])
                        elif t >= 1:
                            TE.wait(toks[("epi", t - 1, 7, hi)])
                        TE.wait(toks[("exp", t, hp, 4 * t + 3)])
                        TE.wait(toks[("mask", t, hp)])
                        first = True
                        for m in range(4):
                            for kc in range(4 * t + m + 1):
                                last = (m == 3) and (kc == 4 * t + 3)
                                TE.emit(
                                    "matmul", pv[:, VW * m:VW * m + VW],
                                    P[:, 512 * kc + 128 * m:512 * kc + 128 * (m + 1)],
                                    vaug[:, kc * H * VW + head * VW:
                                         kc * H * VW + (head + 1) * VW],
                                    start=first, stop=last,
                                    skip_group_check=True)
                                first = False
                        toks[("pv", t, hp, hi)] = TE.tok()

                    @sec_v
                    def _(ve, t=t, hp=hp, hi=hi, head=head, pv=pv):
                        VE.wait(toks[("pv", t, hp, hi)])
                        VE.wait(toks[("x1_load", t)])
                        rs = pv[:, 0:4 * VW].rearrange(
                            "p (m w) -> p m w", w=VW)[:, :, DH:DH + 1]
                        VE.emit("reciprocal", recb[:, 0:4], rs)
                        for m in range(4):
                            xsl = x1st[:, m * D + head * DH:
                                       m * D + (head + 1) * DH]
                            VE.emit("scalar_tensor_tensor", xsl,
                                    pv[:, VW * m:VW * m + DH],
                                    recb[:, m:m + 1], xsl,
                                    ALU.mult, ALU.add)
                        toks[("epi", t, hp, hi)] = VE.tok()

                if hp == 7:
                    @sec_gq
                    def _(g, t=t):
                        toks[("x1_store", t)] = GQ.emit(
                            "dma_start",
                            out=x1scr[QTW * t:QTW * (t + 1), :]
                            .rearrange("(c p) d -> p c d", p=128),
                            in_=x1st[:].rearrange("p (c d) -> p c d", d=D),
                            waits=[toks[("epi", t, 7, 0)],
                                   toks[("epi", t, 7, 1)]])

            for t in range(NQT):
                KC = 4 * t + 4
                q0 = QTW * t
                xpar = t % 2
                flush_pv()

                @sec_gq
                def _(g, t=t):
                    w = [toks["qkv_done_v"]]
                    if t >= 1:
                        w.append(toks[("x1_store", t - 1)])
                    toks[("x1_load", t)] = GQ.emit(
                        "dma_start",
                        out=x1st[:].rearrange("p (c d) -> p c d", d=D),
                        in_=x_h[QTW * t:QTW * (t + 1), :]
                        .rearrange("(c p) d -> p c d", p=128), waits=w)

                for hp in range(8):
                    hA, hB = 2 * hp, 2 * hp + 1

                    def emit_sc(t, hp, kc, KC, q0):
                        hpar = kc % 2

                        @sec_t
                        def _(te, t=t, hp=hp, kc=kc, hpar=hpar, KC=KC,
                              q0=q0):
                            if kc >= 2:
                                TE.wait(toks[("exp", t, hp, kc - 2)])
                            elif hp >= 1:
                                TE.wait(toks[("exp", t, hp - 1, KC - 2 + kc)])
                            elif t >= 1:
                                TE.wait(toks[("exp", t - 1, 7, 4 * t - 2 + kc)])
                            TE.wait(toks["qkv_done_v"])
                            bP = psS[hpar]
                            TE.emit("matmul", bP[:, 0:QTW],
                                    kT[0:64, hp * S + 128 * kc:hp * S + 128 * (kc + 1)],
                                    qT[0:64, hp * S + q0:hp * S + q0 + QTW],
                                    start=True, stop=True,
                                    tile_position=(0, 0),
                                    skip_group_check=True)
                            TE.emit("matmul", bP[:, QTW:2 * QTW],
                                    kT[64:128, hp * S + 128 * kc:hp * S + 128 * (kc + 1)],
                                    qT[64:128, hp * S + q0:hp * S + q0 + QTW],
                                    start=True, stop=True,
                                    tile_position=(64, 0),
                                    skip_group_check=True)
                            toks[("sc", t, hp, kc)] = TE.tok()

                    def exp_pvwaits(t, hp, kc):
                        SE.wait(toks["expbias"])
                        if kc == 0:
                            if hp >= 2:
                                SE.wait(toks[("pv", t, hp - 2, 0)])
                                SE.wait(toks[("pv", t, hp - 2, 1)])
                            elif t >= 1:
                                SE.wait(toks[("pv", t - 1, 6 + hp, 0)])
                                SE.wait(toks[("pv", t - 1, 6 + hp, 1)])

                    def emit_exp1(t, hp, kc):
                        hpar = kc % 2

                        @sec_s
                        def _(se, t=t, hp=hp, kc=kc, hpar=hpar):
                            SE.wait(toks[("sc", t, hp, kc)])
                            exp_pvwaits(t, hp, kc)
                            c0 = max(0, (kc - 4 * t) * 128)
                            bP = psS[hpar]
                            src_ap = bP[:].rearrange(
                                "p (h w) -> p h w", w=QTW)[:, :, c0:QTW]
                            PTp = PT[:, (hp % 2) * 2 * PTW:
                                      (hp % 2 + 1) * 2 * PTW]
                            dst_ap = PTp.rearrange(
                                "p (h w) -> p h w", w=16 * 512)[
                                :, :, 512 * kc + c0:512 * (kc + 1)]
                            toks[("exp", t, hp, kc)] = SE.emit(
                                "activation", out=dst_ap, in_=src_ap,
                                func=AF.Exp, scale=0.125,
                                bias=smalls[:, 6:7])

                    def emit_exp2(t, hp, kc):
                        @sec_s
                        def _(se, t=t, hp=hp, kc=kc):
                            SE.wait(toks[("sc", t, hp, kc)])
                            SE.wait(toks[("sc", t, hp, kc + 1)])
                            exp_pvwaits(t, hp, kc)
                            src_ap = psS4[:].rearrange(
                                "p (x h w) -> p h x w", x=2, h=2, w=QTW)
                            PTp = PT[:, (hp % 2) * 2 * PTW:
                                      (hp % 2 + 1) * 2 * PTW]
                            dst_ap = PTp.rearrange(
                                "p (h w) -> p h w", w=16 * 512)[
                                :, :, 512 * kc:512 * (kc + 2)]
                            dst_ap = dst_ap.rearrange(
                                "p h (x w) -> p h x w", w=QTW)
                            tok = SE.emit(
                                "activation", out=dst_ap, in_=src_ap,
                                func=AF.Exp, scale=0.125,
                                bias=smalls[:, 6:7])
                            toks[("exp", t, hp, kc)] = tok
                            toks[("exp", t, hp, kc + 1)] = tok

                    for kc in range(KC):
                        emit_sc(t, hp, kc, KC, q0)
                        if kc == 1:
                            flush_pv()
                        emit_exp1(t, hp, kc)

                    PTp_A = PT[:, (hp % 2) * 2 * PTW:
                               (hp % 2) * 2 * PTW + PTW]
                    PTp_B = PT[:, (hp % 2) * 2 * PTW + PTW:
                               (hp % 2 + 1) * 2 * PTW]

                    @sec_v
                    def _(ve, t=t, hp=hp, PTp_A=PTp_A, PTp_B=PTp_B):
                        for j in range(4):
                            kc = 4 * t + j
                            VE.wait(toks[("exp", t, hp, kc)])
                            for P in (PTp_A, PTp_B):
                                sl = P[:, 512 * kc + 128 * j:512 * kc + 128 * (j + 1)]
                                VE.emit("tensor_tensor", sl, sl, maskt[:],
                                        ALU.mult)
                        toks[("mask", t, hp)] = VE.tok()

                    pend[0] = (t, hp, PTp_A, PTp_B)
                    emit_wsteps(2)

            flush_pv()
            emit_wsteps(len(wsteps))

            if debug:
                @sec_sq
                def _(sy):
                    SQ.emit("dma_start", out=dbg["d_qt"][:], in_=qT[:],
                            waits=[toks["qkv_done_v"]])
                    SQ.emit("dma_start", out=dbg["d_kt"][:], in_=kT[:])
                    SQ.emit("dma_start", out=dbg["d_vaug"][:], in_=vaug[:])
                    SQ.emit("dma_start", out=dbg["d_x1"][:], in_=x1scr[:],
                            waits=[toks[("x1_store", t)] for t in range(4)])
                    SQ.emit("dma_start", out=dbg["d_t1t"][:], in_=t1scr[:],
                            waits=[toks[("w1q_out", i)] for i in range(16)])
                    SQ.emit("dma_start", out=dbg["d_t2t"][:], in_=t2scr[:],
                            waits=[toks[("w2q_out", i)] for i in range(16)])

            flush_phase()
            toks["A_t"] = TE.tok()
            toks["A_v"] = VE.tok()
            toks["A_a"] = SE.tok()
            toks["A_g"] = GE.tok()
            toks["A_sq"] = SQ.toks()
            toks["A_gq"] = GQ.toks()

        esA.close()
        barrier = ([toks["A_t"], toks["A_v"], toks["A_a"], toks["A_g"]]
                   + toks["A_sq"] + toks["A_gq"])

        esB = ExitStack()
        with esB:
            t2T = esB.enter_context(nc.sbuf_tensor("t2T", [128, 32 * D], BF16))
            xqT = esB.enter_context(nc.sbuf_tensor("xqT", [128, 8 * (S + 2)], BF16))
            hsil = esB.enter_context(nc.sbuf_tensor("hsil", [128, 32 * GRP], F32))
            hq = esB.enter_context(nc.sbuf_tensor("hq", [128, 32 * GRP], BF16))
            t1b = esB.enter_context(nc.sbuf_tensor("t1b", [128, 2 * 8 * 512], BF16))
            x1t = esB.enter_context(nc.sbuf_tensor("x1t", [128, 2 * D], F32))
            x1g = esB.enter_context(nc.sbuf_tensor("x1g", [128, 2 * 2 * D], F32))
            hf16 = esB.enter_context(nc.sbuf_tensor("hf16", [128, 2048], F16))
            xqf16 = hf16[:, 0:D]
            xqs = esB.enter_context(nc.sbuf_tensor("xqs", [128, 2 * D], BF16))
            s2row = esB.enter_context(nc.sbuf_tensor("s2row", [128, GRP], F32))
            tmpc = esB.enter_context(nc.sbuf_tensor("tmpc", [128, 2 * GRP], F32))
            cbuf = esB.enter_context(nc.sbuf_tensor("cbuf", [128, 8 * GRP], F32))
            s2st = esB.enter_context(nc.sbuf_tensor("s2st", [1, 256], F32))
            amx = esB.enter_context(nc.sbuf_tensor("amx", [128, GRP], F32))
            amc = esB.enter_context(nc.sbuf_tensor("amc", [128, 8], F32))
            bn6b = esB.enter_context(nc.sbuf_tensor("bn6b", [128, 12], F32))
            psM0 = esB.enter_context(nc.psum_tensor("psM0", [128, 512], F32))
            psM1 = esB.enter_context(nc.psum_tensor("psM1", [128, 512], F32))
            psM2 = esB.enter_context(nc.psum_tensor("psM2", [128, 512], F32))
            psM3 = esB.enter_context(nc.psum_tensor("psM3", [128, 512], F32))
            psO0 = esB.enter_context(nc.psum_tensor("psO0", [128, 512], F32))
            psO1 = esB.enter_context(nc.psum_tensor("psO1", [128, 512], F32))
            psNb = esB.enter_context(nc.psum_tensor("psNb", [128, 512], F32))
            psTb = esB.enter_context(nc.psum_tensor("psTb", [128, 1024], BF16))
            psM = [psM0, psM1, psM2, psM3]
            psO = [psO0, psO1]

            @sec_sq
            def _(sy):
                for q4 in range(4):
                    toks[("t2T", q4)] = SQ.emit(
                        "dma_start",
                        out=t2T[:, q4 * 8 * D:(q4 + 1) * 8 * D]
                        .rearrange("p (c d) -> p c d", d=D),
                        in_=t2scr[1024 * q4:1024 * (q4 + 1), :]
                        .rearrange("(c p) d -> p c d", p=128),
                        waits=barrier + [toks[("w2q_out", i)]
                                         for i in range(16)])

            @sec_g
            def _(g):
                ap = xqT[:].rearrange("p (c s) -> p c s", s=S + 2)
                GE.emit("memset", ap[:, :, 0:1], 0.0, waits=barrier)
                toks["xq_guard"] = GE.emit("memset", ap[:, :, S + 1:S + 2],
                                           0.0)

            def emit_xq_tile(st):
                par = st % 2

                @sec_gq
                def _(g, st=st, par=par):
                    w = list(barrier)
                    if st >= 2:
                        w.append(toks[("xq_done", st - 2)])
                    toks[("x1t_dma", st)] = GQ.emit(
                        "dma_start", out=x1t[:, par * D:(par + 1) * D],
                        in_=x1scr[128 * st:128 * (st + 1), :], waits=w)

                @sec_v
                def _(ve, st=st, par=par):
                    xs = x1t[:, par * D:(par + 1) * D]
                    VE.wait(toks[("x1t_dma", st)])
                    for b in barrier:
                        VE.wait(b)
                    VE.emit("bn_stats", bn6b[:, 0:6], xs[:, 0:512])
                    VE.emit("bn_stats", bn6b[:, 6:12], xs[:, 512:1024])
                    VE.emit("bn_aggr", smalls[:, 2:4], bn6b[:, 0:12])
                    VE.emit("tensor_reduce", smalls[:, 16:17], xs, AX.X,
                            ALU.max)
                    VE.emit("tensor_reduce", smalls[:, 17:18], xs, AX.X,
                            ALU.max, negate=True)
                    toks[("veps2", st)] = VE.emit(
                        "tensor_scalar_add", smalls[:, 4:5], smalls[:, 3:4],
                        1e-5)

                @sec_s
                def _(se, st=st):
                    SE.wait(toks[("veps2", st)])
                    for b in barrier:
                        SE.wait(b)
                    SE.emit("activation", out=smalls[:, 5:6],
                            in_=smalls[:, 4:5], func=AF.Ln)
                    toks[("rstd2", st)] = SE.emit(
                        "activation", out=smalls[:, 0:1], in_=smalls[:, 5:6],
                        func=AF.Exp, scale=-0.5)

                @sec_v
                def _(ve, st=st, par=par):
                    VE.wait(toks[("rstd2", st)])
                    VE.emit("tensor_tensor", smalls[:, 18:19],
                            smalls[:, 16:17], smalls[:, 2:3], ALU.subtract)
                    VE.emit("tensor_tensor", smalls[:, 19:20],
                            smalls[:, 17:18], smalls[:, 2:3], ALU.add)
                    VE.emit("tensor_tensor", smalls[:, 20:21],
                            smalls[:, 18:19], smalls[:, 19:20], ALU.max)
                    VE.emit("tensor_tensor", smalls[:, 21:22],
                            smalls[:, 20:21], smalls[:, 0:1], ALU.mult)
                    VE.emit("tensor_scalar_max", smalls[:, 22:23],
                            smalls[:, 21:22], 1e-5)
                    VE.emit("reciprocal", smalls[:, 23:24], smalls[:, 22:23])
                    VE.emit("tensor_scalar_mul", smalls[:, 24:25],
                            smalls[:, 23:24], 127.0)
                    VE.emit("tensor_scalar_mul", smalls[:, 25:26],
                            smalls[:, 22:23], 1.0 / 127.0)
                    VE.emit("tensor_tensor", smalls[:, 26:27],
                            smalls[:, 25:26], smalls[:, 9:10], ALU.mult)
                    VE.emit("tensor_tensor", smalls[:, 12:13],
                            smalls[:, 0:1], smalls[:, 24:25], ALU.mult)
                    VE.emit("scalar_tensor_tensor", smalls[:, 13:14],
                            smalls[:, 2:3], -1.0, smalls[:, 12:13],
                            ALU.mult, ALU.mult)
                    VE.emit("tensor_scalar_add", smalls[:, 13:14],
                            smalls[:, 13:14], 1536.0)
                    toks[("xqpre", st)] = VE.emit(
                        "tensor_scalar_mul", smalls[:, 27:28],
                        smalls[:, 26:27], -1536.0)

                @sec_s
                def _(se, st=st, par=par):
                    SE.wait(toks[("xqpre", st)])
                    if st >= 2:
                        SE.wait(toks[("xqtr", st - 2)])
                    SE.emit("activation", out=xqf16,
                            in_=x1t[:, par * D:(par + 1) * D],
                            func=AF.Identity, scale=smalls[:, 12:13],
                            bias=smalls[:, 13:14])
                    toks[("xqs", st)] = SE.emit(
                        "activation", out=xqs[:, par * D:(par + 1) * D],
                        in_=xqf16, func=AF.Identity,
                        scale=smalls[:, 26:27], bias=smalls[:, 27:28])

            def emit_xq_back(st):
                par = st % 2

                @sec_t
                def _(te, st=st, par=par):
                    TE.wait(toks[("xqs", st)])
                    for b in barrier:
                        TE.wait(b)
                    if st >= 1:
                        TE.wait(toks[("xq_done", st - 1)])
                    for c in range(8):
                        TE.emit("transpose", psTb[:, 128 * c:128 * (c + 1)],
                                xqs[:, par * D + 128 * c:par * D + 128 * (c + 1)],
                                idbf[:])
                    toks[("xqtr", st)] = TE.tok()

                @sec_v
                def _(ve, st=st):
                    VE.wait(toks[("xqtr", st)])
                    src = psTb[:].rearrange("p (c w) -> p c w", w=128)
                    dst = xqT[:].rearrange("p (c s) -> p c s", s=S + 2)
                    toks[("xq_done", st)] = VE.emit(
                        "tensor_copy",
                        dst[:, :, 1 + 128 * st:1 + 128 * (st + 1)], src[:])

            def emit_group_A(g):
                gpar = g % 2
                qg = g - 1

                @sec_gq
                def _(gp, g=g, gpar=gpar):
                    w = list(barrier)
                    if g >= 2:
                        w.append(toks[("out_store", g - 2)])
                    toks[("x1g_dma", g)] = GQ.emit(
                        "dma_start",
                        out=x1g[:, gpar * 2 * D:(gpar + 1) * 2 * D]
                        .rearrange("p (c d) -> p c d", d=D),
                        in_=x1scr[GRP * g:GRP * (g + 1), :]
                        .rearrange("(c p) d -> p c d", p=128), waits=w)

                for e in range(8):
                    bpar = (g * 8 + e) % 2
                    if qg >= 0 and e < 2:
                        emit_hmul_ve(qg, e)

                    @sec_sq
                    def _(sy, g=g, e=e, bpar=bpar):
                        w = list(barrier)
                        w += [toks[("w1q_out", i)] for i in range(16)]
                        if g * 8 + e >= 2:
                            w.append(toks[("t1_used", g * 8 + e - 2)])
                        toks[("t1_dma", g, e)] = SQ.emit(
                            "dma_start",
                            out=t1b[:, bpar * 8 * 512:(bpar + 1) * 8 * 512]
                            .rearrange("p (c h) -> p c h", h=512),
                            in_=t1scr[:, 512 * e:512 * (e + 1)]
                            .rearrange("(c p) h -> p c h", p=128), waits=w)

                    for ht in range(4):
                        hh = 4 * e + ht
                        bank = psM[hh % 4]
                        cpar = hh % 2

                        @sec_t
                        def _(te, g=g, e=e, ht=ht, hh=hh, bank=bank,
                              bpar=bpar):
                            TE.wait(toks[("t1_dma", g, e)])
                            TE.wait(toks[("xq_done", min(2 * g + 2, NT - 1))])
                            TE.wait(toks["xq_guard"])
                            if hh >= 4:
                                TE.wait(toks[("conv", g, hh - 4)])
                            elif g >= 1:
                                TE.wait(toks[("conv", g - 1, hh + 28)])
                            for kc in range(8):
                                TE.emit(
                                    "matmul", bank[:, 0:GRP + 2],
                                    t1b[:, bpar * 8 * 512 + 512 * kc + 128 * ht:
                                        bpar * 8 * 512 + 512 * kc + 128 * (ht + 1)],
                                    xqT[:, (S + 2) * kc + GRP * g:
                                        (S + 2) * kc + GRP * g + GRP + 2],
                                    start=(kc == 0), stop=(kc == 7))
                            toks[("mm1", g, hh)] = TE.tok()
                            if ht == 3:
                                toks[("t1_used", g * 8 + e)] = \
                                    toks[("mm1", g, hh)]

                        @sec_s
                        def _(se, g=g, hh=hh, bank=bank, cpar=cpar):
                            SE.wait(toks[("mm1", g, hh)])
                            SE.wait(toks["consts"])
                            if hh >= 2:
                                SE.wait(toks[("conv", g, hh - 2)])
                            elif g >= 1:
                                SE.wait(toks[("conv", g - 1, hh + 30)])
                            t0 = tmpc[:, cpar * GRP:(cpar + 1) * GRP]
                            toks[("cv1", g, hh)] = SE.emit(
                                "activation", out=t0, in_=bank[:, 0:GRP],
                                func=AF.Identity,
                                scale=convw[:, 3 * hh:3 * hh + 1],
                                bias=convb[:, hh:hh + 1])

                        @sec_v
                        def _(ve, g=g, hh=hh, bank=bank, cpar=cpar):
                            VE.wait(toks[("cv1", g, hh)])
                            qlast = hh - 8 - (hh % 4) + 3
                            if qlast >= 0:
                                VE.wait(toks[("sgm", g, qlast)])
                                VE.wait(toks[("silu", g, qlast)])
                            elif g >= 1:
                                VE.wait(toks[("sgm", g - 1, qlast + 32)])
                                VE.wait(toks[("silu", g - 1, qlast + 32)])
                            t0 = tmpc[:, cpar * GRP:(cpar + 1) * GRP]
                            qpar = (hh // 4) % 2
                            co = cbuf[:, (qpar * 4 + hh % 4) * GRP:
                                      (qpar * 4 + hh % 4 + 1) * GRP]
                            VE.emit("scalar_tensor_tensor", t0,
                                    bank[:, 1:GRP + 1],
                                    convw[:, 3 * hh + 1:3 * hh + 2], t0,
                                    ALU.mult, ALU.add)
                            toks[("conv", g, hh)] = VE.emit(
                                "scalar_tensor_tensor", co,
                                bank[:, 2:GRP + 2],
                                convw[:, 3 * hh + 2:3 * hh + 3], t0,
                                ALU.mult, ALU.add)

                        if hh % 4 == 3:
                            if e == 2 and 2 * g + 5 < NT:
                                emit_xq_tile(2 * g + 5)
                            if e == 5 and 2 * g + 6 < NT:
                                emit_xq_tile(2 * g + 6)
                            if qg >= 0 and e % 2 == 0:
                                emit_shift(qg, e // 2)

                            @sec_s
                            def _(se, g=g, hh=hh):
                                SE.wait(toks[("conv", g, hh)])
                                if g >= 1:
                                    SE.wait(toks[("hqt", g - 1,
                                                  (hh - 3) // 8)])
                                qpar = (hh // 4) % 2
                                cq = cbuf[:, qpar * 4 * GRP:
                                          (qpar + 1) * 4 * GRP]
                                toks[("sgm", g, hh)] = SE.emit(
                                    "activation",
                                    out=hsil[:, GRP * (hh - 3):GRP * (hh + 1)],
                                    in_=cq, func=AF.Sigmoid)

                            @sec_g
                            def _(ge, g=g, hh=hh):
                                GE.wait(toks[("sgm", g, hh)])
                                qpar = (hh // 4) % 2
                                cq = cbuf[:, qpar * 4 * GRP:
                                          (qpar + 1) * 4 * GRP]
                                hsl = hsil[:, GRP * (hh - 3):GRP * (hh + 1)]
                                toks[("silu", g, hh)] = GE.emit(
                                    "tensor_tensor", hsl, cq, hsl, ALU.mult)



            def emit_Bamax(g):
                @sec_v
                def _(ve, g=g):
                    for hh in (3, 7, 11, 15, 19, 23, 27, 31):
                        VE.wait(toks[("silu", g, hh)])
                    src = hsil[:].rearrange("p (c s) -> p s c", c=32)
                    toks[("amaxp", g)] = VE.emit(
                        "tensor_reduce", amx[:], src, AX.X, ALU.max,
                        apply_absolute_value=True)

            def emit_Brest(g):
                @sec_t
                def _(te, g=g):
                    TE.wait(toks[("amaxp", g)])
                    if g >= 1:
                        TE.wait(toks[("s2row", g - 1)])
                    TE.emit("transpose", psNb[0:128, 0:128], amx[:, 0:128],
                            idf[:])
                    TE.emit("transpose", psNb[0:128, 128:256],
                            amx[:, 128:256], idf[:])
                    toks[("amtr", g)] = TE.tok()

                @sec_v
                def _(ve, g=g):
                    ab = (g % 2) * 4
                    VE.wait(toks[("amtr", g)])
                    VE.emit("tensor_reduce", amc[:, ab:ab + 1],
                            psNb[:, 0:128], AX.X, ALU.max)
                    VE.emit("tensor_reduce", amc[:, ab + 1:ab + 2],
                            psNb[:, 128:256], AX.X, ALU.max)
                    VE.emit("tensor_scalar_max", amc[:, ab + 2:ab + 3],
                            amc[:, ab:ab + 1], 1e-5)
                    VE.emit("tensor_scalar_max", amc[:, ab + 3:ab + 4],
                            amc[:, ab + 1:ab + 2], 1e-5)
                    VE.emit("tensor_scalar_mul", amc[:, ab:ab + 1],
                            amc[:, ab + 2:ab + 3], 1.0 / 127.0)
                    VE.emit("tensor_scalar_mul", amc[:, ab + 1:ab + 2],
                            amc[:, ab + 3:ab + 4], 1.0 / 127.0)
                    VE.emit("tensor_tensor", amc[:, ab:ab + 1],
                            amc[:, ab:ab + 1], smalls[:, 11:12], ALU.mult)
                    VE.emit("tensor_tensor", amc[:, ab + 1:ab + 2],
                            amc[:, ab + 1:ab + 2], smalls[:, 11:12],
                            ALU.mult)
                    VE.emit("reciprocal", amc[:, ab + 2:ab + 3],
                            amc[:, ab + 2:ab + 3])
                    VE.emit("reciprocal", amc[:, ab + 3:ab + 4],
                            amc[:, ab + 3:ab + 4])
                    VE.emit("tensor_scalar_mul", amc[:, ab + 2:ab + 3],
                            amc[:, ab + 2:ab + 3], 127.0)
                    toks[("s2col", g)] = VE.emit(
                        "tensor_scalar_mul", amc[:, ab + 3:ab + 4],
                        amc[:, ab + 3:ab + 4], 127.0)

                @sec_t
                def _(te, g=g):
                    ab = (g % 2) * 4
                    TE.wait(toks[("s2col", g)])
                    TE.emit("transpose", psNb[0:1, 256:384],
                            amc[:, ab + 2:ab + 3], idf[:])
                    TE.emit("transpose", psNb[0:1, 384:512],
                            amc[:, ab + 3:ab + 4], idf[:])
                    toks[("s2tr", g)] = TE.tok()

                @sec_v
                def _(ve, g=g):
                    VE.wait(toks[("s2tr", g)])
                    VE.emit("tensor_copy", s2st[0:1, 0:128],
                            psNb[0:1, 256:384])
                    toks[("s2st", g)] = VE.emit(
                        "tensor_copy", s2st[0:1, 128:256],
                        psNb[0:1, 384:512])

                @sec_t
                def _(te, g=g):
                    TE.wait(toks[("s2st", g)])
                    TE.emit("matmul", psNb[:, 256:512], onesrow[0:1, :],
                            s2st[0:1, 0:256], start=True, stop=True)
                    toks[("s2bc", g)] = TE.tok()

                @sec_v
                def _(ve, g=g):
                    VE.wait(toks[("s2bc", g)])
                    if g >= 1:
                        VE.wait(toks[("hmul", g - 1, 3)])
                    toks[("s2row", g)] = VE.emit(
                        "tensor_copy", s2row[:], psNb[:, 256:512])

                if debug and g == 0:
                    @sec_sq
                    def _(sy, g=g):
                        toks["dbg_g0a"] = SQ.emit(
                            "dma_start", out=dbg["d_hsil"][:],
                            in_=hsil[:], waits=[toks[("amaxp", g)]])


            def emit_Q_pool(g):
                @sec_g
                def _(ge, g=g):
                    GE.wait(toks[("s2row", g)])
                    if g == 0 and "dbg_g0a" in toks:
                        GE.wait(toks["dbg_g0a"])
                    for sl in (2, 3):
                        for c in range(8):
                            hh = 8 * sl + c
                            hsl = hsil[:, GRP * hh:GRP * (hh + 1)]
                            GE.emit("tensor_tensor", hsl, hsl, s2row[:],
                                    ALU.mult)
                        toks[("hmul", g, sl)] = GE.tok()

            def emit_hmul_ve(qg, sl):
                @sec_v
                def _(ve, qg=qg, sl=sl):
                    VE.wait(toks[("s2row", qg)])
                    if qg == 0 and "dbg_g0a" in toks:
                        VE.wait(toks["dbg_g0a"])
                    for c in range(8):
                        hh = 8 * sl + c
                        hsl = hsil[:, GRP * hh:GRP * (hh + 1)]
                        VE.emit("tensor_tensor", hsl, hsl, s2row[:],
                                ALU.mult)
                    toks[("hmul", qg, sl)] = VE.tok()

            def emit_shift(qg, sl):
                @sec_s
                def _(se, qg=qg, sl=sl):
                    SE.wait(toks[("hmul", qg, sl)])
                    SE.wait(toks["consts"])
                    if qg >= 1 and sl == 0:
                        SE.wait(toks[("mm2", qg - 1, 1, 1)])
                    if qg == 1 and sl == 0 and "dbg_g0b" in toks:
                        SE.wait(toks["dbg_g0b"])
                    SE.emit("activation", out=hf16[:],
                            in_=hsil[:, 8 * GRP * sl:8 * GRP * (sl + 1)],
                            func=AF.Identity, bias=smalls[:, 7:8])
                    toks[("hqt", qg, sl)] = SE.emit(
                        "activation",
                        out=hq[:, 8 * GRP * sl:8 * GRP * (sl + 1)],
                        in_=hf16[:], func=AF.Identity,
                        bias=smalls[:, 28:29])

            def emit_Cmm(g, m):
                for n in range(2):
                        ob = psO[n]

                        @sec_t
                        def _(te, g=g, m=m, n=n, ob=ob):
                            for sl in range(4):
                                TE.wait(toks[("hqt", g, sl)])
                            for q4 in range(4):
                                TE.wait(toks[("t2T", q4)])
                            if m >= 1:
                                TE.wait(toks[("mm2_ev", g, m - 1, n)])
                            elif g >= 1:
                                TE.wait(toks[("mm2_ev", g - 1, 1, n)])
                            for kc in range(32):
                                TE.emit(
                                    "matmul", ob[:],
                                    hq[:, GRP * kc + 128 * m:GRP * kc + 128 * (m + 1)],
                                    t2T[:, D * kc + 512 * n:D * kc + 512 * (n + 1)],
                                    start=(kc == 0), stop=(kc == 31))
                            toks[("mm2", g, m, n)] = TE.tok()

            def emit_Cev(g, m, store=False):
                gpar = g % 2
                for n in range(2):
                        ob = psO[n]

                        @sec_v
                        def _(ve, g=g, m=m, n=n, ob=ob, gpar=gpar):
                            VE.wait(toks[("mm2", g, m, n)])
                            VE.wait(toks[("x1g_dma", g)])
                            xsl = x1g[:, gpar * 2 * D + m * D + 512 * n:
                                      gpar * 2 * D + m * D + 512 * (n + 1)]
                            toks[("mm2_ev", g, m, n)] = VE.emit(
                                "scalar_tensor_tensor", xsl, ob[:],
                                amc[:, (g % 2) * 4 + m:(g % 2) * 4 + m + 1],
                                xsl, ALU.mult, ALU.add)

                if not store:
                    return

                @sec_gq
                def _(gp, g=g, gpar=gpar):
                    toks[("out_store", g)] = GQ.emit(
                        "dma_start",
                        out=out_h[GRP * g:GRP * (g + 1), :]
                        .rearrange("(c p) d -> p c d", p=128),
                        in_=x1g[:, gpar * 2 * D:(gpar + 1) * 2 * D]
                        .rearrange("p (c d) -> p c d", d=D),
                        waits=[toks[("mm2_ev", g, m, n)]
                               for m in range(2) for n in range(2)])

            for st in range(5):
                emit_xq_tile(st)
                if st < 3:
                    emit_xq_back(st)
            for g in range(NGRP):
                if g >= 1:
                    emit_Q_pool(g - 1)
                for st in (2 * g + 3, 2 * g + 4):
                    if st < NT:
                        emit_xq_back(st)
                emit_group_A(g)
                if debug and g == 1:
                    @sec_sq
                    def _(sy):
                        toks["dbg_g0b"] = SQ.emit(
                            "dma_start", out=dbg["d_hq"][:], in_=hq[:],
                            waits=[toks[("hqt", 0, 3)]])
                if g >= 1:
                    emit_Cmm(g - 1, 0)
                    emit_Cev(g - 1, 0)
                emit_Bamax(g)
                if g >= 1:
                    emit_Cmm(g - 1, 1)
                emit_Brest(g)
                if g >= 1:
                    emit_Cev(g - 1, 1, store=True)
            emit_Q_pool(NGRP - 1)
            emit_hmul_ve(NGRP - 1, 0)
            emit_hmul_ve(NGRP - 1, 1)
            for sl in range(4):
                emit_shift(NGRP - 1, sl)
            emit_Cmm(NGRP - 1, 0)
            emit_Cev(NGRP - 1, 0)
            emit_Cmm(NGRP - 1, 1)
            emit_Cev(NGRP - 1, 1, store=True)

            if debug:
                @sec_sq
                def _(sy):
                    SQ.emit("dma_start", out=dbg["d_xqt"][:], in_=xqT[:],
                            waits=[toks[("xq_done", NT - 1)],
                                   toks["xq_guard"]])

            @sec_gq
            def _(gp):
                GQ.wait(toks[("out_store", NGRP - 1)])
                for t in SQ.toks():
                    GQ.wait(t)
                for t in GQ.toks():
                    GQ.wait(t)

            flush_phase()

    nc.finalize()
    return nc, dbg


_PROG_CACHE = {}


def _get_program(debug=False):
    key = bool(debug)
    if key not in _PROG_CACHE:
        _PROG_CACHE[key] = build_program(debug=debug)
    return _PROG_CACHE[key]


def make_inputs(L0_attn_x, L0_attn_W_query, L0_attn_W_key, L0_attn_W_value,
                L0_ffn_w1, L0_ffn_conv_w, L0_ffn_conv_b, L0_ffn_w2):
    import ml_dtypes
    x = np.ascontiguousarray(np.asarray(L0_attn_x, dtype=np.float32))
    wq = np.ascontiguousarray(np.asarray(L0_attn_W_query, dtype=np.float32))
    wk = np.ascontiguousarray(np.asarray(L0_attn_W_key, dtype=np.float32))
    wv = np.ascontiguousarray(np.asarray(L0_attn_W_value, dtype=np.float32))
    w1t = np.ascontiguousarray(np.asarray(L0_ffn_w1, dtype=np.float32).T)
    w2t = np.ascontiguousarray(np.asarray(L0_ffn_w2, dtype=np.float32).T)
    cw = np.asarray(L0_ffn_conv_w, dtype=np.float32).reshape(HID, 3)
    cb = np.asarray(L0_ffn_conv_b, dtype=np.float32)
    convw = np.ascontiguousarray(
        cw.reshape(32, 128, 3).transpose(1, 0, 2).reshape(128, 96))
    convb = np.ascontiguousarray(cb.reshape(32, 128).T)
    idbf = np.eye(128, dtype=ml_dtypes.bfloat16)
    idf = np.eye(128, dtype=np.float32)
    mask = np.triu(np.ones((128, 128), dtype=np.float32)).astype(
        ml_dtypes.bfloat16)
    onesb = np.ones((128, 1), dtype=ml_dtypes.bfloat16)
    onesf = np.ones((128, 1), dtype=np.float32)
    onesrow = np.ones((1, 128), dtype=np.float32)

    shared = dict(wq=wq, wk=wk, wv=wv, w1t=w1t, w2t=w2t, convw=convw,
                  convb=convb, idbf=idbf, idf=idf, mask=mask, onesb=onesb,
                  onesf=onesf, onesrow=onesrow)
    in_maps = []
    for c in range(NCORES):
        m = dict(shared)
        m["x"] = np.ascontiguousarray(x[c])
        in_maps.append(m)
    return in_maps


def kernel(L0_attn_x, L0_attn_W_query, L0_attn_W_key, L0_attn_W_value,
           L0_ffn_w1, L0_ffn_conv_w, L0_ffn_conv_b, L0_ffn_w2):
    in_maps = make_inputs(L0_attn_x, L0_attn_W_query, L0_attn_W_key,
                          L0_attn_W_value, L0_ffn_w1, L0_ffn_conv_w,
                          L0_ffn_conv_b, L0_ffn_w2)
    nc, _ = _get_program(debug=False)
    res = run_bass_kernel_spmd(nc, in_maps, core_ids=list(range(NCORES)))
    out = np.stack([np.asarray(r["out"]) for r in res.results], axis=0)
    return out.astype(np.float32)


# version 17
# speedup vs baseline: 1.0438x; 1.0011x over previous
from contextlib import ExitStack

import numpy as np
import concourse.bass as bass
import concourse.mybir as mybir
from concourse.bass_utils import run_bass_kernel_spmd

F32 = mybir.dt.float32
BF16 = mybir.dt.bfloat16
F16 = mybir.dt.float16
AF = mybir.ActivationFunctionType
ALU = mybir.AluOpType
AX = mybir.AxisListType

S = 2048
D = 1024
H = 16
DH = 64
HID = 4096
NCORES = 8
NT = S // 128
QTW = 512
NQT = S // QTW
GRP = 256
NGRP = S // GRP
VW = DH + 1

DRY = [False]


class Eng:

    def __init__(self, raw, sem, chain):
        self.raw = raw
        self.sem = sem
        self.chain = chain
        self.n = 0
        self._waited = {}

    def state(self):
        return (self.n, dict(self._waited))

    def restore(self, st):
        self.n, self._waited = st[0], dict(st[1])

    def endcount(self):
        return self.n

    def tok(self):
        return (self.sem, self.n)

    def wait(self, tok):
        if tok is None:
            return
        if isinstance(tok, list):
            for t in tok:
                self.wait(t)
            return
        sem, val = tok
        if val <= 0:
            return
        key = id(sem)
        if self._waited.get(key, -1) >= val:
            return
        if not DRY[0]:
            self.raw.wait_ge(sem, val)
        self._waited[key] = val

    def emit(self, name, *a, waits=(), **kw):
        if self.chain and self.n > 0:
            self.wait((self.sem, self.n))
        for t in waits:
            self.wait(t)
        if not DRY[0]:
            inst = getattr(self.raw, name)(*a, **kw)
            inst.then_inc(self.sem, 1)
        self.n += 1
        return (self.sem, self.n)


class Queue:

    def __init__(self, raw, sems):
        self.raw = raw
        self.sems = list(sems)
        self.counts = [0] * len(self.sems)
        self.i = 0
        self._waited = {}

    def state(self):
        return (list(self.counts), self.i, dict(self._waited))

    def restore(self, st):
        self.counts, self.i, self._waited = list(st[0]), st[1], dict(st[2])

    def endcount(self):
        return tuple(self.counts)

    def wait(self, tok):
        if tok is None:
            return
        if isinstance(tok, list):
            for t in tok:
                self.wait(t)
            return
        sem, val = tok
        if val <= 0:
            return
        key = id(sem)
        if self._waited.get(key, -1) >= val:
            return
        if not DRY[0]:
            self.raw.wait_ge(sem, val)
        self._waited[key] = val

    def emit(self, name, *a, waits=(), **kw):
        j = self.i % len(self.sems)
        self.i += 1
        self.wait((self.sems[j], self.counts[j]))
        for t in waits:
            self.wait(t)
        if not DRY[0]:
            inst = getattr(self.raw, name)(*a, **kw)
            inst.then_inc(self.sems[j], 16)
        self.counts[j] += 16
        return (self.sems[j], self.counts[j])

    def toks(self):
        return [(s, c) for s, c in zip(self.sems, self.counts) if c > 0]


def build_program(debug=False):
    nc = bass.Bass("TRN2", target_bir_lowering=False, debug=False)

    x_h = nc.dram_tensor("x", [S, D], F32, kind="ExternalInput").ap()
    wq_h = nc.dram_tensor("wq", [D, D], F32, kind="ExternalInput").ap()
    wk_h = nc.dram_tensor("wk", [D, D], F32, kind="ExternalInput").ap()
    wv_h = nc.dram_tensor("wv", [D, D], F32, kind="ExternalInput").ap()
    w1t_h = nc.dram_tensor("w1t", [D, HID], F32, kind="ExternalInput").ap()
    w2t_h = nc.dram_tensor("w2t", [HID, D], F32, kind="ExternalInput").ap()
    convw_h = nc.dram_tensor("convw", [128, 96], F32, kind="ExternalInput").ap()
    convb_h = nc.dram_tensor("convb", [128, 32], F32, kind="ExternalInput").ap()
    idbf_h = nc.dram_tensor("idbf", [128, 128], BF16, kind="ExternalInput").ap()
    idf_h = nc.dram_tensor("idf", [128, 128], F32, kind="ExternalInput").ap()
    mask_h = nc.dram_tensor("mask", [128, 128], BF16, kind="ExternalInput").ap()
    onesb_h = nc.dram_tensor("onesb", [128, 1], BF16, kind="ExternalInput").ap()
    onesf_h = nc.dram_tensor("onesf", [128, 1], F32, kind="ExternalInput").ap()
    onesrow_h = nc.dram_tensor("onesrow", [1, 128], F32, kind="ExternalInput").ap()

    out_h = nc.dram_tensor("out", [S, D], F32, kind="ExternalOutput").ap()

    x1scr = nc.dram_tensor("x1scr", [S, D], F32, kind="Internal").ap()
    t1scr = nc.dram_tensor("t1scr", [D, HID], BF16, kind="Internal").ap()
    t2scr = nc.dram_tensor("t2scr", [HID, D], BF16, kind="Internal").ap()

    dbg = {}
    if debug:
        def dbg_t(name, shape, dt=F32):
            dbg[name] = nc.dram_tensor(name, shape, dt, kind="ExternalOutput").ap()
        dbg_t("d_qt", [128, 8 * S], BF16)
        dbg_t("d_kt", [128, 8 * S], BF16)
        dbg_t("d_vaug", [128, NT * H * VW], BF16)
        dbg_t("d_x1", [S, D], F32)
        dbg_t("d_xqt", [128, 8 * (S + 2)], BF16)
        dbg_t("d_t1t", [D, HID], BF16)
        dbg_t("d_t2t", [HID, D], BF16)
        dbg_t("d_hq", [128, 32 * GRP], BF16)
        dbg_t("d_hsil", [128, 32 * GRP], F32)
        dbg_t("d_nb", [1, 16], F32)

    es = ExitStack()
    with es:
        s_sq = [es.enter_context(nc.semaphore(f"s_sq{i}")) for i in range(4)]
        s_gq = [es.enter_context(nc.semaphore(f"s_gq{i}")) for i in range(4)]
        s_t = es.enter_context(nc.semaphore("s_t"))
        s_v = es.enter_context(nc.semaphore("s_v"))
        s_a = es.enter_context(nc.semaphore("s_a"))
        s_g = es.enter_context(nc.semaphore("s_g"))
        idbf = es.enter_context(nc.sbuf_tensor("idbf_s", [128, 128], BF16))
        idf = es.enter_context(nc.sbuf_tensor("idf_s", [128, 128], F32))
        maskt = es.enter_context(nc.sbuf_tensor("mask_s", [128, 128], BF16))
        onesb = es.enter_context(nc.sbuf_tensor("onesb_s", [128, 1], BF16))
        onesf = es.enter_context(nc.sbuf_tensor("onesf_s", [128, 1], F32))
        onesrow = es.enter_context(nc.sbuf_tensor("onesrow_s", [1, 128], F32))
        convw = es.enter_context(nc.sbuf_tensor("convw_s", [128, 96], F32))
        convb = es.enter_context(nc.sbuf_tensor("convb_s", [128, 32], F32))
        smalls = es.enter_context(nc.sbuf_tensor("smalls", [128, 64], F32))
        nrm = es.enter_context(nc.sbuf_tensor("nrm", [1, 64], F32))
        block = es.enter_context(nc.Block())

        SQ = Queue(nc.sync, s_sq)
        GQ = Queue(nc.gpsimd, s_gq)
        TE = Eng(nc.tensor, s_t, chain=False)
        VE = Eng(nc.vector, s_v, chain=True)
        SE = Eng(nc.scalar, s_a, chain=True)
        GE = Eng(nc.gpsimd, s_g, chain=True)
        ENGS = [SQ, GQ, TE, VE, SE, GE]

        toks = {}
        phase_fns = []

        def sec(key):
            def deco(fn):
                phase_fns.append((key, fn))
                return fn
            return deco

        sec_sq = sec("sync")
        sec_gq = sec("gpsimd")
        sec_t = sec("tensor")
        sec_v = sec("vector")
        sec_s = sec("scalar")
        sec_g = sec("gpsimd")

        def flush_phase():
            snaps = [e.state() for e in ENGS]
            DRY[0] = True
            for _, fn in phase_fns:
                fn(None)
            DRY[0] = False
            ends = [e.endcount() for e in ENGS]
            for e, st in zip(ENGS, snaps):
                e.restore(st)
            for key, raw in (("sync", None), ("gpsimd", None),
                             ("tensor", None), ("vector", None),
                             ("scalar", None)):
                fns = [f for k, f in phase_fns if k == key]
                if not fns:
                    continue

                def runner(eng, fns=fns):
                    for f in fns:
                        f(eng)
                getattr(block, key)(runner)
            for e, ec in zip(ENGS, ends):
                assert e.endcount() == ec, "dry/real emission diverged"
            phase_fns.clear()

        @sec_sq
        def _(sy):
            SQ.emit("dma_start", out=idbf[:], in_=idbf_h[:])
            SQ.emit("dma_start", out=idf[:], in_=idf_h[:])
            SQ.emit("dma_start", out=maskt[:], in_=mask_h[:])
            SQ.emit("dma_start", out=onesb[:], in_=onesb_h[:])
            SQ.emit("dma_start", out=onesf[:], in_=onesf_h[:])
            SQ.emit("dma_start", out=onesrow[:], in_=onesrow_h[:])
            SQ.emit("dma_start", out=convw[:], in_=convw_h[:])
            SQ.emit("dma_start", out=convb[:], in_=convb_h[:])
            toks["consts"] = SQ.toks()

        @sec_v
        def _(ve):
            toks["c1536"] = VE.emit("memset", smalls[:, 7:8], 1536.0)
            toks["cn1536"] = VE.emit("memset", smalls[:, 28:29], -1536.0)

        flush_phase()

        esA = ExitStack()
        qT = esA.enter_context(nc.sbuf_tensor("qT", [128, 8 * S], BF16))
        kT = esA.enter_context(nc.sbuf_tensor("kT", [128, 8 * S], BF16))
        vaug = esA.enter_context(nc.sbuf_tensor("vaug", [128, NT * H * VW], BF16))

        esA1 = ExitStack()
        with esA1:
            Wqb = esA1.enter_context(nc.sbuf_tensor("Wqb", [128, 8 * D], BF16))
            Wkb = esA1.enter_context(nc.sbuf_tensor("Wkb", [128, 8 * D], BF16))
            Wvb = esA1.enter_context(nc.sbuf_tensor("Wvb", [128, 8 * D], BF16))
            wstage = esA1.enter_context(nc.sbuf_tensor("wstage", [128, 2 * 2048], F32))
            xstage = esA1.enter_context(nc.sbuf_tensor("xstage", [128, 2 * D], F32))
            lnstage = esA1.enter_context(nc.sbuf_tensor("lnstage", [128, 2 * D], BF16))
            ln1T = esA1.enter_context(nc.sbuf_tensor("ln1T", [128, 8 * 512], BF16))
            bn6 = esA1.enter_context(nc.sbuf_tensor("bn6", [128, 12], F32))
            psQK0 = esA1.enter_context(nc.psum_tensor("psQK0", [128, 512], F32))
            psQK1 = esA1.enter_context(nc.psum_tensor("psQK1", [128, 512], F32))
            psQK2 = esA1.enter_context(nc.psum_tensor("psQK2", [128, 512], F32))
            psTr = esA1.enter_context(nc.psum_tensor("psTr", [128, 1024], BF16))
            psQK = [psQK0, psQK1, psQK2]

            def emit_wcast(qq):
                wi, quarter = divmod(qq, 4)
                w_h, Wb = [(wq_h, Wqb), (wk_h, Wkb), (wv_h, Wvb)][wi]
                wsl = wstage[:, (qq % 2) * 2048:(qq % 2 + 1) * 2048]
                pk = (("wcast", (qq - 2) // 4, (qq - 2) % 4)
                      if qq >= 2 else None)

                @sec_sq
                def _(sy, wi=wi, w_h=w_h, quarter=quarter, wsl=wsl, pk=pk):
                    toks[("wdma", wi, quarter)] = SQ.emit(
                        "dma_start", out=wsl,
                        in_=w_h[256 * quarter:256 * (quarter + 1), :]
                        .rearrange("(c p) d -> p c d", p=128),
                        waits=([toks[pk]] if pk else ()))

                @sec_s
                def _(se, wi=wi, Wb=Wb, quarter=quarter, wsl=wsl):
                    toks[("wcast", wi, quarter)] = SE.emit(
                        "activation",
                        out=Wb[:, (2 * quarter) * D:(2 * quarter + 2) * D],
                        in_=wsl, func=AF.Identity,
                        waits=[toks[("wdma", wi, quarter)]])
            @sec_g
            def _(g):
                ap = vaug[:].rearrange("p (s h w) -> p (s h) w", h=H, w=VW)
                toks["vones"] = GE.emit("memset", ap[:, :, DH:DH + 1], 1.0)

            for sb in range(4):
                for st4 in range(4):
                    st = 4 * sb + st4
                    par = st % 2
                    if st < 4:
                        for qq in range(3 * st, 3 * st + 3):
                            emit_wcast(qq)

                    @sec_gq
                    def _(g, st=st, par=par):
                        w = []
                        if st >= 2:
                            w.append(toks[("ln", st - 2)])
                        toks[("xdma", st)] = GQ.emit(
                            "dma_start",
                            out=xstage[:, par * D:(par + 1) * D],
                            in_=x_h[128 * st:128 * (st + 1), :], waits=w)

                    @sec_v
                    def _(ve, st=st, par=par):
                        xs = xstage[:, par * D:(par + 1) * D]
                        VE.wait(toks[("xdma", st)])
                        VE.emit("bn_stats", bn6[:, 0:6], xs[:, 0:512])
                        VE.emit("bn_stats", bn6[:, 6:12], xs[:, 512:1024])
                        VE.emit("bn_aggr", smalls[:, 2:4], bn6[:, 0:12])
                        toks[("veps", st)] = VE.emit(
                            "tensor_scalar_add", smalls[:, 4:5],
                            smalls[:, 3:4], 1e-5)

                    @sec_s
                    def _(se, st=st):
                        SE.wait(toks[("veps", st)])
                        SE.emit("activation", out=smalls[:, 5:6],
                                in_=smalls[:, 4:5], func=AF.Ln)
                        toks[("rstd", st)] = SE.emit(
                            "activation", out=smalls[:, 0:1],
                            in_=smalls[:, 5:6], func=AF.Exp, scale=-0.5)

                    @sec_v
                    def _(ve, st=st, par=par):
                        VE.wait(toks[("rstd", st)])
                        VE.emit("scalar_tensor_tensor", smalls[:, 1:2],
                                smalls[:, 2:3], -1.0, smalls[:, 0:1],
                                ALU.mult, ALU.mult)
                        xs = xstage[:, par * D:(par + 1) * D]
                        ls = lnstage[:, par * D:(par + 1) * D]
                        toks[("ln", st)] = VE.emit(
                            "tensor_scalar", ls, xs, smalls[:, 0:1],
                            smalls[:, 1:2], ALU.mult, ALU.add)

                    @sec_t
                    def _(te, st=st, par=par):
                        ls = lnstage[:, par * D:(par + 1) * D]
                        TE.wait(toks[("ln", st)])
                        TE.wait(toks["consts"])
                        if st >= 1:
                            TE.wait(toks[("lnT", st - 1)])
                        for c in range(8):
                            TE.emit("transpose",
                                    psTr[:, 128 * c:128 * (c + 1)],
                                    ls[:, 128 * c:128 * (c + 1)], idbf[:])
                        toks[("lntr", st)] = TE.tok()

                    @sec_v
                    def _(ve, st=st, st4=st4):
                        VE.wait(toks[("lntr", st)])
                        src = psTr[:].rearrange("p (c w) -> p c w", w=128)
                        dst = ln1T[:].rearrange("p (c w) -> p c w", w=512)
                        toks[("lnT", st)] = VE.emit(
                            "tensor_copy",
                            dst[:, :, 128 * st4:128 * (st4 + 1)], src[:])

                bank_last = {}

                def qkv_mm(Wb, outbuf, sb, proj):
                    for mt in range(8):
                        bidx = (proj * 8 + mt) % 3
                        b = psQK[bidx]

                        @sec_t
                        def _(te, Wb=Wb, mt=mt, b=b, sb=sb, proj=proj,
                              lk=bank_last.get(bidx)):
                            TE.wait(toks[("lnT", 4 * sb + 3)])
                            TE.wait(toks[("wcast", proj, 3)])
                            if lk is not None:
                                TE.wait(toks[lk])
                            for kc in range(8):
                                TE.emit(
                                    "matmul", b[:],
                                    Wb[:, kc * D + 128 * mt:kc * D + 128 * (mt + 1)],
                                    ln1T[:, 512 * kc:512 * (kc + 1)],
                                    start=(kc == 0), stop=(kc == 7))
                            toks[("qkv_mm", sb, proj, mt)] = TE.tok()

                        @sec_v
                        def _(ve, outbuf=outbuf, mt=mt, b=b, sb=sb,
                              proj=proj):
                            VE.wait(toks[("qkv_mm", sb, proj, mt)])
                            toks[("qkv_ev", sb, proj, mt)] = VE.emit(
                                "tensor_copy",
                                outbuf[:, mt * S + 512 * sb:mt * S + 512 * (sb + 1)],
                                b[:])
                        bank_last[bidx] = ("qkv_ev", sb, proj, mt)

                qkv_mm(Wqb, qT, sb, 0)
                qkv_mm(Wkb, kT, sb, 1)
                for ms in range(4):
                    for n in range(2):
                        j = ms * 2 + n
                        bidx = (16 + j) % 3
                        b = psQK[bidx]

                        @sec_t
                        def _(te, ms=ms, n=n, b=b, sb=sb, j=j,
                              lk=bank_last.get(bidx)):
                            TE.wait(toks[("lnT", 4 * sb + 3)])
                            TE.wait(toks[("wcast", 2, 3)])
                            if lk is not None:
                                TE.wait(toks[lk])
                            for kc in range(8):
                                TE.emit(
                                    "matmul", b[:],
                                    ln1T[:, 512 * kc + 128 * ms:512 * kc + 128 * (ms + 1)],
                                    Wvb[:, kc * D + 512 * n:kc * D + 512 * (n + 1)],
                                    start=(kc == 0), stop=(kc == 7))
                            toks[("qkv_mm", sb, 2, j)] = TE.tok()

                        @sec_v
                        def _(ve, ms=ms, n=n, b=b, sb=sb, j=j):
                            st = 4 * sb + ms
                            VE.wait(toks[("qkv_mm", sb, 2, j)])
                            VE.wait(toks["vones"])
                            dst = vaug[:, st * H * VW + 8 * n * VW:
                                       st * H * VW + (8 * n + 8) * VW]
                            dst = dst.rearrange("p (h w) -> p h w", w=VW)
                            src = b[:].rearrange("p (h w) -> p h w", w=DH)
                            toks[("qkv_ev", sb, 2, j)] = VE.emit(
                                "tensor_copy", dst[:, :, 0:DH], src[:])
                        bank_last[bidx] = ("qkv_ev", sb, 2, j)

            flush_phase()
            toks["qkv_done_v"] = VE.tok()

        esA2 = ExitStack()
        with esA2:
            PT = esA2.enter_context(nc.sbuf_tensor("PT", [128, 4 * 16 * 512], BF16))
            PTW = 16 * 512
            x1st = esA2.enter_context(nc.sbuf_tensor("x1st", [128, 4 * D], F32))
            sqb = esA2.enter_context(nc.sbuf_tensor("sqb", [128, 2048], BF16))
            wbuf = esA2.enter_context(nc.sbuf_tensor("wbuf", [128, 2 * 2048], F32))
            t1s = esA2.enter_context(nc.sbuf_tensor("t1s", [128, 2 * 2048], BF16))
            recb = esA2.enter_context(nc.sbuf_tensor("recb", [128, 4], F32))
            psS4 = esA2.enter_context(nc.psum_tensor("psS4", [128, 2048], F32))
            psPV0 = esA2.enter_context(nc.psum_tensor("psPV0", [128, 512], F32))
            psPV1 = esA2.enter_context(nc.psum_tensor("psPV1", [128, 512], F32))
            psN = esA2.enter_context(nc.psum_tensor("psN", [128, 512], F32))
            psS = [psS4[:, 0:1024], psS4[:, 1024:2048]]
            psPV = [psPV0, psPV1]

            for wix, buf in ((0, qT), (1, kT)):
                for n in range(4):
                    def sq_half(wix, n, hf, buf):
                        @sec_v
                        def _(ve, buf=buf, wix=wix, n=n, hf=hf):
                            VE.wait(toks["qkv_done_v"])
                            if hf == 1:
                                VE.wait(toks[("nrm_mm_h", wix, n)])
                            if (wix, n) != (0, 0):
                                pw = (wix, n - 1) if n else (wix - 1, 3)
                                VE.wait(toks[("nrm_mm",) + pw])
                            src = buf[:].rearrange(
                                "p (c s) -> p c s",
                                s=S)[:, 4 * hf:4 * hf + 4,
                                     512 * n:512 * (n + 1)]
                            dst = sqb[:].rearrange("p (c s) -> p c s",
                                                   s=512)
                            toks[("sq", wix, n, hf)] = VE.emit(
                                "tensor_tensor", dst, src, src, ALU.mult)

                    sq_half(wix, n, 0, buf)

                    @sec_t
                    def _(te, wix=wix, n=n):
                        TE.wait(toks["consts"])
                        if (wix, n) != (0, 0):
                            pw = (wix, n - 1) if n else (wix - 1, 3)
                            TE.wait(toks[("nrm_rd",) + pw])
                        for c in range(4):
                            TE.wait(toks[("sq", wix, n, 0)])
                            TE.emit("matmul", psN[0:1, 0:512],
                                    onesb[:, 0:1],
                                    sqb[:, 512 * c:512 * (c + 1)],
                                    start=(c == 0), stop=False)
                        toks[("nrm_mm_h", wix, n)] = TE.tok()

                    sq_half(wix, n, 1, buf)

                    @sec_t
                    def _(te, wix=wix, n=n):
                        for c in range(4):
                            TE.wait(toks[("sq", wix, n, 1)])
                            TE.emit("matmul", psN[0:1, 0:512],
                                    onesb[:, 0:1],
                                    sqb[:, 512 * c:512 * (c + 1)],
                                    start=False, stop=(c == 3))
                        toks[("nrm_mm", wix, n)] = TE.tok()

                    @sec_v
                    def _(ve, wix=wix, n=n):
                        VE.wait(toks[("nrm_mm", wix, n)])
                        col = 8 * wix + n
                        toks[("nrm_rd", wix, n)] = VE.emit(
                            "tensor_reduce", nrm[0:1, col:col + 1],
                            psN[0:1, 0:512], AX.X, ALU.max)

            @sec_v
            def _(ve):
                VE.emit("tensor_reduce", nrm[0:1, 16:17], nrm[0:1, 0:4],
                        AX.X, ALU.max)
                VE.emit("tensor_reduce", nrm[0:1, 17:18], nrm[0:1, 8:12],
                        AX.X, ALU.max)
                toks["c2"] = VE.emit(
                    "tensor_tensor", nrm[0:1, 18:19], nrm[0:1, 16:17],
                    nrm[0:1, 17:18], ALU.mult)

            @sec_s
            def _(se):
                SE.wait(toks["c2"])
                SE.emit("activation", out=nrm[0:1, 19:20],
                        in_=nrm[0:1, 18:19], func=AF.Ln)
                toks["cval"] = SE.emit(
                    "activation", out=nrm[0:1, 20:21],
                    in_=nrm[0:1, 19:20], func=AF.Exp, scale=0.5)

            @sec_v
            def _(ve):
                VE.wait(toks["cval"])
                VE.emit("tensor_scalar", nrm[0:1, 21:22], nrm[0:1, 20:21],
                        0.125, 80.0, ALU.mult, ALU.min)
                toks["negb"] = VE.emit(
                    "tensor_scalar_mul", nrm[0:1, 22:23], nrm[0:1, 21:22],
                    -1.0)

            @sec_t
            def _(te):
                TE.wait(toks["negb"])
                TE.emit("matmul", psN[:, 0:1], onesrow[0:1, :],
                        nrm[0:1, 22:23], start=True, stop=True)
                toks["negb_mm"] = TE.tok()

            @sec_v
            def _(ve):
                VE.wait(toks["negb_mm"])
                toks["expbias"] = VE.emit(
                    "tensor_copy", smalls[:, 6:7], psN[:, 0:1])

            if debug:
                @sec_v
                def _(ve):
                    toks["dbg_nb"] = VE.emit("tensor_copy", nrm[0:1, 32:39],
                                             nrm[0:1, 16:23])

                @sec_sq
                def _(sy):
                    SQ.emit("dma_start", out=dbg["d_nb"][0:1, 0:7],
                            in_=nrm[0:1, 32:39], waits=[toks["dbg_nb"]])

            def mk_w1_abs(i):
                c, hf = divmod(i, 2)

                def f():
                    @sec_sq
                    def _(sy):
                        w = [toks["qkv_done_v"]]
                        if i >= 2:
                            w.append(toks[("w1_abs", i - 2)])
                        toks[("w1_dma", i)] = SQ.emit(
                            "dma_start",
                            out=wbuf[:, (i % 2) * 2048:(i % 2 + 1) * 2048],
                            in_=w1t_h[128 * c:128 * (c + 1),
                                      2048 * hf:2048 * (hf + 1)], waits=w)

                    @sec_s
                    def _(se):
                        SE.wait(toks[("w1_dma", i)])
                        wsl = wbuf[:, (i % 2) * 2048:(i % 2 + 1) * 2048]
                        toks[("w1_abs", i)] = SE.emit(
                            "activation", out=wsl, in_=wsl,
                            func=AF.Abs, accum_out=smalls[:, 32 + i:33 + i])
                return f

            def mk_w2_abs(i):
                def f():
                    @sec_sq
                    def _(sy):
                        w = [toks["qkv_done_v"]]
                        if i >= 2:
                            w.append(toks[("w2_abs", i - 2)])
                        else:
                            w.append(toks[("w1_abs", 14 + i)])
                        toks[("w2_dma", i)] = SQ.emit(
                            "dma_start",
                            out=wbuf[:, (i % 2) * 2048:(i % 2 + 1) * 2048]
                            .rearrange("p (c d) -> p c d", d=D),
                            in_=w2t_h[256 * i:256 * (i + 1), :]
                            .rearrange("(c p) d -> p c d", p=128), waits=w)

                    @sec_s
                    def _(se):
                        SE.wait(toks[("w2_dma", i)])
                        wsl = wbuf[:, (i % 2) * 2048:(i % 2 + 1) * 2048]
                        toks[("w2_abs", i)] = SE.emit(
                            "activation", out=wsl, in_=wsl,
                            func=AF.Abs, accum_out=smalls[:, 48 + i:49 + i])
                return f

            def mk_finalize(which):
                base = 32 if which == 1 else 48
                scol = 8 if which == 1 else 10
                icol = 9 if which == 1 else 11
                nelem = float(D * HID)

                def f():
                    @sec_v
                    def _(ve):
                        toks[("wfin_red", which)] = VE.emit(
                            "tensor_reduce", smalls[:, 30:31],
                            smalls[:, base:base + 16], AX.X, ALU.add,
                            waits=[toks[("w1_abs" if which == 1 else
                                         "w2_abs", 15)]])

                    @sec_t
                    def _(te):
                        TE.wait(toks[("wfin_red", which)])
                        if which == 2:
                            TE.wait(toks[("wsc", 1)])
                        TE.emit("matmul", psN[0:1, 8:9], onesf[:, 0:1],
                                smalls[:, 30:31], start=True, stop=True)
                        toks[("wfin_mm", which)] = TE.tok()

                    @sec_v
                    def _(ve):
                        VE.wait(toks[("wfin_mm", which)])
                        VE.emit("tensor_scalar", nrm[0:1, 40:41],
                                psN[0:1, 8:9], 1.0 / nelem, 1e-5,
                                ALU.mult, ALU.max)
                        toks[("wfin_sc", which)] = VE.emit(
                            "reciprocal", nrm[0:1, 41:42], nrm[0:1, 40:41])

                    @sec_t
                    def _(te):
                        TE.wait(toks[("wfin_sc", which)])
                        TE.emit("matmul", psN[:, 9:10], onesrow[0:1, :],
                                nrm[0:1, 41:42], start=True, stop=True)
                        TE.emit("matmul", psN[:, 10:11], onesrow[0:1, :],
                                nrm[0:1, 40:41], start=True, stop=True)
                        toks[("wfin_bc", which)] = TE.tok()

                    @sec_v
                    def _(ve):
                        VE.wait(toks[("wfin_bc", which)])
                        VE.emit("tensor_copy", smalls[:, scol:scol + 1],
                                psN[:, 9:10])
                        toks[("wsc", which)] = VE.emit(
                            "tensor_copy", smalls[:, icol:icol + 1],
                            psN[:, 10:11])
                return f

            def mk_w1_quant(i):
                c, hf = divmod(i, 2)

                def f():
                    @sec_sq
                    def _(sy):
                        w = []
                        if i >= 2:
                            w.append(toks[("w1q_t", i - 2)])
                        else:
                            w.append(toks[("w2_abs", 14 + i)])
                        toks[("w1q_dma", i)] = SQ.emit(
                            "dma_start",
                            out=wbuf[:, (i % 2) * 2048:(i % 2 + 1) * 2048],
                            in_=w1t_h[128 * c:128 * (c + 1),
                                      2048 * hf:2048 * (hf + 1)], waits=w)

                    @sec_v
                    def _(ve):
                        VE.wait(toks[("w1q_dma", i)])
                        VE.wait(toks[("wsc", 1)])
                        if i >= 2:
                            VE.wait(toks[("w1q_out", i - 2)])
                        wsl = wbuf[:, (i % 2) * 2048:(i % 2 + 1) * 2048]
                        tsl = t1s[:, (i % 2) * 2048:(i % 2 + 1) * 2048]
                        VE.emit("tensor_scalar", tsl, wsl,
                                smalls[:, 8:9], 0.5, ALU.mult, ALU.is_ge)
                        VE.emit("tensor_scalar", wsl, wsl,
                                smalls[:, 8:9], -0.5, ALU.mult, ALU.is_le)
                        toks[("w1q_t", i)] = VE.emit(
                            "tensor_tensor", tsl, tsl, wsl, ALU.subtract)

                    @sec_sq
                    def _(sy):
                        toks[("w1q_out", i)] = SQ.emit(
                            "dma_start",
                            out=t1scr[128 * c:128 * (c + 1),
                                      2048 * hf:2048 * (hf + 1)],
                            in_=t1s[:, (i % 2) * 2048:(i % 2 + 1) * 2048],
                            waits=[toks[("w1q_t", i)]])
                return f

            def mk_w2_quant(i):
                def f():
                    @sec_sq
                    def _(sy):
                        w = []
                        if i >= 2:
                            w.append(toks[("w2q_t", i - 2)])
                        else:
                            w.append(toks[("w1q_t", 14 + i)])
                        toks[("w2q_dma", i)] = SQ.emit(
                            "dma_start",
                            out=wbuf[:, (i % 2) * 2048:(i % 2 + 1) * 2048]
                            .rearrange("p (c d) -> p c d", d=D),
                            in_=w2t_h[256 * i:256 * (i + 1), :]
                            .rearrange("(c p) d -> p c d", p=128), waits=w)

                    @sec_v
                    def _(ve):
                        VE.wait(toks[("w2q_dma", i)])
                        VE.wait(toks[("wsc", 2)])
                        if i >= 2:
                            VE.wait(toks[("w2q_out", i - 2)])
                        else:
                            VE.wait(toks[("w1q_out", 14 + i)])
                        wsl = wbuf[:, (i % 2) * 2048:(i % 2 + 1) * 2048]
                        tsl = t1s[:, (i % 2) * 2048:(i % 2 + 1) * 2048]
                        VE.emit("tensor_scalar", tsl, wsl,
                                smalls[:, 10:11], 0.5, ALU.mult, ALU.is_ge)
                        VE.emit("tensor_scalar", wsl, wsl,
                                smalls[:, 10:11], -0.5, ALU.mult, ALU.is_le)
                        toks[("w2q_t", i)] = VE.emit(
                            "tensor_tensor", tsl, tsl, wsl, ALU.subtract)

                    @sec_sq
                    def _(sy):
                        toks[("w2q_out", i)] = SQ.emit(
                            "dma_start",
                            out=t2scr[256 * i:256 * (i + 1), :]
                            .rearrange("(c p) d -> p c d", p=128),
                            in_=t1s[:, (i % 2) * 2048:(i % 2 + 1) * 2048]
                            .rearrange("p (c d) -> p c d", d=D),
                            waits=[toks[("w2q_t", i)]])
                return f

            wsteps = ([mk_w1_abs(i) for i in range(16)] + [mk_finalize(1)]
                      + [mk_w2_abs(i) for i in range(16)] + [mk_finalize(2)]
                      + [mk_w1_quant(i) for i in range(16)]
                      + [mk_w2_quant(i) for i in range(16)])
            wstep_i = [0]

            def emit_wsteps(k):
                while k > 0 and wstep_i[0] < len(wsteps):
                    wsteps[wstep_i[0]]()
                    wstep_i[0] += 1
                    k -= 1

            pend = [None]

            def flush_pv():
                if pend[0] is None:
                    return
                t, hp, PTp_A, PTp_B = pend[0]
                pend[0] = None
                hA, hB = 2 * hp, 2 * hp + 1
                for hi, (head, P) in enumerate(((hA, PTp_A), (hB, PTp_B))):
                    pv = psPV[hi]

                    @sec_t
                    def _(te, t=t, hp=hp, hi=hi, head=head, P=P, pv=pv):
                        if hp >= 1:
                            TE.wait(toks[("epi", t, hp - 1, hi)])
                        elif t >= 1:
                            TE.wait(toks[("epi", t - 1, 7, hi)])
                        TE.wait(toks[("exp", t, hp, 4 * t + 3)])
                        TE.wait(toks[("mask", t, hp)])
                        first = True
                        for m in range(4):
                            for kc in range(4 * t + m + 1):
                                last = (m == 3) and (kc == 4 * t + 3)
                                TE.emit(
                                    "matmul", pv[:, VW * m:VW * m + VW],
                                    P[:, 512 * kc + 128 * m:512 * kc + 128 * (m + 1)],
                                    vaug[:, kc * H * VW + head * VW:
                                         kc * H * VW + (head + 1) * VW],
                                    start=first, stop=last,
                                    skip_group_check=True)
                                first = False
                        toks[("pv", t, hp, hi)] = TE.tok()

                    @sec_v
                    def _(ve, t=t, hp=hp, hi=hi, head=head, pv=pv):
                        VE.wait(toks[("pv", t, hp, hi)])
                        VE.wait(toks[("x1_load", t)])
                        rs = pv[:, 0:4 * VW].rearrange(
                            "p (m w) -> p m w", w=VW)[:, :, DH:DH + 1]
                        VE.emit("reciprocal", recb[:, 0:4], rs)
                        for m in range(4):
                            xsl = x1st[:, m * D + head * DH:
                                       m * D + (head + 1) * DH]
                            VE.emit("scalar_tensor_tensor", xsl,
                                    pv[:, VW * m:VW * m + DH],
                                    recb[:, m:m + 1], xsl,
                                    ALU.mult, ALU.add)
                        toks[("epi", t, hp, hi)] = VE.tok()

                if hp == 7:
                    @sec_gq
                    def _(g, t=t):
                        toks[("x1_store", t)] = GQ.emit(
                            "dma_start",
                            out=x1scr[QTW * t:QTW * (t + 1), :]
                            .rearrange("(c p) d -> p c d", p=128),
                            in_=x1st[:].rearrange("p (c d) -> p c d", d=D),
                            waits=[toks[("epi", t, 7, 0)],
                                   toks[("epi", t, 7, 1)]])

            for t in range(NQT):
                KC = 4 * t + 4
                q0 = QTW * t
                xpar = t % 2

                for hp in range(8):
                    hA, hB = 2 * hp, 2 * hp + 1

                    def emit_sc(t, hp, kc, KC, q0):
                        hpar = kc % 2

                        @sec_t
                        def _(te, t=t, hp=hp, kc=kc, hpar=hpar, KC=KC,
                              q0=q0):
                            if kc >= 2:
                                TE.wait(toks[("exp", t, hp, kc - 2)])
                            elif hp >= 1:
                                TE.wait(toks[("exp", t, hp - 1, KC - 2 + kc)])
                            elif t >= 1:
                                TE.wait(toks[("exp", t - 1, 7, 4 * t - 2 + kc)])
                            TE.wait(toks["qkv_done_v"])
                            bP = psS[hpar]
                            TE.emit("matmul", bP[:, 0:QTW],
                                    kT[0:64, hp * S + 128 * kc:hp * S + 128 * (kc + 1)],
                                    qT[0:64, hp * S + q0:hp * S + q0 + QTW],
                                    start=True, stop=True,
                                    tile_position=(0, 0),
                                    skip_group_check=True)
                            TE.emit("matmul", bP[:, QTW:2 * QTW],
                                    kT[64:128, hp * S + 128 * kc:hp * S + 128 * (kc + 1)],
                                    qT[64:128, hp * S + q0:hp * S + q0 + QTW],
                                    start=True, stop=True,
                                    tile_position=(64, 0),
                                    skip_group_check=True)
                            toks[("sc", t, hp, kc)] = TE.tok()

                    def exp_pvwaits(t, hp, kc):
                        SE.wait(toks["expbias"])
                        if kc == 0:
                            if hp >= 2:
                                SE.wait(toks[("pv", t, hp - 2, 0)])
                                SE.wait(toks[("pv", t, hp - 2, 1)])
                            elif t >= 1:
                                SE.wait(toks[("pv", t - 1, 6 + hp, 0)])
                                SE.wait(toks[("pv", t - 1, 6 + hp, 1)])

                    def emit_exp1(t, hp, kc):
                        hpar = kc % 2

                        @sec_s
                        def _(se, t=t, hp=hp, kc=kc, hpar=hpar):
                            SE.wait(toks[("sc", t, hp, kc)])
                            exp_pvwaits(t, hp, kc)
                            c0 = max(0, (kc - 4 * t) * 128)
                            bP = psS[hpar]
                            src_ap = bP[:].rearrange(
                                "p (h w) -> p h w", w=QTW)[:, :, c0:QTW]
                            PTp = PT[:, (hp % 2) * 2 * PTW:
                                      (hp % 2 + 1) * 2 * PTW]
                            dst_ap = PTp.rearrange(
                                "p (h w) -> p h w", w=16 * 512)[
                                :, :, 512 * kc + c0:512 * (kc + 1)]
                            toks[("exp", t, hp, kc)] = SE.emit(
                                "activation", out=dst_ap, in_=src_ap,
                                func=AF.Exp, scale=0.125,
                                bias=smalls[:, 6:7])

                    def emit_exp2(t, hp, kc):
                        @sec_s
                        def _(se, t=t, hp=hp, kc=kc):
                            SE.wait(toks[("sc", t, hp, kc)])
                            SE.wait(toks[("sc", t, hp, kc + 1)])
                            exp_pvwaits(t, hp, kc)
                            src_ap = psS4[:].rearrange(
                                "p (x h w) -> p h x w", x=2, h=2, w=QTW)
                            PTp = PT[:, (hp % 2) * 2 * PTW:
                                      (hp % 2 + 1) * 2 * PTW]
                            dst_ap = PTp.rearrange(
                                "p (h w) -> p h w", w=16 * 512)[
                                :, :, 512 * kc:512 * (kc + 2)]
                            dst_ap = dst_ap.rearrange(
                                "p h (x w) -> p h x w", w=QTW)
                            tok = SE.emit(
                                "activation", out=dst_ap, in_=src_ap,
                                func=AF.Exp, scale=0.125,
                                bias=smalls[:, 6:7])
                            toks[("exp", t, hp, kc)] = tok
                            toks[("exp", t, hp, kc + 1)] = tok

                    for kc in range(KC):
                        emit_sc(t, hp, kc, KC, q0)
                        if kc == 3:
                            flush_pv()
                        emit_exp1(t, hp, kc)

                    if hp == 0:
                        @sec_gq
                        def _(g, t=t):
                            w = [toks["qkv_done_v"]]
                            if t >= 1:
                                w.append(toks[("x1_store", t - 1)])
                            toks[("x1_load", t)] = GQ.emit(
                                "dma_start",
                                out=x1st[:].rearrange("p (c d) -> p c d",
                                                      d=D),
                                in_=x_h[QTW * t:QTW * (t + 1), :]
                                .rearrange("(c p) d -> p c d", p=128),
                                waits=w)

                    PTp_A = PT[:, (hp % 2) * 2 * PTW:
                               (hp % 2) * 2 * PTW + PTW]
                    PTp_B = PT[:, (hp % 2) * 2 * PTW + PTW:
                               (hp % 2 + 1) * 2 * PTW]

                    @sec_v
                    def _(ve, t=t, hp=hp, PTp_A=PTp_A, PTp_B=PTp_B):
                        for j in range(4):
                            kc = 4 * t + j
                            VE.wait(toks[("exp", t, hp, kc)])
                            for P in (PTp_A, PTp_B):
                                sl = P[:, 512 * kc + 128 * j:512 * kc + 128 * (j + 1)]
                                VE.emit("tensor_tensor", sl, sl, maskt[:],
                                        ALU.mult)
                        toks[("mask", t, hp)] = VE.tok()

                    pend[0] = (t, hp, PTp_A, PTp_B)
                    emit_wsteps(2)

            flush_pv()
            emit_wsteps(len(wsteps))

            if debug:
                @sec_sq
                def _(sy):
                    SQ.emit("dma_start", out=dbg["d_qt"][:], in_=qT[:],
                            waits=[toks["qkv_done_v"]])
                    SQ.emit("dma_start", out=dbg["d_kt"][:], in_=kT[:])
                    SQ.emit("dma_start", out=dbg["d_vaug"][:], in_=vaug[:])
                    SQ.emit("dma_start", out=dbg["d_x1"][:], in_=x1scr[:],
                            waits=[toks[("x1_store", t)] for t in range(4)])
                    SQ.emit("dma_start", out=dbg["d_t1t"][:], in_=t1scr[:],
                            waits=[toks[("w1q_out", i)] for i in range(16)])
                    SQ.emit("dma_start", out=dbg["d_t2t"][:], in_=t2scr[:],
                            waits=[toks[("w2q_out", i)] for i in range(16)])

            flush_phase()
            toks["A_t"] = TE.tok()
            toks["A_v"] = VE.tok()
            toks["A_a"] = SE.tok()
            toks["A_g"] = GE.tok()
            toks["A_sq"] = SQ.toks()
            toks["A_gq"] = GQ.toks()

        esA.close()
        barrier = ([toks["A_t"], toks["A_v"], toks["A_a"], toks["A_g"]]
                   + toks["A_sq"] + toks["A_gq"])

        esB = ExitStack()
        with esB:
            t2T = esB.enter_context(nc.sbuf_tensor("t2T", [128, 32 * D], BF16))
            xqT = esB.enter_context(nc.sbuf_tensor("xqT", [128, 8 * (S + 2)], BF16))
            hsil = esB.enter_context(nc.sbuf_tensor("hsil", [128, 32 * GRP], F32))
            hq = esB.enter_context(nc.sbuf_tensor("hq", [128, 32 * GRP], BF16))
            t1b = esB.enter_context(nc.sbuf_tensor("t1b", [128, 2 * 8 * 512], BF16))
            x1t = esB.enter_context(nc.sbuf_tensor("x1t", [128, 2 * D], F32))
            x1g = esB.enter_context(nc.sbuf_tensor("x1g", [128, 2 * 2 * D], F32))
            hf16 = esB.enter_context(nc.sbuf_tensor("hf16", [128, 2048], F16))
            xqf16 = hf16[:, 0:D]
            xqs = esB.enter_context(nc.sbuf_tensor("xqs", [128, 2 * D], BF16))
            s2row = esB.enter_context(nc.sbuf_tensor("s2row", [128, GRP], F32))
            tmpc = esB.enter_context(nc.sbuf_tensor("tmpc", [128, 2 * GRP], F32))
            cbuf = esB.enter_context(nc.sbuf_tensor("cbuf", [128, 8 * GRP], F32))
            s2st = esB.enter_context(nc.sbuf_tensor("s2st", [1, 256], F32))
            amx = esB.enter_context(nc.sbuf_tensor("amx", [128, GRP], F32))
            amc = esB.enter_context(nc.sbuf_tensor("amc", [128, 8], F32))
            bn6b = esB.enter_context(nc.sbuf_tensor("bn6b", [128, 12], F32))
            psM0 = esB.enter_context(nc.psum_tensor("psM0", [128, 512], F32))
            psM1 = esB.enter_context(nc.psum_tensor("psM1", [128, 512], F32))
            psM2 = esB.enter_context(nc.psum_tensor("psM2", [128, 512], F32))
            psM3 = esB.enter_context(nc.psum_tensor("psM3", [128, 512], F32))
            psO0 = esB.enter_context(nc.psum_tensor("psO0", [128, 512], F32))
            psO1 = esB.enter_context(nc.psum_tensor("psO1", [128, 512], F32))
            psNb = esB.enter_context(nc.psum_tensor("psNb", [128, 512], F32))
            psTb = esB.enter_context(nc.psum_tensor("psTb", [128, 1024], BF16))
            psM = [psM0, psM1, psM2, psM3]
            psO = [psO0, psO1]

            @sec_sq
            def _(sy):
                for q4 in range(4):
                    toks[("t2T", q4)] = SQ.emit(
                        "dma_start",
                        out=t2T[:, q4 * 8 * D:(q4 + 1) * 8 * D]
                        .rearrange("p (c d) -> p c d", d=D),
                        in_=t2scr[1024 * q4:1024 * (q4 + 1), :]
                        .rearrange("(c p) d -> p c d", p=128),
                        waits=barrier + [toks[("w2q_out", i)]
                                         for i in range(16)])

            @sec_g
            def _(g):
                ap = xqT[:].rearrange("p (c s) -> p c s", s=S + 2)
                GE.emit("memset", ap[:, :, 0:1], 0.0, waits=barrier)
                toks["xq_guard"] = GE.emit("memset", ap[:, :, S + 1:S + 2],
                                           0.0)

            def emit_xq_tile(st):
                par = st % 2

                @sec_gq
                def _(g, st=st, par=par):
                    w = list(barrier)
                    if st >= 2:
                        w.append(toks[("xq_done", st - 2)])
                    toks[("x1t_dma", st)] = GQ.emit(
                        "dma_start", out=x1t[:, par * D:(par + 1) * D],
                        in_=x1scr[128 * st:128 * (st + 1), :], waits=w)

                @sec_v
                def _(ve, st=st, par=par):
                    xs = x1t[:, par * D:(par + 1) * D]
                    VE.wait(toks[("x1t_dma", st)])
                    for b in barrier:
                        VE.wait(b)
                    VE.emit("bn_stats", bn6b[:, 0:6], xs[:, 0:512])
                    VE.emit("bn_stats", bn6b[:, 6:12], xs[:, 512:1024])
                    VE.emit("bn_aggr", smalls[:, 2:4], bn6b[:, 0:12])
                    VE.emit("tensor_reduce", smalls[:, 16:17], xs, AX.X,
                            ALU.max)
                    VE.emit("tensor_reduce", smalls[:, 17:18], xs, AX.X,
                            ALU.max, negate=True)
                    toks[("veps2", st)] = VE.emit(
                        "tensor_scalar_add", smalls[:, 4:5], smalls[:, 3:4],
                        1e-5)

                @sec_s
                def _(se, st=st):
                    SE.wait(toks[("veps2", st)])
                    for b in barrier:
                        SE.wait(b)
                    SE.emit("activation", out=smalls[:, 5:6],
                            in_=smalls[:, 4:5], func=AF.Ln)
                    toks[("rstd2", st)] = SE.emit(
                        "activation", out=smalls[:, 0:1], in_=smalls[:, 5:6],
                        func=AF.Exp, scale=-0.5)

                @sec_v
                def _(ve, st=st, par=par):
                    VE.wait(toks[("rstd2", st)])
                    VE.emit("tensor_tensor", smalls[:, 18:19],
                            smalls[:, 16:17], smalls[:, 2:3], ALU.subtract)
                    VE.emit("tensor_tensor", smalls[:, 19:20],
                            smalls[:, 17:18], smalls[:, 2:3], ALU.add)
                    VE.emit("tensor_tensor", smalls[:, 20:21],
                            smalls[:, 18:19], smalls[:, 19:20], ALU.max)
                    VE.emit("tensor_tensor", smalls[:, 21:22],
                            smalls[:, 20:21], smalls[:, 0:1], ALU.mult)
                    VE.emit("tensor_scalar_max", smalls[:, 22:23],
                            smalls[:, 21:22], 1e-5)
                    VE.emit("reciprocal", smalls[:, 23:24], smalls[:, 22:23])
                    VE.emit("tensor_scalar_mul", smalls[:, 24:25],
                            smalls[:, 23:24], 127.0)
                    VE.emit("tensor_scalar_mul", smalls[:, 25:26],
                            smalls[:, 22:23], 1.0 / 127.0)
                    VE.emit("tensor_tensor", smalls[:, 26:27],
                            smalls[:, 25:26], smalls[:, 9:10], ALU.mult)
                    VE.emit("tensor_tensor", smalls[:, 12:13],
                            smalls[:, 0:1], smalls[:, 24:25], ALU.mult)
                    VE.emit("scalar_tensor_tensor", smalls[:, 13:14],
                            smalls[:, 2:3], -1.0, smalls[:, 12:13],
                            ALU.mult, ALU.mult)
                    VE.emit("tensor_scalar_add", smalls[:, 13:14],
                            smalls[:, 13:14], 1536.0)
                    toks[("xqpre", st)] = VE.emit(
                        "tensor_scalar_mul", smalls[:, 27:28],
                        smalls[:, 26:27], -1536.0)

                @sec_s
                def _(se, st=st, par=par):
                    SE.wait(toks[("xqpre", st)])
                    if st >= 2:
                        SE.wait(toks[("xqtr", st - 2)])
                    SE.emit("activation", out=xqf16,
                            in_=x1t[:, par * D:(par + 1) * D],
                            func=AF.Identity, scale=smalls[:, 12:13],
                            bias=smalls[:, 13:14])
                    toks[("xqs", st)] = SE.emit(
                        "activation", out=xqs[:, par * D:(par + 1) * D],
                        in_=xqf16, func=AF.Identity,
                        scale=smalls[:, 26:27], bias=smalls[:, 27:28])

            def emit_xq_back(st):
                par = st % 2

                @sec_t
                def _(te, st=st, par=par):
                    TE.wait(toks[("xqs", st)])
                    for b in barrier:
                        TE.wait(b)
                    if st >= 1:
                        TE.wait(toks[("xq_done", st - 1)])
                    for c in range(8):
                        TE.emit("transpose", psTb[:, 128 * c:128 * (c + 1)],
                                xqs[:, par * D + 128 * c:par * D + 128 * (c + 1)],
                                idbf[:])
                    toks[("xqtr", st)] = TE.tok()

                @sec_v
                def _(ve, st=st):
                    VE.wait(toks[("xqtr", st)])
                    src = psTb[:].rearrange("p (c w) -> p c w", w=128)
                    dst = xqT[:].rearrange("p (c s) -> p c s", s=S + 2)
                    toks[("xq_done", st)] = VE.emit(
                        "tensor_copy",
                        dst[:, :, 1 + 128 * st:1 + 128 * (st + 1)], src[:])

            def emit_group_A(g):
                gpar = g % 2
                qg = g - 1

                @sec_gq
                def _(gp, g=g, gpar=gpar):
                    w = list(barrier)
                    if g >= 2:
                        w.append(toks[("out_store", g - 2)])
                    toks[("x1g_dma", g)] = GQ.emit(
                        "dma_start",
                        out=x1g[:, gpar * 2 * D:(gpar + 1) * 2 * D]
                        .rearrange("p (c d) -> p c d", d=D),
                        in_=x1scr[GRP * g:GRP * (g + 1), :]
                        .rearrange("(c p) d -> p c d", p=128), waits=w)

                for e in range(8):
                    bpar = (g * 8 + e) % 2
                    if qg >= 0 and e < 2:
                        emit_hmul_ve(qg, e)

                    @sec_sq
                    def _(sy, g=g, e=e, bpar=bpar):
                        w = list(barrier)
                        w += [toks[("w1q_out", i)] for i in range(16)]
                        if g * 8 + e >= 2:
                            w.append(toks[("t1_used", g * 8 + e - 2)])
                        toks[("t1_dma", g, e)] = SQ.emit(
                            "dma_start",
                            out=t1b[:, bpar * 8 * 512:(bpar + 1) * 8 * 512]
                            .rearrange("p (c h) -> p c h", h=512),
                            in_=t1scr[:, 512 * e:512 * (e + 1)]
                            .rearrange("(c p) h -> p c h", p=128), waits=w)

                    for ht in range(4):
                        hh = 4 * e + ht
                        bank = psM[hh % 4]
                        cpar = hh % 2

                        @sec_t
                        def _(te, g=g, e=e, ht=ht, hh=hh, bank=bank,
                              bpar=bpar):
                            TE.wait(toks[("t1_dma", g, e)])
                            TE.wait(toks[("xq_done", min(2 * g + 2, NT - 1))])
                            TE.wait(toks["xq_guard"])
                            if hh >= 4:
                                TE.wait(toks[("conv", g, hh - 4)])
                            elif g >= 1:
                                TE.wait(toks[("conv", g - 1, hh + 28)])
                            for kc in range(8):
                                TE.emit(
                                    "matmul", bank[:, 0:GRP + 2],
                                    t1b[:, bpar * 8 * 512 + 512 * kc + 128 * ht:
                                        bpar * 8 * 512 + 512 * kc + 128 * (ht + 1)],
                                    xqT[:, (S + 2) * kc + GRP * g:
                                        (S + 2) * kc + GRP * g + GRP + 2],
                                    start=(kc == 0), stop=(kc == 7))
                            toks[("mm1", g, hh)] = TE.tok()
                            if ht == 3:
                                toks[("t1_used", g * 8 + e)] = \
                                    toks[("mm1", g, hh)]

                        @sec_s
                        def _(se, g=g, hh=hh, bank=bank, cpar=cpar):
                            SE.wait(toks[("mm1", g, hh)])
                            SE.wait(toks["consts"])
                            if hh >= 2:
                                SE.wait(toks[("conv", g, hh - 2)])
                            elif g >= 1:
                                SE.wait(toks[("conv", g - 1, hh + 30)])
                            t0 = tmpc[:, cpar * GRP:(cpar + 1) * GRP]
                            toks[("cv1", g, hh)] = SE.emit(
                                "activation", out=t0, in_=bank[:, 0:GRP],
                                func=AF.Identity,
                                scale=convw[:, 3 * hh:3 * hh + 1],
                                bias=convb[:, hh:hh + 1])

                        @sec_v
                        def _(ve, g=g, hh=hh, bank=bank, cpar=cpar):
                            VE.wait(toks[("cv1", g, hh)])
                            qlast = hh - 8 - (hh % 4) + 3
                            if qlast >= 0:
                                VE.wait(toks[("sgm", g, qlast)])
                                VE.wait(toks[("silu", g, qlast)])
                            elif g >= 1:
                                VE.wait(toks[("sgm", g - 1, qlast + 32)])
                                VE.wait(toks[("silu", g - 1, qlast + 32)])
                            t0 = tmpc[:, cpar * GRP:(cpar + 1) * GRP]
                            qpar = (hh // 4) % 2
                            co = cbuf[:, (qpar * 4 + hh % 4) * GRP:
                                      (qpar * 4 + hh % 4 + 1) * GRP]
                            VE.emit("scalar_tensor_tensor", t0,
                                    bank[:, 1:GRP + 1],
                                    convw[:, 3 * hh + 1:3 * hh + 2], t0,
                                    ALU.mult, ALU.add)
                            toks[("conv", g, hh)] = VE.emit(
                                "scalar_tensor_tensor", co,
                                bank[:, 2:GRP + 2],
                                convw[:, 3 * hh + 2:3 * hh + 3], t0,
                                ALU.mult, ALU.add)

                        if hh % 4 == 3:
                            if e == 2 and 2 * g + 5 < NT:
                                emit_xq_tile(2 * g + 5)
                            if e == 5 and 2 * g + 6 < NT:
                                emit_xq_tile(2 * g + 6)
                            if qg >= 0 and e % 2 == 0:
                                emit_shift(qg, e // 2)

                            @sec_s
                            def _(se, g=g, hh=hh):
                                SE.wait(toks[("conv", g, hh)])
                                if g >= 1:
                                    SE.wait(toks[("hqt", g - 1,
                                                  (hh - 3) // 8)])
                                qpar = (hh // 4) % 2
                                cq = cbuf[:, qpar * 4 * GRP:
                                          (qpar + 1) * 4 * GRP]
                                toks[("sgm", g, hh)] = SE.emit(
                                    "activation",
                                    out=hsil[:, GRP * (hh - 3):GRP * (hh + 1)],
                                    in_=cq, func=AF.Sigmoid)

                            @sec_g
                            def _(ge, g=g, hh=hh):
                                GE.wait(toks[("sgm", g, hh)])
                                qpar = (hh // 4) % 2
                                cq = cbuf[:, qpar * 4 * GRP:
                                          (qpar + 1) * 4 * GRP]
                                hsl = hsil[:, GRP * (hh - 3):GRP * (hh + 1)]
                                toks[("silu", g, hh)] = GE.emit(
                                    "tensor_tensor", hsl, cq, hsl, ALU.mult)



            def emit_Bamax(g):
                @sec_v
                def _(ve, g=g):
                    for hh in (3, 7, 11, 15, 19, 23, 27, 31):
                        VE.wait(toks[("silu", g, hh)])
                    src = hsil[:].rearrange("p (c s) -> p s c", c=32)
                    toks[("amaxp", g)] = VE.emit(
                        "tensor_reduce", amx[:], src, AX.X, ALU.max,
                        apply_absolute_value=True)

            def emit_Brest(g):
                @sec_t
                def _(te, g=g):
                    TE.wait(toks[("amaxp", g)])
                    if g >= 1:
                        TE.wait(toks[("s2row", g - 1)])
                    TE.emit("transpose", psNb[0:128, 0:128], amx[:, 0:128],
                            idf[:])
                    TE.emit("transpose", psNb[0:128, 128:256],
                            amx[:, 128:256], idf[:])
                    toks[("amtr", g)] = TE.tok()

                @sec_v
                def _(ve, g=g):
                    ab = (g % 2) * 4
                    VE.wait(toks[("amtr", g)])
                    VE.emit("tensor_reduce", amc[:, ab:ab + 1],
                            psNb[:, 0:128], AX.X, ALU.max)
                    VE.emit("tensor_reduce", amc[:, ab + 1:ab + 2],
                            psNb[:, 128:256], AX.X, ALU.max)
                    VE.emit("tensor_scalar_max", amc[:, ab + 2:ab + 3],
                            amc[:, ab:ab + 1], 1e-5)
                    VE.emit("tensor_scalar_max", amc[:, ab + 3:ab + 4],
                            amc[:, ab + 1:ab + 2], 1e-5)
                    VE.emit("tensor_scalar_mul", amc[:, ab:ab + 1],
                            amc[:, ab + 2:ab + 3], 1.0 / 127.0)
                    VE.emit("tensor_scalar_mul", amc[:, ab + 1:ab + 2],
                            amc[:, ab + 3:ab + 4], 1.0 / 127.0)
                    VE.emit("tensor_tensor", amc[:, ab:ab + 1],
                            amc[:, ab:ab + 1], smalls[:, 11:12], ALU.mult)
                    VE.emit("tensor_tensor", amc[:, ab + 1:ab + 2],
                            amc[:, ab + 1:ab + 2], smalls[:, 11:12],
                            ALU.mult)
                    VE.emit("reciprocal", amc[:, ab + 2:ab + 3],
                            amc[:, ab + 2:ab + 3])
                    VE.emit("reciprocal", amc[:, ab + 3:ab + 4],
                            amc[:, ab + 3:ab + 4])
                    VE.emit("tensor_scalar_mul", amc[:, ab + 2:ab + 3],
                            amc[:, ab + 2:ab + 3], 127.0)
                    toks[("s2col", g)] = VE.emit(
                        "tensor_scalar_mul", amc[:, ab + 3:ab + 4],
                        amc[:, ab + 3:ab + 4], 127.0)

                @sec_t
                def _(te, g=g):
                    ab = (g % 2) * 4
                    TE.wait(toks[("s2col", g)])
                    TE.emit("transpose", psNb[0:1, 256:384],
                            amc[:, ab + 2:ab + 3], idf[:])
                    TE.emit("transpose", psNb[0:1, 384:512],
                            amc[:, ab + 3:ab + 4], idf[:])
                    toks[("s2tr", g)] = TE.tok()

                @sec_v
                def _(ve, g=g):
                    VE.wait(toks[("s2tr", g)])
                    VE.emit("tensor_copy", s2st[0:1, 0:128],
                            psNb[0:1, 256:384])
                    toks[("s2st", g)] = VE.emit(
                        "tensor_copy", s2st[0:1, 128:256],
                        psNb[0:1, 384:512])

                @sec_t
                def _(te, g=g):
                    TE.wait(toks[("s2st", g)])
                    TE.emit("matmul", psNb[:, 256:512], onesrow[0:1, :],
                            s2st[0:1, 0:256], start=True, stop=True)
                    toks[("s2bc", g)] = TE.tok()

                @sec_v
                def _(ve, g=g):
                    VE.wait(toks[("s2bc", g)])
                    if g >= 1:
                        VE.wait(toks[("hmul", g - 1, 3)])
                    toks[("s2row", g)] = VE.emit(
                        "tensor_copy", s2row[:], psNb[:, 256:512])

                if debug and g == 0:
                    @sec_sq
                    def _(sy, g=g):
                        toks["dbg_g0a"] = SQ.emit(
                            "dma_start", out=dbg["d_hsil"][:],
                            in_=hsil[:], waits=[toks[("amaxp", g)]])


            def emit_Q_pool(g):
                @sec_g
                def _(ge, g=g):
                    GE.wait(toks[("s2row", g)])
                    if g == 0 and "dbg_g0a" in toks:
                        GE.wait(toks["dbg_g0a"])
                    for sl in (2, 3):
                        for c in range(8):
                            hh = 8 * sl + c
                            hsl = hsil[:, GRP * hh:GRP * (hh + 1)]
                            GE.emit("tensor_tensor", hsl, hsl, s2row[:],
                                    ALU.mult)
                        toks[("hmul", g, sl)] = GE.tok()

            def emit_hmul_ve(qg, sl):
                @sec_v
                def _(ve, qg=qg, sl=sl):
                    VE.wait(toks[("s2row", qg)])
                    if qg == 0 and "dbg_g0a" in toks:
                        VE.wait(toks["dbg_g0a"])
                    for c in range(8):
                        hh = 8 * sl + c
                        hsl = hsil[:, GRP * hh:GRP * (hh + 1)]
                        VE.emit("tensor_tensor", hsl, hsl, s2row[:],
                                ALU.mult)
                    toks[("hmul", qg, sl)] = VE.tok()

            def emit_shift(qg, sl):
                @sec_s
                def _(se, qg=qg, sl=sl):
                    SE.wait(toks[("hmul", qg, sl)])
                    SE.wait(toks["consts"])
                    if qg >= 1 and sl == 0:
                        SE.wait(toks[("mm2", qg - 1, 1, 1)])
                    if qg == 1 and sl == 0 and "dbg_g0b" in toks:
                        SE.wait(toks["dbg_g0b"])
                    SE.emit("activation", out=hf16[:],
                            in_=hsil[:, 8 * GRP * sl:8 * GRP * (sl + 1)],
                            func=AF.Identity, bias=smalls[:, 7:8])
                    toks[("hqt", qg, sl)] = SE.emit(
                        "activation",
                        out=hq[:, 8 * GRP * sl:8 * GRP * (sl + 1)],
                        in_=hf16[:], func=AF.Identity,
                        bias=smalls[:, 28:29])

            def emit_Cmm(g, m):
                for n in range(2):
                        ob = psO[n]

                        @sec_t
                        def _(te, g=g, m=m, n=n, ob=ob):
                            for sl in range(4):
                                TE.wait(toks[("hqt", g, sl)])
                            for q4 in range(4):
                                TE.wait(toks[("t2T", q4)])
                            if m >= 1:
                                TE.wait(toks[("mm2_ev", g, m - 1, n)])
                            elif g >= 1:
                                TE.wait(toks[("mm2_ev", g - 1, 1, n)])
                            for kc in range(32):
                                TE.emit(
                                    "matmul", ob[:],
                                    hq[:, GRP * kc + 128 * m:GRP * kc + 128 * (m + 1)],
                                    t2T[:, D * kc + 512 * n:D * kc + 512 * (n + 1)],
                                    start=(kc == 0), stop=(kc == 31))
                            toks[("mm2", g, m, n)] = TE.tok()

            def emit_Cev(g, m, store=False):
                gpar = g % 2
                for n in range(2):
                        ob = psO[n]

                        @sec_v
                        def _(ve, g=g, m=m, n=n, ob=ob, gpar=gpar):
                            VE.wait(toks[("mm2", g, m, n)])
                            VE.wait(toks[("x1g_dma", g)])
                            xsl = x1g[:, gpar * 2 * D + m * D + 512 * n:
                                      gpar * 2 * D + m * D + 512 * (n + 1)]
                            toks[("mm2_ev", g, m, n)] = VE.emit(
                                "scalar_tensor_tensor", xsl, ob[:],
                                amc[:, (g % 2) * 4 + m:(g % 2) * 4 + m + 1],
                                xsl, ALU.mult, ALU.add)

                if not store:
                    return

                @sec_gq
                def _(gp, g=g, gpar=gpar):
                    toks[("out_store", g)] = GQ.emit(
                        "dma_start",
                        out=out_h[GRP * g:GRP * (g + 1), :]
                        .rearrange("(c p) d -> p c d", p=128),
                        in_=x1g[:, gpar * 2 * D:(gpar + 1) * 2 * D]
                        .rearrange("p (c d) -> p c d", d=D),
                        waits=[toks[("mm2_ev", g, m, n)]
                               for m in range(2) for n in range(2)])

            for st in range(5):
                emit_xq_tile(st)
                if st < 3:
                    emit_xq_back(st)
            for g in range(NGRP):
                if g >= 1:
                    emit_Q_pool(g - 1)
                for st in (2 * g + 3, 2 * g + 4):
                    if st < NT:
                        emit_xq_back(st)
                emit_group_A(g)
                if debug and g == 1:
                    @sec_sq
                    def _(sy):
                        toks["dbg_g0b"] = SQ.emit(
                            "dma_start", out=dbg["d_hq"][:], in_=hq[:],
                            waits=[toks[("hqt", 0, 3)]])
                if g >= 1:
                    emit_Cmm(g - 1, 0)
                    emit_Cev(g - 1, 0)
                emit_Bamax(g)
                if g >= 1:
                    emit_Cmm(g - 1, 1)
                emit_Brest(g)
                if g >= 1:
                    emit_Cev(g - 1, 1, store=True)
            emit_Q_pool(NGRP - 1)
            emit_hmul_ve(NGRP - 1, 0)
            emit_hmul_ve(NGRP - 1, 1)
            for sl in range(4):
                emit_shift(NGRP - 1, sl)
            emit_Cmm(NGRP - 1, 0)
            emit_Cev(NGRP - 1, 0)
            emit_Cmm(NGRP - 1, 1)
            emit_Cev(NGRP - 1, 1, store=True)

            if debug:
                @sec_sq
                def _(sy):
                    SQ.emit("dma_start", out=dbg["d_xqt"][:], in_=xqT[:],
                            waits=[toks[("xq_done", NT - 1)],
                                   toks["xq_guard"]])

            @sec_gq
            def _(gp):
                GQ.wait(toks[("out_store", NGRP - 1)])
                for t in SQ.toks():
                    GQ.wait(t)
                for t in GQ.toks():
                    GQ.wait(t)

            flush_phase()

    nc.finalize()
    return nc, dbg


_PROG_CACHE = {}


def _get_program(debug=False):
    key = bool(debug)
    if key not in _PROG_CACHE:
        _PROG_CACHE[key] = build_program(debug=debug)
    return _PROG_CACHE[key]


def make_inputs(L0_attn_x, L0_attn_W_query, L0_attn_W_key, L0_attn_W_value,
                L0_ffn_w1, L0_ffn_conv_w, L0_ffn_conv_b, L0_ffn_w2):
    import ml_dtypes
    x = np.ascontiguousarray(np.asarray(L0_attn_x, dtype=np.float32))
    wq = np.ascontiguousarray(np.asarray(L0_attn_W_query, dtype=np.float32))
    wk = np.ascontiguousarray(np.asarray(L0_attn_W_key, dtype=np.float32))
    wv = np.ascontiguousarray(np.asarray(L0_attn_W_value, dtype=np.float32))
    w1t = np.ascontiguousarray(np.asarray(L0_ffn_w1, dtype=np.float32).T)
    w2t = np.ascontiguousarray(np.asarray(L0_ffn_w2, dtype=np.float32).T)
    cw = np.asarray(L0_ffn_conv_w, dtype=np.float32).reshape(HID, 3)
    cb = np.asarray(L0_ffn_conv_b, dtype=np.float32)
    convw = np.ascontiguousarray(
        cw.reshape(32, 128, 3).transpose(1, 0, 2).reshape(128, 96))
    convb = np.ascontiguousarray(cb.reshape(32, 128).T)
    idbf = np.eye(128, dtype=ml_dtypes.bfloat16)
    idf = np.eye(128, dtype=np.float32)
    mask = np.triu(np.ones((128, 128), dtype=np.float32)).astype(
        ml_dtypes.bfloat16)
    onesb = np.ones((128, 1), dtype=ml_dtypes.bfloat16)
    onesf = np.ones((128, 1), dtype=np.float32)
    onesrow = np.ones((1, 128), dtype=np.float32)

    shared = dict(wq=wq, wk=wk, wv=wv, w1t=w1t, w2t=w2t, convw=convw,
                  convb=convb, idbf=idbf, idf=idf, mask=mask, onesb=onesb,
                  onesf=onesf, onesrow=onesrow)
    in_maps = []
    for c in range(NCORES):
        m = dict(shared)
        m["x"] = np.ascontiguousarray(x[c])
        in_maps.append(m)
    return in_maps


def kernel(L0_attn_x, L0_attn_W_query, L0_attn_W_key, L0_attn_W_value,
           L0_ffn_w1, L0_ffn_conv_w, L0_ffn_conv_b, L0_ffn_w2):
    in_maps = make_inputs(L0_attn_x, L0_attn_W_query, L0_attn_W_key,
                          L0_attn_W_value, L0_ffn_w1, L0_ffn_conv_w,
                          L0_ffn_conv_b, L0_ffn_w2)
    nc, _ = _get_program(debug=False)
    res = run_bass_kernel_spmd(nc, in_maps, core_ids=list(range(NCORES)))
    out = np.stack([np.asarray(r["out"]) for r in res.results], axis=0)
    return out.astype(np.float32)
